# version 12
# baseline (speedup 1.0000x reference)
import math
from contextlib import ExitStack

import numpy as np
import ml_dtypes

import concourse.bass as bass
import concourse.mybir as mybir
from concourse.bass import AP
from concourse.bass_utils import run_bass_kernel_spmd

F32 = mybir.dt.float32
BF16 = mybir.dt.bfloat16
AF = mybir.ActivationFunctionType
ALU = mybir.AluOpType
AX = mybir.AxisListType

ENGINES = ("sync", "tensor", "vector", "scalar", "gpsimd")

B = 2
T = 2048
NT = B * T
D = 1024
NCHUNK = T // 256

DEV = False



class Sem:
    __slots__ = ("h", "n")

    def __init__(self, h):
        self.h = h
        self.n = 0

    def inc(self, k=1):
        self.n += k
        return (self.h, k)


class Rec:

    def __init__(self, prog, name):
        self._p = prog
        self._name = name

    def __getattr__(self, meth):
        p, name = self._p, self._name

        def rec(*args, _inc=None, **kw):
            def run(eng, meth=meth, args=args, kw=kw, _inc=_inc):
                ins = getattr(eng, meth)(*args, **kw)
                if _inc is not None:
                    ins.then_inc(_inc[0], _inc[1])
            p.emitters[name].append(run)
        return rec

    def wait(self, sem):
        self.wait_ge(sem.h, sem.n)


class Prog:
    def __init__(self, nc):
        self.nc = nc
        self.emitters = {e: [] for e in ENGINES}
        self.E = {e: Rec(self, e) for e in ENGINES}
        self._sems = {}
        self._guards = {}
        self._order = []

    def sem(self, name):
        if name not in self._sems:
            g = self.nc.semaphore(name)
            self._sems[name] = Sem(g.__enter__())
            self._guards["sem_" + name] = g
        return self._sems[name]

    def sbuf(self, name, shape, dt):
        g = self.nc.sbuf_tensor(name, shape, dt)
        h = g.__enter__()
        self._guards[name] = g
        self._order = getattr(self, "_order", [])
        self._order.append(name)
        return h

    def psum(self, name, shape, dt=F32):
        g = self.nc.psum_tensor(name, shape, dt)
        h = g.__enter__()
        self._guards[name] = g
        self._order = getattr(self, "_order", [])
        self._order.append(name)
        return h

    def free(self, name):
        g = self._guards.pop(name)
        self._order.remove(name)
        g.__exit__(None, None, None)

    def free_prefix(self, prefix):
        for name in [n for n in reversed(self._order) if n.startswith(prefix)]:
            self.free(name)

    def barrier(self):
        for e in ENGINES:
            for s in self._sems.values():
                self.E[e].wait_ge(s.h, s.n)

    def finish(self):
        def runner(name):
            fns = self.emitters[name]

            def run(eng):
                for f in fns:
                    f(eng)
            return run
        with self.nc.Block() as block:
            block.sync(runner("sync"))
            block.tensor(runner("tensor"))
            block.vector(runner("vector"))
            block.scalar(runner("scalar"))
            block.gpsimd(runner("gpsimd"))
        return self.nc



def _prep_host(inputs):
    H = {}
    f32 = np.float32

    u = np.ascontiguousarray(np.asarray(inputs["L0_attn_u"], f32))
    H["_u_full"] = u

    H["inprojT"] = np.ascontiguousarray(np.asarray(inputs["L0_attn_in_proj_w"], f32).T)
    cw = np.asarray(inputs["L0_attn_conv_w"], f32)
    H["convw"] = np.ascontiguousarray(cw.reshape(17, 128, 4).transpose(1, 0, 2).reshape(128, 68))
    cb = np.asarray(inputs["L0_attn_conv_b"], f32)
    H["convb"] = np.ascontiguousarray(cb.reshape(17, 128).T)
    H["dtb"] = np.asarray(inputs["L0_attn_dt_bias"], f32).reshape(16, 1)
    H["alog"] = np.asarray(inputs["L0_attn_A_log"], f32).reshape(16, 1)
    H["dp"] = np.asarray(inputs["L0_attn_Dp"], f32).reshape(1, 16)
    H["normw"] = np.asarray(inputs["L0_attn_norm_w"], f32).reshape(1, 2048)
    H["oprojT"] = np.ascontiguousarray(np.asarray(inputs["L0_attn_out_proj_w"], f32).T)

    for nm, key in (("wq", "L0_ffn_Wq"), ("wk", "L0_ffn_Wk"),
                    ("wv", "L0_ffn_Wv"), ("wo", "L0_ffn_Wo")):
        H[nm] = np.ascontiguousarray(np.asarray(inputs[key], f32))
    for nm, key in (("bq", "L0_ffn_bq"), ("bk", "L0_ffn_bk")):
        H[nm] = np.ascontiguousarray(np.asarray(inputs[key], f32).reshape(8, 128).T)
    H["bv"] = np.asarray(inputs["L0_ffn_bv"], f32).reshape(1, 1024)
    H["bo"] = np.asarray(inputs["L0_ffn_bo"], f32).reshape(1, 1024)
    for nm, key in (("g1", "L0_ffn_g1"), ("be1", "L0_ffn_be1"),
                    ("g2ln", "L0_ffn_g2"), ("be2ln", "L0_ffn_be2")):
        H[nm] = np.asarray(inputs[key], f32).reshape(1, 1024)

    H["wg"] = np.ascontiguousarray(np.asarray(inputs["L0_ffn_Wg"], f32))
    H["bgv"] = np.asarray(inputs["L0_ffn_bg"], f32).reshape(1, 8)
    H["w1"] = np.ascontiguousarray(np.asarray(inputs["L0_ffn_W1"], f32))
    H["b1"] = np.ascontiguousarray(
        np.asarray(inputs["L0_ffn_b1"], f32).reshape(8, 32, 128).transpose(0, 2, 1))
    H["w2"] = np.ascontiguousarray(np.asarray(inputs["L0_ffn_W2"], f32))
    H["b2"] = np.ascontiguousarray(np.asarray(inputs["L0_ffn_b2"], f32))

    ipw = np.asarray(inputs["L1_attn_in_proj_weight"], f32)
    ipb = np.asarray(inputs["L1_attn_in_proj_bias"], f32)
    H["wq1T"] = np.ascontiguousarray(ipw[0:1024].T)
    H["wk1T"] = np.ascontiguousarray(ipw[1024:2048].T)
    H["bq1"] = np.ascontiguousarray(ipb[0:1024].reshape(8, 128).T)
    H["bk1"] = np.ascontiguousarray(ipb[1024:2048].reshape(8, 128).T)

    H["wtT"] = np.ascontiguousarray(np.asarray(inputs["L1_ffn_weight"], f32).T)
    H["alpha"] = np.asarray(inputs["L1_ffn_alpha"], f32).reshape(1, 1)
    H["bias1"] = np.asarray(inputs["L1_ffn_bias"], f32).reshape(1, 1024)

    inv_freq = (1.0 / 10000.0 ** (np.arange(0, 64, 2, dtype=f32) / 64.0)).astype(f32)
    t = np.arange(T, dtype=f32)
    freqs = t[:, None] * inv_freq[None, :]
    H["cos32"] = np.ascontiguousarray(np.cos(freqs).T.astype(f32))
    H["sin32"] = np.ascontiguousarray(np.sin(freqs).T.astype(f32))

    import jax
    import jax.numpy as jnp
    try:
        cpu = jax.local_devices(backend="cpu")[0]
    except Exception:
        cpu = None

    def _mk_gum():
        uu = jax.random.uniform(jax.random.key(42), (16, T, T), jnp.float32,
                                1e-06, 1.0 - 1e-06)
        return -jnp.log(-jnp.log(uu))
    if cpu is not None:
        with jax.default_device(cpu):
            gum = np.asarray(jax.jit(_mk_gum, backend="cpu")())
    else:
        gum = np.asarray(_mk_gum())
    idx = np.arange(T)
    invalid = (idx[None, :] >= idx[:, None]).astype(f32) * (-1.0e9)
    g2 = (gum + invalid[None]) * 0.5
    H["_gum2_full"] = np.ascontiguousarray(g2.astype(f32))
    return H


SHARED_IN = {
    "inprojT": ((1024, 4240), F32), "convw": ((128, 68), F32),
    "convb": ((128, 17), F32), "dtb": ((16, 1), F32), "alog": ((16, 1), F32),
    "dp": ((1, 16), F32), "normw": ((1, 2048), F32), "oprojT": ((2048, 1024), F32),
    "wq": ((1024, 1024), F32), "wk": ((1024, 1024), F32),
    "wv": ((1024, 1024), F32), "wo": ((1024, 1024), F32),
    "bq": ((128, 8), F32), "bk": ((128, 8), F32), "bv": ((1, 1024), F32),
    "bo": ((1, 1024), F32),
    "g1": ((1, 1024), F32), "be1": ((1, 1024), F32),
    "g2ln": ((1, 1024), F32), "be2ln": ((1, 1024), F32),
    "wg": ((1024, 8), F32), "bgv": ((1, 8), F32),
    "w1": ((8, 1024, 4096), F32), "b1": ((8, 128, 32), F32),
    "w2": ((8, 4096, 1024), F32), "b2": ((8, 1024), F32),
    "wq1T": ((1024, 1024), F32), "wk1T": ((1024, 1024), F32),
    "bq1": ((128, 8), F32), "bk1": ((128, 8), F32),
    "wtT": ((1024, 1024), F32), "alpha": ((1, 1), F32), "bias1": ((1, 1024), F32),
    "cos32": ((32, T), F32), "sin32": ((32, T), F32),
}
PERCORE_IN = {
    "u": ((NT, D), F32),
    "gum2": ((NT, T), F32),
}
SCRATCH = {
    "xn_tm": ((NT, D), BF16), "xn_fm": ((D, NT), BF16),
    "mamba_tm": ((NT, D), F32),
    "x1_fm": ((D, NT), BF16), "x1_tm": ((NT, D), BF16),
    "y_acc": ((NT, D), F32),
    "ffn_tm": ((NT, D), F32),
    "xnew_tm": ((NT, D), F32),
    "xn2_tm": ((NT, D), BF16), "xn2_fm": ((D, NT), BF16),
    "l1ffn_tm": ((NT, D), F32),
    "dwT": ((B * T, T), BF16),
}
OUT_NAME = "out"
OUT_SHAPE = (NT, D)
_PHASE_PREFIX = {"prep": "prep_", "mamba": "mb_", "attn": "at_", "moe": "mo_",
                 "l1": "l1_"}



def fm_ap(dram_ap, dc=None, tsl=None):
    r = dram_ap.rearrange("(c p) t -> p c t", p=128)
    if dc is not None:
        r = r[:, dc:dc + 1, :]
    if tsl is not None:
        r = r[:, :, tsl[0]:tsl[1]]
    return r


def make_consts(P, D_):
    nc = P.nc
    C = {}
    C["ident_bf"] = P.sbuf("ident_bf", [128, 128], BF16)
    C["ones_row_bf"] = P.sbuf("ones_row_bf", [1, 128], BF16)
    C["ones_row"] = P.sbuf("ones_row", [1, 128], F32)
    C["ones_col"] = P.sbuf("ones_col", [128, 1], F32)
    C["one1"] = P.sbuf("one1", [1, 1], F32)
    C["eps_col"] = P.sbuf("eps_col", [128, 1], F32)
    g = P.E["gpsimd"]
    gp = P.sem("gp")
    g.memset(C["ident_bf"][:], 0.0)
    g.memset(C["ones_row_bf"][:], 1.0)
    g.memset(C["ones_row"][:], 1.0)
    g.memset(C["ones_col"][:], 1.0)
    g.memset(C["eps_col"][:], 1.0e-5)
    g.memset(C["one1"][:], 1.0, _inc=gp.inc())
    g.wait(gp)
    g.affine_select(out=C["ident_bf"][:], in_=C["ident_bf"][:],
                    compare_op=ALU.not_equal, fill=1.0, base=0,
                    pattern=[[-1, 128]], channel_multiplier=1, _inc=gp.inc())
    for e in ("tensor", "vector", "scalar"):
        P.E[e].wait(gp)
    return C


def ph_prep(P, D_, C):
    E = P.E
    sy, v, s, t, g = E["sync"], E["vector"], E["scalar"], E["tensor"], E["gpsimd"]
    dv = P.sem("dv")
    ac = P.sem("ac")
    pe = P.sem("pe")
    u = D_["u"]
    xn_tm = D_["xn_tm"]
    xn_fm = D_["xn_fm"]

    nbuf = 3
    lds = [P.sem(f"prep_ld{i}") for i in range(nbuf)]
    sds = [P.sem(f"prep_sd{i}") for i in range(nbuf)]
    ut = [P.sbuf(f"prep_ut{i}", [128, 1024], F32) for i in range(nbuf)]
    cent = [P.sbuf(f"prep_cent{i}", [128, 1024], F32) for i in range(nbuf)]
    xnt = [P.sbuf(f"prep_xnt{i}", [128, 1024], BF16) for i in range(nbuf)]
    xft = [P.sbuf(f"prep_xft{i}", [128, 1024], BF16) for i in range(nbuf)]
    sq = P.sbuf("prep_sq", [128, 1024], F32)
    stm = [P.sbuf(f"prep_stm{i}", [128, 1], F32) for i in range(nbuf)]
    sts = [P.sbuf(f"prep_sts{i}", [128, 1], F32) for i in range(nbuf)]
    ptr = P.psum("prep_ptr", [128, 256], BF16)

    done_iter = []
    for i in range(32):
        buf = i % nbuf
        if i >= nbuf:
            dvn, pen, acn = done_iter[i - nbuf]
            sy.wait_ge(dv.h, dvn)
            sy.wait_ge(ac.h, acn)
            v.wait_ge(sds[buf].h, sds[buf].n)
            v.wait_ge(pe.h, pen)
        sy.dma_start(out=ut[buf][:], in_=u[i * 128:(i + 1) * 128, :],
                     _inc=lds[buf].inc(16))
        emit_ln_tm(P, ut[buf][:], xnt[buf][:], cent[buf][:], sq, stm[buf], sts[buf],
                   C, ld_sem=lds[buf])
        s.wait(dv)
        s.dma_start(out=xn_tm[i * 128:(i + 1) * 128, :], in_=xnt[buf][:],
                    _inc=sds[buf].inc(16))
        for j in range(4):
            t.wait(dv)
            t.transpose(ptr[:, 0:128], xnt[buf][:, (2 * j) * 128:(2 * j + 1) * 128], C["ident_bf"][:])
            t.transpose(ptr[:, 128:256], xnt[buf][:, (2 * j + 1) * 128:(2 * j + 2) * 128],
                        C["ident_bf"][:], _inc=pe.inc())
            v.wait(pe)
            v.tensor_copy(xft[buf][:, (2 * j) * 128:(2 * j + 2) * 128], ptr[:], _inc=dv.inc())
        s.wait(dv)
        s.dma_start(out=fm_ap(xn_fm, tsl=(i * 128, (i + 1) * 128)),
                    in_=xft[buf][:].rearrange("p (c t) -> p c t", t=128),
                    _inc=sds[buf].inc(16))
        done_iter.append((dv.n, pe.n, ac.n))
    return {}


def emit_ln_tm(P, x_ap, out_ap, cent, sqbuf, stm, sts, C, ld_sem=None,
               gain_bc=None, bias_bc=None, nfeat=1024):
    E = P.E
    v, s = E["vector"], E["scalar"]
    dv, ac = P.sem("dv"), P.sem("ac")
    if ld_sem is not None:
        v.wait(ld_sem)
    v.wait(dv)
    v.reduce_sum(stm[:], x_ap, axis=AX.X, _inc=dv.inc())
    s.wait(dv)
    s.mul(sts[:], stm[:], 1.0 / nfeat, _inc=ac.inc())
    v.wait(ac)
    v.tensor_scalar(cent[:], x_ap, sts[:], None, ALU.subtract, _inc=dv.inc())
    s.wait(dv)
    s.activation(out=sqbuf[:, 0:nfeat], in_=cent[:], func=AF.Square,
                 accum_out=stm[:], _inc=ac.inc())
    s.wait(ac)
    s.activation(out=sts[:], in_=stm[:], func=AF.Sqrt, scale=1.0 / nfeat,
                 bias=C["eps_col"][:], _inc=ac.inc())
    v.wait(ac)
    v.reciprocal(stm[:], sts[:], _inc=dv.inc())
    v.wait(dv)
    if gain_bc is None:
        v.tensor_scalar(out_ap, cent[:], stm[:], None, ALU.mult, _inc=dv.inc())
    else:
        v.tensor_scalar(cent[:], cent[:], stm[:], None, ALU.mult, _inc=dv.inc())
        v.wait(dv)
        v.tensor_tensor(cent[:], cent[:], gain_bc, ALU.mult, _inc=dv.inc())
        v.wait(dv)
        v.tensor_tensor(out_ap, cent[:], bias_bc, ALU.add, _inc=dv.inc())


def ph_mamba(P, D_, C):
    E = P.E
    sy, v, s, t, g = E["sync"], E["vector"], E["scalar"], E["tensor"], E["gpsimd"]
    ld = P.sem("mbld")
    wt = P.sem("mbwt")
    dv = P.sem("dv")
    ac = P.sem("ac")
    pe = P.sem("pe")
    sd = P.sem("mbsd")
    gp = P.sem("gp")
    xfs = [P.sem(f"mbxf{i}") for i in range(2)]
    wqs = [P.sem(f"mbwq{i}") for i in range(2)]

    QEC = (9, 9, 9, 7)
    QE0 = (0, 9, 18, 27)
    wqb = [P.sbuf(f"mb_wq{i}", [128, 8 * 9 * 128], BF16) for i in range(2)]
    opT = P.sbuf("mb_opT", [128, 16 * 1024], BF16)
    cw = P.sbuf("mb_cw", [128, 68], F32)
    cb = P.sbuf("mb_cb", [128, 17], F32)
    dtb = P.sbuf("mb_dtb", [16, 1], F32)
    acol = P.sbuf("mb_acol", [16, 1], F32)
    nw = P.sbuf("mb_nw", [1, 2048], F32)
    dpc = P.sbuf("mb_dpc", [128, 16], F32)
    u0m = P.sbuf("mb_u0", [128, 256], F32)
    u1m = P.sbuf("mb_u1", [128, 256], F32)
    z16 = P.sbuf("mb_z16", [16, 256], F32)
    id16 = P.sbuf("mb_id16", [16, 16], F32)
    hrow = P.sbuf("mb_hrow", [16, 16 * 128], F32)

    g.dma_start(out=opT[:].rearrange("p (c e) -> p c e", e=1024),
                in_=D_["oprojT"].rearrange("(c p) e -> p c e", p=128), _inc=wt.inc(16))
    dprow = P.sbuf("mb_dprow", [1, 16], F32)
    sy.dma_start(out=cw[:], in_=D_["convw"], _inc=ld.inc(16))
    sy.dma_start(out=cb[:], in_=D_["convb"], _inc=ld.inc(16))
    sy.dma_start(out=dtb[:], in_=D_["dtb"], _inc=ld.inc(16))
    sy.dma_start(out=acol[:], in_=D_["alog"], _inc=ld.inc(16))
    sy.dma_start(out=nw[:], in_=D_["normw"], _inc=ld.inc(16))
    sy.dma_start(out=dprow[:], in_=D_["dp"], _inc=ld.inc(16))
    s.wait(ld)
    s.activation(out=acol[:], in_=acol[:], func=AF.Exp, _inc=ac.inc())
    v.wait(ac)
    v.tensor_scalar_mul(acol[:], acol[:], -1.0, _inc=dv.inc())
    sel64 = P.sbuf("mb_sel64", [128, 64], BF16)
    g.memset(u0m[:], 0.0)
    g.memset(u1m[:], 0.0)
    g.memset(z16[:], 0.0)
    g.memset(id16[:], 0.0)
    g.memset(sel64[:], 0.0)
    g.memset(hrow[:], 0.0, _inc=gp.inc())
    g.wait(gp)
    g.affine_select(out=u0m[:], in_=u0m[:], compare_op=ALU.is_gt, fill=1.0,
                    base=0, pattern=[[-1, 256]], channel_multiplier=1)
    g.affine_select(out=u1m[:], in_=u1m[:], compare_op=ALU.is_gt, fill=1.0,
                    base=128, pattern=[[-1, 256]], channel_multiplier=1)
    g.affine_select(out=id16[:], in_=id16[:], compare_op=ALU.not_equal, fill=1.0,
                    base=0, pattern=[[-1, 16]], channel_multiplier=1)
    g.affine_select(out=sel64[:], in_=sel64[:], compare_op=ALU.not_equal,
                    fill=1.0, base=-64, pattern=[[-1, 64]], channel_multiplier=1)
    hr3 = hrow[:].rearrange("p (h k) -> p h k", k=128)
    for h in range(16):
        g.affine_select(out=hr3[:, h, :], in_=hr3[:, h, :],
                        compare_op=ALU.not_equal, fill=1.0,
                        base=-h, pattern=[[0, 128]], channel_multiplier=1,
                        _inc=gp.inc() if h == 15 else None)

    pbank_a = P.psum("mb_pa", [128, 512], F32)
    pbank_b = P.psum("mb_pb", [128, 512], F32)
    psc = P.psum("mb_psc", [128, 512], F32)
    pbq = P.psum("mb_pbq", [128, 512], F32)
    paux = P.psum("mb_paux", [128, 512], F32)
    py = P.psum("mb_py", [128, 256], F32)
    pnb = P.psum("mb_pnb", [128, 512], F32)
    ptx = P.psum("mb_ptx", [128, 256], BF16)
    t.wait(ld)
    t.wait(gp)
    t.matmul(pbq[:, 0:16], C["ones_row"][:], dprow[:], start=True, stop=True, _inc=pe.inc())
    v.wait(pe)
    v.tensor_copy(dpc[:], pbq[:, 0:16], _inc=dv.inc())

    xnfb = [P.sbuf(f"mb_xnf{i}", [128, 8 * 256], BF16) for i in range(2)]
    xbcraw = P.sbuf("mb_raw", [128, 17 * 259], F32)
    raw3 = xbcraw[:].rearrange("p (c w) -> p c w", w=259)
    xbcs = P.sbuf("mb_xbcs", [128, 17 * 256], BF16)
    xbcs3 = xbcs[:].rearrange("p (c w) -> p c w", w=256)
    sz = P.sbuf("mb_sz", [128, 16 * 256], BF16)
    sz3 = sz[:].rearrange("p (c w) -> p c w", w=256)
    yz = P.sbuf("mb_yz", [128, 16 * 256], BF16)
    yz3 = yz[:].rearrange("p (c w) -> p c w", w=256)
    ynb = P.sbuf("mb_ynb", [128, 16 * 256], BF16)
    ynb3 = ynb[:].rearrange("p (c w) -> p c w", w=256)
    convo = P.sbuf("mb_convo", [128, 256], F32)
    dtrow = P.sbuf("mb_dtrow", [16, 256], F32)
    dtc = P.sbuf("mb_dtc", [16, 256], F32)
    acum = P.sbuf("mb_acum", [16, 256], F32)
    dsr = P.sbuf("mb_dsr", [16, 256], F32)
    dtds = P.sbuf("mb_dtds", [16, 256], F32)
    eacum = P.sbuf("mb_eacum", [16, 256], F32)
    blrow = P.sbuf("mb_blrow", [16, 256], F32)
    cdcol = P.sbuf("mb_cdcol", [16, 1], F32)
    tm16a = P.sbuf("mb_tm16a", [16, 256], F32)
    tm16b = P.sbuf("mb_tm16b", [16, 256], F32)
    sU = P.sbuf("mb_sU", [128, 2 * 256], F32)
    sU3 = sU[:].rearrange("p (k w) -> p k w", w=256)
    btm = P.sbuf("mb_btm", [128, 2 * 64], BF16)
    btm3 = btm[:].rearrange("p (k w) -> p k w", w=64)
    xtm3d = []
    xw3d = []
    argbd = []
    expdd = []
    msb3d = []
    cwtd = []
    for i in range(2):
        xtm3d.append(P.sbuf(f"mb_xtm{i}", [128, 2 * 128], BF16)[:]
                     .rearrange("p (k w) -> p k w", w=128))
        xw3d.append(P.sbuf(f"mb_xw{i}", [128, 2 * 128], BF16)[:]
                    .rearrange("p (k w) -> p k w", w=128))
        if i == 0:
            argbd.append(P.sbuf("mb_argb0", [128, 512], F32))
            expdd.append(P.sbuf("mb_expd0", [128, 512], BF16))
        else:
            argbd.append(argbd[0])
            expdd.append(expdd[0])
        msb3d.append(P.sbuf(f"mb_msb{i}", [128, 2 * 256], BF16)[:]
                     .rearrange("p (k w) -> p k w", w=256))
        cwtd.append(P.sbuf(f"mb_cwt{i}", [64, 256], BF16))
    ctile = P.sbuf("mb_ctile", [64, 256], BF16)
    blc = P.sbuf("mb_blc", [128, 32], F32)
    ddc = P.sbuf("mb_ddc", [128, 32], F32)
    cdc = P.sbuf("mb_cdc", [64, 16], F32)
    cdr = P.sbuf("mb_cdr", [1, 16], F32)
    yfull = P.sbuf("mb_yfull", [128, 256], F32)
    Sf = P.sbuf("mb_Sf", [64, 16 * 128], F32)
    Sf3 = Sf[:].rearrange("p (h w) -> p h w", w=128)
    Sb = P.sbuf("mb_Sb", [64, 16 * 128], BF16)
    Sb3 = Sb[:].rearrange("p (h w) -> p h w", w=128)
    sqb = [P.sbuf(f"mb_sqb{i}", [128, 256], F32) for i in range(2)]
    rsd = P.sbuf("mb_rsd", [1, 256], F32)
    mo = P.sbuf("mb_mo", [128, 512], F32)

    wq3 = [w[:].rearrange("p (c e) -> p c e", e=9 * 128) for w in wqb]
    ipTr = D_["inprojT"].rearrange("(c p) e -> p c e", p=128)
    opT3 = opT[:].rearrange("p (c e) -> p c e", e=1024)
    t.wait(wt)
    v.wait(dv)

    ipg = {}
    sq_done = {}
    nb_done = {}
    wq_loaded = {}
    wq_used = {}
    xbcsd = [xbcs, P.sbuf("mb_xbcs2", [128, 17 * 256], BF16)]
    xbcs3d = [xx[:].rearrange("p (c w) -> p c w", w=256) for xx in xbcsd]
    szd = [sz, P.sbuf("mb_sz2", [128, 16 * 256], BF16)]
    sz3d = [xx[:].rearrange("p (c w) -> p c w", w=256) for xx in szd]
    acumd = [acum, P.sbuf("mb_acum2", [16, 256], F32)]
    eacumd = [eacum, P.sbuf("mb_eacum2", [16, 256], F32)]
    dtdsd = [dtds, P.sbuf("mb_dtds2", [16, 256], F32)]
    blrowd = [blrow, P.sbuf("mb_blrow2", [16, 256], F32)]
    cdcold = [cdcol, P.sbuf("mb_cdcol2", [16, 1], F32)]
    back_pe = [0, 0]
    back_dv = [0, 0]
    back_ac = [0, 0]

    def front_gen(gc):
        b = gc // NCHUNK
        cl = gc % NCHUNK
        gtok0 = b * T + cl * 256
        par = gc % 2
        xbcs3 = xbcs3d[par]
        sz3 = sz3d[par]
        acum_, eacum_, dtds_, blrow_, cdcol_ = (acumd[par], eacumd[par],
                                                dtdsd[par], blrowd[par],
                                                cdcold[par])
        if gc >= 2:
            sy.wait_ge(pe.h, ipg[gc - 2])
        sy.dma_start(out=xnfb[par][:].rearrange("p (c w) -> p c w", w=256),
                     in_=fm_ap(D_["xn_fm"], tsl=(gtok0, gtok0 + 256)),
                     _inc=xfs[par].inc(16))
        xnf3 = xnfb[par][:].rearrange("p (c w) -> p c w", w=256)
        v.wait(dv)
        if cl == 0:
            v.memset(raw3[:, :, 0:3], 0.0, _inc=dv.inc())
        else:
            v.tensor_copy(raw3[:, :, 0:3], raw3[:, :, 256:259], _inc=dv.inc())
        xf_gate = (xfs[par], xfs[par].n)

        def issue_wq(q):
            key = (gc, q - 2) if q >= 2 else (gc - 1, q + 2)
            if key in wq_used:
                g.wait_ge(pe.h, wq_used[key])
            ewid = min((QE0[q] + QEC[q]) * 128, 4240) - QE0[q] * 128
            g.dma_start(out=wq3[q % 2][:, :, 0:ewid],
                        in_=ipTr[:, :, QE0[q] * 128:QE0[q] * 128 + ewid],
                        _inc=wqs[q % 2].inc(16))
            wq_loaded[(gc, q)] = (wqs[q % 2], wqs[q % 2].n)

        issue_wq(0)
        issue_wq(1)
        t.wait_ge(xf_gate[0].h, xf_gate[1])
        for ec in range(34):
            pp = pbank_a if ec % 2 == 0 else pbank_b
            width = 128 if ec < 33 else 16
            q = min(ec // 9, 3)
            eo = (ec - QE0[q]) * 128
            if ec == 9:
                issue_wq(2)
            if ec == 18:
                issue_wq(3)
            if ec == QE0[q]:
                wsem, wcnt = wq_loaded[(gc, q)]
                t.wait_ge(wsem.h, wcnt)
            if (gc, ec - 2) in evac_done:
                sem, cnt = evac_done[(gc, ec - 2)]
                t.wait_ge(sem.h, cnt)
            for dc in range(8):
                t.matmul(pp[0:width, 0:256], wq3[q % 2][:, dc, eo:eo + width],
                         xnf3[:, dc, :], start=(dc == 0), stop=(dc == 7),
                         _inc=pe.inc() if dc == 7 else None)
            if ec == QE0[q] + QEC[q] - 1:
                wq_used[(gc, q)] = pe.n
            if ec < 16:
                s.wait(pe)
                s.wait_ge(dv.h, back_dv[par])
                s.activation(out=sz3[:, ec, :], in_=pp[:, 0:256], func=AF.Silu,
                             _inc=ac.inc())
                evac_done[(gc, ec)] = (ac, ac.n)
            elif ec < 33:
                v.wait(pe)
                v.tensor_copy(raw3[:, ec - 16, 3:259], pp[:, 0:256], _inc=dv.inc())
                evac_done[(gc, ec)] = (dv, dv.n)
            else:
                v.wait(pe)
                v.tensor_scalar(dtrow[:], pp[0:16, 0:256], dtb[:], None, ALU.add,
                                _inc=dv.inc())
                evac_done[(gc, ec)] = (dv, dv.n)
            if ec % 4 == 3:
                yield
        ipg[gc] = pe.n
        for ec in range(17):
            v.wait(dv)
            v.wait(ac)
            v.tensor_scalar(convo[:], raw3[:, ec, 0:256], cw[:, 4 * ec:4 * ec + 1],
                            cb[:, ec:ec + 1], ALU.mult, ALU.add, _inc=dv.inc())
            for w in range(1, 4):
                v.wait(dv)
                v.scalar_tensor_tensor(convo[:], raw3[:, ec, w:w + 256],
                                       cw[:, 4 * ec + w:4 * ec + w + 1],
                                       convo[:], ALU.mult, ALU.add, _inc=dv.inc())
            s.wait(dv)
            s.wait_ge(pe.h, back_pe[par])
            s.wait_ge(dv.h, back_dv[par])
            s.activation(out=xbcs3[:, ec, :], in_=convo[:], func=AF.Silu,
                         _inc=ac.inc())
            if ec % 4 == 3:
                yield
        s.wait(dv)
        s.activation(out=tm16a[:], in_=dtrow[:], func=AF.Abs, _inc=ac.inc())
        s.wait(ac)
        s.activation(out=tm16a[:], in_=tm16a[:], func=AF.Exp, scale=-1.0, _inc=ac.inc())
        s.wait(ac)
        s.activation(out=tm16a[:], in_=tm16a[:], func=AF.Ln,
                     bias=C["ones_col"][0:16, :], _inc=ac.inc())
        v.wait(ac)
        v.tensor_relu(tm16b[:], dtrow[:], _inc=dv.inc())
        v.wait(dv)
        v.tensor_tensor(dtc[:], tm16a[:], tm16b[:], ALU.add, _inc=dv.inc())
        yield
        v.wait(dv)
        v.wait_ge(pe.h, back_pe[par])
        v.tensor_scalar(tm16a[:], dtc[:], acol[:], None, ALU.mult, _inc=dv.inc())
        v.wait(dv)
        v.tensor_tensor_scan(acum_[:], tm16a[:], z16[:], 0.0, ALU.add, ALU.add,
                             _inc=dv.inc())
        s.wait(dv)
        s.activation(out=dsr[:], in_=acum_[:], func=AF.Exp, scale=-1.0,
                     bias=acum_[:, 255:256], _inc=ac.inc())
        s.activation(out=eacum_[:], in_=acum_[:], func=AF.Exp, _inc=ac.inc())
        s.activation(out=cdcol_[:], in_=acum_[:, 255:256], func=AF.Exp, _inc=ac.inc())
        s.activation(out=tm16b[:], in_=dtc[:], func=AF.Ln, _inc=ac.inc())
        v.wait(ac)
        v.tensor_tensor(dtds_[:], dtc[:], dsr[:], ALU.mult, _inc=dv.inc())
        v.tensor_tensor(blrow_[:], tm16b[:], acum_[:], ALU.subtract, _inc=dv.inc())
        yield

    def back_gen(gc):
        b = gc // NCHUNK
        cl = gc % NCHUNK
        gtok0 = b * T + cl * 256
        par = gc % 2
        xbcs3 = xbcs3d[par]
        sz3 = sz3d[par]
        acum_, eacum_, dtds_, blrow_, cdcol_ = (acumd[par], eacumd[par],
                                                dtdsd[par], blrowd[par],
                                                cdcold[par])
        if cl == 0:
            g.wait_ge(dv.h, dv.n)
            g.wait_ge(pe.h, pe.n)
            g.memset(Sf[:], 0.0)
            g.memset(Sb[:], 0.0, _inc=gp.inc())
            v.wait(gp)
            t.wait(gp)
        t.wait(ac)
        t.wait(dv)
        t.matmul(paux[0:64, 0:256], sel64[:], xbcs3[:, 16, :],
                 start=True, stop=True, _inc=pe.inc())
        v.wait(pe)
        v.tensor_copy(ctile[:], paux[0:64, 0:256], _inc=dv.inc())
        t.wait(dv)
        t.matmul(psc[:, 0:256], xbcs3[0:64, 16, 0:128], ctile[:],
                 start=True, stop=True)
        t.matmul(psc[:, 256:512], xbcs3[0:64, 16, 128:256], ctile[:],
                 start=True, stop=True, _inc=pe.inc())
        v.wait(pe)
        v.tensor_tensor(sU3[:, 0, :], psc[:, 0:256], u0m[:], ALU.mult, _inc=dv.inc())
        v.tensor_tensor(sU3[:, 1, :], psc[:, 256:512], u1m[:], ALU.mult, _inc=dv.inc())
        t.transpose(ptx[:, 0:64], xbcs3[0:64, 16, 0:128], C["ident_bf"][0:64, 0:64])
        t.transpose(ptx[:, 64:128], xbcs3[0:64, 16, 128:256],
                    C["ident_bf"][0:64, 0:64], _inc=pe.inc())
        v.wait(pe)
        v.tensor_copy(btm3[:, 0, :], ptx[:, 0:64])
        v.tensor_copy(btm3[:, 1, :], ptx[:, 64:128], _inc=dv.inc())
        t.wait(dv)
        t.wait(ac)
        for kt in range(2):
            t.matmul(pbq[:, 256 + kt * 16:256 + (kt + 1) * 16],
                     blrow_[:, kt * 128:(kt + 1) * 128], id16[:],
                     start=True, stop=True)
            t.matmul(pbq[:, 288 + kt * 16:288 + (kt + 1) * 16],
                     dtds_[:, kt * 128:(kt + 1) * 128], id16[:],
                     start=True, stop=True)
        t.matmul(paux[0:1, 400:416], cdcol_[:], id16[:], start=True, stop=True,
                 _inc=pe.inc())
        v.wait(pe)
        v.tensor_copy(blc[:], pbq[:, 256:288])
        v.tensor_copy(ddc[:], pbq[:, 288:320])
        v.tensor_copy(cdr[:], paux[0:1, 400:416], _inc=dv.inc())
        t.wait(dv)
        t.matmul(paux[0:64, 384:400], C["ones_row"][0:1, 0:64], cdr[:],
                 start=True, stop=True, _inc=pe.inc())
        v.wait(pe)
        v.tensor_copy(cdc[:], paux[0:64, 384:400], _inc=dv.inc())
        yield
        base_dv = dv.n
        base_pe = pe.n
        ptxf = [base_dv, base_dv]
        bqf = [base_dv, base_dv]
        eabf = [base_dv, base_dv]
        pyf = base_dv
        stf = [base_dv, base_dv]
        ymm = [base_pe, base_pe]
        for h in range(16):
            p2 = h % 2
            hsel = hr3[:, h, :][0:16, :]
            po = p2 * 128
            bq_bank = pbq if p2 == 0 else pnb
            eab_reg = (paux if p2 == 0 else psc)[0:64, 0:256]
            st_reg = (paux if p2 == 0 else psc)[0:64, 256:384]
            py_bank = py
            xtm_p = xtm3d[p2]
            xw_p = xw3d[p2]
            argb_p = argbd[p2]
            expd_p = expdd[p2]
            msb_p = msb3d[p2]
            cwt_p = cwtd[p2]
            t.wait_ge(dv.h, ptxf[p2])
            t.wait_ge(pe.h, ymm[p2])
            t.transpose(ptx[:, po:po + 128], xbcs3[:, h, 0:128], C["ident_bf"][:],
                        _inc=pe.inc())
            tr0 = pe.n
            v.wait_ge(pe.h, tr0)
            v.tensor_copy(xtm_p[:, 0, :], ptx[:, po:po + 128], _inc=dv.inc())
            ev0 = dv.n
            t.wait_ge(dv.h, ev0)
            t.transpose(ptx[:, po:po + 128], xbcs3[:, h, 128:256], C["ident_bf"][:],
                        _inc=pe.inc())
            tr1 = pe.n
            v.wait_ge(pe.h, tr1)
            v.tensor_copy(xtm_p[:, 1, :], ptx[:, po:po + 128], _inc=dv.inc())
            ptxf[p2] = dv.n
            t.wait_ge(dv.h, max(bqf[p2], eabf[p2]))
            t.matmul(bq_bank[:, 0:256], hsel, acum_[:], start=True, stop=True)
            t.matmul(eab_reg, hsel[:, 0:64], eacum_[:], start=True, stop=True,
                     _inc=pe.inc())
            bcpe = pe.n
            v.wait_ge(pe.h, bcpe)
            v.tensor_tensor(cwt_p[:], ctile[:], eab_reg, ALU.mult, _inc=dv.inc())
            eabf[p2] = dv.n
            v.wait(dv)
            ddc_bc = AP(ddc[:].tensor, h, [ddc[:].ap[0], [16, 2], [0, 128]])
            v.tensor_tensor(xw_p[:, :, :], xtm_p[:, :, :], ddc_bc, ALU.mult,
                            _inc=dv.inc())
            for kt in range(2):
                v.tensor_scalar(argb_p[:, kt * 256:kt * 256 + 256],
                                bq_bank[:, 0:256],
                                blc[:, kt * 16 + h:kt * 16 + h + 1],
                                30.0, ALU.add, ALU.min, _inc=dv.inc())
            bqf[p2] = dv.n
            s.wait_ge(dv.h, dv.n)
            s.activation(out=expd_p[:], in_=argb_p[:], func=AF.Exp, _inc=ac.inc())
            v.wait_ge(ac.h, ac.n)
            v.tensor_tensor(msb_p[:, :, :], sU3[:, :, :],
                            expd_p[:].rearrange("p (k w) -> p k w", w=256),
                            ALU.mult, _inc=dv.inc())
            msbdv = dv.n
            t.wait_ge(dv.h, max(msbdv, pyf))
            t.matmul(py_bank[:], xtm_p[:, 0, :], msb_p[:, 0, :], start=True, stop=False)
            t.matmul(py_bank[:], xtm_p[:, 1, :], msb_p[:, 1, :], start=False, stop=False)
            t.matmul(py_bank[:], Sb3[:, h, :], cwt_p[:], start=False, stop=True,
                     _inc=pe.inc())
            t.matmul(st_reg, btm3[:, 0, :], xw_p[:, 0, :], start=True, stop=False)
            t.matmul(st_reg, btm3[:, 1, :], xw_p[:, 1, :], start=False, stop=True,
                     _inc=pe.inc())
            ymm[p2] = pe.n
            v.wait_ge(pe.h, ymm[p2])
            v.scalar_tensor_tensor(yfull[:], xbcs3[:, h, :], dpc[:, h:h + 1],
                                   py_bank[:], ALU.mult, ALU.add, _inc=dv.inc())
            pyf = dv.n
            v.wait(dv)
            v.tensor_tensor(yz3[:, h, :], yfull[:], sz3[:, h, :], ALU.mult,
                            _inc=dv.inc())
            v.scalar_tensor_tensor(Sf3[:, h, :], Sf3[:, h, :], cdc[:, h:h + 1],
                                   st_reg, ALU.mult, ALU.add, _inc=dv.inc())
            stf[p2] = dv.n
            v.wait(dv)
            v.tensor_copy(Sb3[:, h, :], Sf3[:, h, :], _inc=dv.inc())
            yield
        for ec in range(16):
            sq = sqb[ec % 2]
            if ec >= 2:
                s.wait_ge(pe.h, sq_done[ec - 2])
            s.wait(dv)
            s.activation(out=sq[:], in_=yz3[:, ec, :], func=AF.Square, _inc=ac.inc())
            t.wait(ac)
            t.matmul(pnb[0:1, 256:512], C["ones_col"][:], sq[:],
                     start=(ec == 0), stop=(ec == 15),
                     _inc=pe.inc())
            sq_done[ec] = pe.n
        s.wait(pe)
        s.activation(out=rsd[:], in_=pnb[0:1, 256:512], func=AF.Sqrt,
                     scale=1.0 / 2048.0, bias=C["eps_col"][0:1, :], _inc=ac.inc())
        v.wait(ac)
        v.reciprocal(rsd[:], rsd[:], _inc=dv.inc())
        yield
        for ec in range(16):
            slot = ec % 2
            dst = pnb[:, 0:256] if slot == 0 else pnb[:, 256:512]
            t.wait(dv)
            t.matmul(dst, nw[:, ec * 128:(ec + 1) * 128], rsd[:],
                     start=True, stop=True, _inc=pe.inc())
            v.wait(pe)
            v.tensor_tensor(ynb3[:, ec, :], yz3[:, ec, :], dst, ALU.mult,
                            _inc=dv.inc())
            if ec == 7:
                yield
        yield
        for tc in range(2):
            pp = psc if tc == 0 else pnb
            for dh in range(2):
                t.wait(dv)
                for ec in range(16):
                    t.matmul(pp[:, 0:512], ynb3[:, ec, tc * 128:tc * 128 + 128],
                             opT3[:, ec, dh * 512:(dh + 1) * 512],
                             start=(ec == 0), stop=(ec == 15),
                             _inc=pe.inc() if ec == 15 else None)
                v.wait(pe)
                v.wait(sd)
                v.tensor_copy(mo[:], pp[:, 0:512], _inc=dv.inc())
                s.wait(dv)
                s.dma_start(out=D_["mamba_tm"][gtok0 + tc * 128:gtok0 + tc * 128 + 128,
                                               dh * 512:(dh + 1) * 512],
                            in_=mo[:], _inc=sd.inc(16))
            yield
        back_pe[par] = pe.n
        back_dv[par] = dv.n
        back_ac[par] = ac.n

    evac_done = {}
    NG = B * NCHUNK
    for _ in front_gen(0):
        pass
    fg = None
    for gc in range(NG):
        fg = front_gen(gc + 1) if gc + 1 < NG else None
        for _ in back_gen(gc):
            if fg is not None:
                next(fg, None)
                next(fg, None)
        if fg is not None:
            for _ in fg:
                pass
    return {}


def ph_attn(P, D_, C):
    E = P.E
    sy, v, s, t, g = E["sync"], E["vector"], E["scalar"], E["tensor"], E["gpsimd"]
    dv, ac, pe, gp = P.sem("dv"), P.sem("ac"), P.sem("pe"), P.sem("gp")
    ld = P.sem("atld")
    gl = P.sem("atgl")
    sd = P.sem("atsd")
    xfs = P.sem("atxf")
    wbs = [P.sem(f"atwb{i}") for i in range(2)]
    xts = [P.sem(f"atxt{i}") for i in range(2)]
    aos = [P.sem(f"atao{i}") for i in range(2)]

    wbuf = [P.sbuf("at_wb0", [128, 8 * 1024], BF16)]
    wbuf.append(wbuf[0])
    wb3 = [wbuf[0][:].rearrange("p (c e) -> p c e", e=1024)] * 2
    bq = P.sbuf("at_bq", [128, 8], F32)
    bk = P.sbuf("at_bk", [128, 8], F32)
    cosr = P.sbuf("at_cosr", [128, 2048], BF16)
    sinr = P.sbuf("at_sinr", [128, 2048], BF16)
    bvb = P.sbuf("at_bvb", [128, 1024], BF16)
    bob = P.sbuf("at_bob", [128, 1024], BF16)
    g1b = P.sbuf("at_g1b", [128, 1024], BF16)
    be1b = P.sbuf("at_be1b", [128, 1024], BF16)
    hselc = P.sbuf("at_hselc", [128, 8 * 16], BF16)
    hs3 = hselc[:].rearrange("p (c m) -> p c m", m=16)
    id16 = P.sbuf("at_id16", [16, 16], F32)
    negcb = P.sbuf("at_negcb", [128, 16], F32)
    mq = P.sbuf("at_mq", [16, 2], F32)
    crow = P.sbuf("at_crow", [1, 16], F32)

    sy.dma_start(out=bq[:], in_=D_["bq"], _inc=ld.inc(16))
    sy.dma_start(out=bk[:], in_=D_["bk"], _inc=ld.inc(16))
    g.dma_start(out=bvb[0:1, :], in_=D_["bv"], _inc=gl.inc(16))
    g.dma_start(out=bob[0:1, :], in_=D_["bo"], _inc=gl.inc(16))
    g.dma_start(out=g1b[0:1, :], in_=D_["g1"], _inc=gl.inc(16))
    g.dma_start(out=be1b[0:1, :], in_=D_["be1"], _inc=gl.inc(16))
    for j in range(4):
        g.dma_start(out=cosr[32 * j:32 * (j + 1), :], in_=D_["cos32"], _inc=gl.inc(16))
        g.dma_start(out=sinr[32 * j:32 * (j + 1), :], in_=D_["sin32"], _inc=gl.inc(16))
    g.memset(hselc[:], 0.0)
    g.memset(id16[:], 0.0, _inc=gp.inc())
    g.wait(gp)
    for oc in range(8):
        g.memset(hs3[0:64, oc, 2 * oc:2 * oc + 1], 1.0)
        g.memset(hs3[64:128, oc, 2 * oc + 1:2 * oc + 2], 1.0)
    g.affine_select(out=id16[:], in_=id16[:], compare_op=ALU.not_equal, fill=1.0,
                    base=0, pattern=[[-1, 16]], channel_multiplier=1, _inc=gp.inc())

    pA = [P.psum(f"at_pA{i}", [128, 512], F32) for i in range(2)]
    pA2 = [P.psum(f"at_pA2{i}", [128, 512], F32) for i in range(2)]
    pav = P.psum("at_pav", [128, 1024], F32)
    pav3 = pav[:].rearrange("p (q w) -> p q w", w=512)
    pN = P.psum("at_pN", [128, 512], F32)
    pT = P.psum("at_pT", [128, 256], BF16)

    t.wait(ld)
    t.wait(gl)
    t.wait(gp)
    for dst in (bvb, bob, g1b, be1b):
        for half in range(2):
            t.matmul(pN[:, 0:512], C["ones_row_bf"][:],
                     dst[0:1, half * 512:(half + 1) * 512],
                     start=True, stop=True, _inc=pe.inc())
            v.wait(pe)
            v.tensor_copy(dst[:, half * 512:(half + 1) * 512], pN[:, 0:512],
                          _inc=dv.inc())
            t.wait(dv)

    xnf = P.sbuf("at_xnf", [128, 8 * 2048], BF16)
    xnf3 = xnf[:].rearrange("p (c t) -> p c t", t=2048)
    ao3 = xnf3
    q3b = P.sbuf("at_q3", [128, 8 * 2048], BF16)
    q3 = q3b[:].rearrange("p (c t) -> p c t", t=2048)
    k3b = P.sbuf("at_k3", [128, 8 * 2048], BF16)
    k3 = k3b[:].rearrange("p (c t) -> p c t", t=2048)
    vtm = P.sbuf("at_vtm", [128, 16 * 1040], BF16)
    vtm3 = vtm[:].rearrange("p (c w) -> p c w", w=1040)
    ra = P.sbuf("at_ra", [128, 2048], BF16)
    rb = P.sbuf("at_rb", [128, 2048], BF16)
    rs = P.sbuf("at_rs", [128, 2048], BF16)
    pt_ = [P.sbuf(f"at_pt{i}", [128, 512], BF16) for i in range(2)]
    pt2_ = [P.sbuf(f"at_pt2{i}", [128, 512], BF16) for i in range(2)]
    drow = P.sbuf("at_drow", [1, 512], F32)
    rbsb = P.sbuf("at_rbsb", [64, 512], F32)
    stg = [P.sbuf(f"at_stg{i}", [64, 512], BF16) for i in range(2)]
    xnt = [P.sbuf(f"at_xnt{i}", [128, 1024], BF16) for i in range(2)]
    sf32 = P.sbuf("at_sf32", [128, 1024], F32)
    cent = P.sbuf("at_cent", [128, 1024], F32)
    sqd = pav[:, 0:1024]
    stm = P.sbuf("at_stm", [128, 1], F32)
    sts = P.sbuf("at_sts", [128, 1], F32)
    x1t = P.sbuf("at_x1t", [128, 1024], BF16)
    xft = P.sbuf("at_xft", [128, 1024], BF16)

    wb_used = [0, 0]
    xnt_used = [0, 0]
    exp_done = [0, 0]
    av_done = [0, 0]
    for b in range(B):
        bt0 = b * T
        sy.wait_ge(pe.h, pe.n)
        sy.dma_start(out=xnf3[:], in_=fm_ap(D_["xn_fm"], tsl=(bt0, bt0 + T)),
                     _inc=xfs.inc(16))
        g.wait_ge(pe.h, pe.n)
        ones_ap = AP(vtm[:].tensor, 64,
                     [vtm[:].ap[0], [1040, 16], [65, 16], [1, 1]])
        g.memset(ones_ap, 1.0, _inc=gp.inc())
        t.wait(gp)

        def load_w(wname, slot):
            g.wait_ge(pe.h, pe.n)
            g.dma_start(out=wb3[0][:, :, :],
                        in_=D_[wname].rearrange("(c p) e -> p c e", p=128),
                        _inc=wbs[0].inc(16))
        qk_dst = (q3, k3)
        for iw in range(2):
            slot = 0
            load_w(("wq", "wk")[iw], slot)
            t.wait_ge(wbs[slot].h, wbs[slot].n)
            t.wait_ge(xfs.h, xfs.n)
            dst = qk_dst[iw]
            bcol = (bq, bk)[iw]
            for oc in range(8):
                for qt in range(4):
                    pp = pA[(oc * 4 + qt) % 2]
                    t.wait(ac)
                    for dc in range(8):
                        t.matmul(pp[:, 0:512], wb3[slot][:, dc, oc * 128:(oc + 1) * 128],
                                 xnf3[:, dc, qt * 512:(qt + 1) * 512],
                                 start=(dc == 0), stop=(dc == 7),
                                 _inc=pe.inc() if dc == 7 else None)
                    s.wait(pe)
                    s.activation(out=dst[:, oc, qt * 512:(qt + 1) * 512], in_=pp[:, 0:512],
                                 func=AF.Identity, bias=bcol[:, oc:oc + 1], _inc=ac.inc())
            wb_used[slot] = pe.n
        load_w("wv", 0)
        t.wait_ge(wbs[0].h, wbs[0].n)
        t.wait_ge(xfs.h, xfs.n)
        for tc in range(16):
            for ot in range(2):
                pp = pA[(tc * 2 + ot) % 2]
                t.wait(dv)
                t.wait(ac)
                for dc in range(8):
                    t.matmul(pp[:, 0:512], xnf3[:, dc, tc * 128:(tc + 1) * 128],
                             wb3[0][:, dc, ot * 512:(ot + 1) * 512],
                             start=(dc == 0), stop=(dc == 7),
                             _inc=pe.inc() if dc == 7 else None)
                v.wait(pe)
                dst = AP(vtm[:].tensor, tc * 1040 + ot * 8 * 65,
                         [vtm[:].ap[0], [65, 8], [1, 64]])
                v.tensor_tensor(dst, pp[:, 0:512].rearrange("p (h w) -> p h w", w=64),
                                bvb[:, ot * 512:(ot + 1) * 512]
                                .rearrange("p (h w) -> p h w", w=64),
                                ALU.add, _inc=dv.inc())
        wb_used[0] = pe.n
        for dst3 in (q3, k3):
            for oc in range(8):
                v.wait(ac)
                v.wait(dv)
                v.tensor_tensor(ra[:], dst3[:, oc, :], cosr[:], ALU.mult)
                v.tensor_tensor(rb[:], dst3[:, oc, :], sinr[:], ALU.mult, _inc=dv.inc())
                v.wait(dv)
                v.tensor_scalar_mul(rs[0:32, :], rb[32:64, :], -1.0)
                v.tensor_copy(rs[32:64, :], rb[0:32, :])
                v.tensor_scalar_mul(rs[64:96, :], rb[96:128, :], -1.0)
                v.tensor_copy(rs[96:128, :], rb[64:96, :], _inc=dv.inc())
                v.wait(dv)
                v.tensor_tensor(dst3[:, oc, :], ra[:], rs[:], ALU.add, _inc=dv.inc())
        for iqk, src3 in enumerate((q3, k3)):
            for oc in range(8):
                s.wait(dv)
                s.wait(ac)
                if oc > 0:
                    s.wait_ge(pe.h, pe.n)
                s.activation(out=ra[:], in_=src3[:, oc, :], func=AF.Square, _inc=ac.inc())
                t.wait(ac)
                for qt in range(4):
                    ndst = pav3[0:16, qt, :] if qt < 2 else pA2[qt - 2][0:16, 0:512]
                    t.matmul(ndst, hs3[:, oc, :], ra[:, qt * 512:(qt + 1) * 512],
                             start=(oc == 0), stop=(oc == 7),
                             _inc=pe.inc() if qt == 3 else None)
            for qt in range(4):
                nsrc = pav3[0:16, qt, :] if qt < 2 else pA2[qt - 2][0:16, 0:512]
                v.wait(pe)
                if qt == 0:
                    v.reduce_max(mq[:, iqk:iqk + 1], nsrc, axis=AX.X,
                                 _inc=dv.inc())
                else:
                    v.wait(dv)
                    v.reduce_max(sts[0:16, :], nsrc,
                                 axis=AX.X, _inc=dv.inc())
                    v.wait(dv)
                    v.tensor_tensor(mq[:, iqk:iqk + 1], mq[:, iqk:iqk + 1],
                                    sts[0:16, :], ALU.max, _inc=dv.inc())
        s.wait(dv)
        s.activation(out=mq[:, 0:1], in_=mq[:, 0:1], func=AF.Sqrt, _inc=ac.inc())
        s.wait(ac)
        s.activation(out=mq[:, 1:2], in_=mq[:, 1:2], func=AF.Sqrt, _inc=ac.inc())
        v.wait(ac)
        v.tensor_tensor(mq[:, 0:1], mq[:, 0:1], mq[:, 1:2], ALU.mult, _inc=dv.inc())
        t.wait(dv)
        t.matmul(pN[0:1, 0:16], mq[:, 0:1], id16[:], start=True, stop=True, _inc=pe.inc())
        v.wait(pe)
        v.tensor_scalar_mul(crow[:], pN[0:1, 0:16], -0.125, _inc=dv.inc())
        t.wait(dv)
        t.matmul(pN[:, 16:32], C["ones_row"][:], crow[:], start=True, stop=True,
                 _inc=pe.inc())
        v.wait(pe)
        v.tensor_copy(negcb[:], pN[:, 16:32], _inc=dv.inc())
        for hp in range(8):
            oc = hp
            h0, h1 = 2 * hp, 2 * hp + 1
            for qt in range(4):
                t.wait(dv)
                s_pe = {}

                def emit_S(kc_):
                    j_ = kc_ % 2
                    t.wait_ge(ac.h, exp_done[j_])
                    t.matmul(pA[j_][:, 0:512],
                             k3[0:64, oc, kc_ * 128:(kc_ + 1) * 128],
                             q3[0:64, oc, qt * 512:(qt + 1) * 512],
                             start=True, stop=True, tile_position=(0, 0))
                    t.matmul(pA2[j_][:, 0:512],
                             k3[64:128, oc, kc_ * 128:(kc_ + 1) * 128],
                             q3[64:128, oc, qt * 512:(qt + 1) * 512],
                             start=True, stop=True, tile_position=(64, 0),
                             _inc=pe.inc())
                    s_pe[kc_] = pe.n

                emit_S(0)
                for kc in range(16):
                    j = kc % 2
                    if kc + 1 < 16:
                        emit_S(kc + 1)
                    s.wait_ge(pe.h, max(s_pe[kc], av_done[j]))
                    s.activation(out=pt_[j][:], in_=pA[j][:, 0:512], func=AF.Exp,
                                 scale=0.125, bias=negcb[:, h0:h0 + 1])
                    s.activation(out=pt2_[j][:], in_=pA2[j][:, 0:512], func=AF.Exp,
                                 scale=0.125, bias=negcb[:, h1:h1 + 1], _inc=ac.inc())
                    exp_done[j] = ac.n
                    t.wait_ge(ac.h, exp_done[j])
                    t.matmul(pav3[0:65, 0, :], vtm3[:, kc, h0 * 65:h0 * 65 + 65],
                             pt_[j][:], start=(kc == 0), stop=(kc == 15))
                    t.matmul(pav3[0:65, 1, :], vtm3[:, kc, h1 * 65:h1 * 65 + 65],
                             pt2_[j][:], start=(kc == 0), stop=(kc == 15),
                             _inc=pe.inc())
                    av_done[j] = pe.n
                for hi in range(2):
                    v.wait(pe)
                    v.wait(dv)
                    v.tensor_copy(drow[:], pav3[64:65, hi, :], _inc=dv.inc())
                    v.wait(dv)
                    v.reciprocal(drow[:], drow[:], _inc=dv.inc())
                    t.wait(dv)
                    t.matmul(pN[0:64, 0:512], C["ones_row"][0:1, 0:64], drow[:],
                             start=True, stop=True, _inc=pe.inc())
                    v.wait(pe)
                    v.tensor_copy(rbsb[:], pN[0:64, 0:512], _inc=dv.inc())
                    v.wait(dv)
                    if hi == 0:
                        v.tensor_tensor(ao3[0:64, oc, qt * 512:(qt + 1) * 512],
                                        pav3[0:64, 0, :], rbsb[:], ALU.mult,
                                        _inc=dv.inc())
                    else:
                        sg = stg[qt % 2]
                        v.wait_ge(aos[qt % 2].h, aos[qt % 2].n)
                        v.tensor_tensor(sg[:], pav3[0:64, 1, :], rbsb[:], ALU.mult,
                                        _inc=dv.inc())
                        sy.wait_ge(dv.h, dv.n)
                        sy.dma_start(out=ao3[64:128, oc, qt * 512:(qt + 1) * 512],
                                     in_=sg[:], _inc=aos[qt % 2].inc(16))
        if DEV and b == 0:
            for nm, ap_ in (("dbg_q3", q3b[:]), ("dbg_k3", k3b[:]),
                            ("dbg_vtm", vtm[:]), ("dbg_ao", xnf[:]),
                            ("dbg_negcb", negcb[:])):
                d = P.nc.dram_tensor(nm, list(ap_.shape), ap_.tensor.dtype,
                                     kind="ExternalOutput").ap()
                s.wait(dv)
                s.wait(ac)
                for a_ in aos:
                    s.wait_ge(a_.h, a_.n)
                s.dma_start(out=d, in_=ap_, _inc=sd.inc(16))
        load_w("wo", 1)
        t.wait_ge(wbs[0].h, wbs[0].n)
        for a_ in aos:
            t.wait_ge(a_.h, a_.n)
        t.wait(dv)
        for tc in range(16):
            xslot = tc % 2
            sy.wait_ge(dv.h, xnt_used[xslot])
            sy.dma_start(out=xnt[xslot][:],
                         in_=D_["xn_tm"][bt0 + tc * 128:bt0 + (tc + 1) * 128, :],
                         _inc=xts[xslot].inc(16))
            for dh in range(2):
                pp = pA[dh]
                t.wait(dv)
                for oc in range(8):
                    t.matmul(pp[:, 0:512], ao3[:, oc, tc * 128:(tc + 1) * 128],
                             wb3[1][:, oc, dh * 512:(dh + 1) * 512],
                             start=(oc == 0), stop=(oc == 7),
                             _inc=pe.inc() if oc == 7 else None)
                v.wait(pe)
                v.wait_ge(xts[xslot].h, xts[xslot].n)
                v.tensor_tensor(sf32[:, dh * 512:(dh + 1) * 512], pp[:, 0:512],
                                bob[:, dh * 512:(dh + 1) * 512], ALU.add, _inc=dv.inc())
                v.wait(dv)
                v.tensor_tensor(sf32[:, dh * 512:(dh + 1) * 512],
                                sf32[:, dh * 512:(dh + 1) * 512],
                                xnt[xslot][:, dh * 512:(dh + 1) * 512], ALU.add,
                                _inc=dv.inc())
            xnt_used[xslot] = dv.n
            emit_ln_tm(P, sf32[:], x1t[:], cent, sqd, stm, sts, C,
                       gain_bc=g1b[:], bias_bc=be1b[:])
            s.wait(dv)
            s.dma_start(out=D_["x1_tm"][bt0 + tc * 128:bt0 + (tc + 1) * 128, :],
                        in_=x1t[:], _inc=sd.inc(16))
            for j in range(4):
                t.wait(dv)
                t.transpose(pT[:, 0:128], x1t[:, (2 * j) * 128:(2 * j + 1) * 128],
                            C["ident_bf"][:])
                t.transpose(pT[:, 128:256], x1t[:, (2 * j + 1) * 128:(2 * j + 2) * 128],
                            C["ident_bf"][:], _inc=pe.inc())
                v.wait(pe)
                v.tensor_copy(xft[:, (2 * j) * 128:(2 * j + 2) * 128], pT[:],
                              _inc=dv.inc())
            s.wait(dv)
            s.dma_start(out=fm_ap(D_["x1_fm"], tsl=(bt0 + tc * 128, bt0 + (tc + 1) * 128)),
                        in_=xft[:].rearrange("p (c t) -> p c t", t=128), _inc=sd.inc(16))
            v.wait_ge(sd.h, sd.n)
    return {}


def ph_moe(P, D_, C):
    E = P.E
    sy, v, s, t, g = E["sync"], E["vector"], E["scalar"], E["tensor"], E["gpsimd"]
    dv, ac, pe, gp = P.sem("dv"), P.sem("ac"), P.sem("pe"), P.sem("gp")
    ld = P.sem("mold")
    gl = P.sem("mogl")
    sd = P.sem("mosd")
    x1s = [P.sem(f"mox1{i}") for i in range(2)]
    w1s = P.sem("mow1")
    w2s = [P.sem(f"mow2{i}") for i in range(2)]
    yts = [P.sem(f"moyt{i}") for i in range(2)]
    b1s = P.sem("mob1")
    fls = [P.sem(f"mofl{i}") for i in range(2)]

    wg_sb = P.sbuf("mo_wg", [128, 8 * 8], BF16)
    wg3 = wg_sb[:].rearrange("p (c e) -> p c e", e=8)
    bgb = P.sbuf("mo_bgb", [128, 8], F32)
    g2b = P.sbuf("mo_g2b", [128, 1024], BF16)
    be2b = P.sbuf("mo_be2b", [128, 1024], BF16)
    b2row = P.sbuf("mo_b2row", [1, 1024], BF16)
    comb = P.sbuf("mo_comb", [128, 32 * 8], F32)
    comb3 = comb[:].rearrange("p (c e) -> p c e", e=8)
    gbuf = P.sbuf("mo_gbuf", [128, 8], F32)
    e8 = P.sbuf("mo_e8", [128, 8], F32)
    c1 = P.sbuf("mo_c1", [128, 1], F32)
    c2 = P.sbuf("mo_c2", [128, 1], F32)
    c3 = P.sbuf("mo_c3", [128, 1], F32)
    x1b = [P.sbuf(f"mo_x1{i}", [128, 8 * 512], BF16) for i in range(2)]
    x13 = [x[:].rearrange("p (c t) -> p c t", t=512) for x in x1b]
    b1c = P.sbuf("mo_b1c", [128, 32], F32)
    yt = [P.sbuf(f"mo_yt{i}", [128, 512], F32) for i in range(2)]
    pH = [P.psum(f"mo_pH{i}", [128, 512], F32) for i in range(2)]
    pY = [P.psum(f"mo_pY{i}", [128, 512], F32) for i in range(2)]
    pG = P.psum("mo_pG", [128, 512], F32)
    w1b = P.sbuf("mo_w1b", [128, 8 * 4096], BF16)
    w13 = w1b[:].rearrange("p (c f) -> p c f", f=4096)
    w2b = [P.sbuf(f"mo_w2b{i}", [128, 32 * 512], BF16) for i in range(2)]
    w23 = [w[:].rearrange("p (c d) -> p c d", d=512) for w in w2b]
    h3b = P.sbuf("mo_h3", [128, 32 * 512], BF16)
    h3 = h3b[:].rearrange("p (c t) -> p c t", t=512)
    x1_used = [0, 0]
    h_done = {}
    fl_used = [0, 0]

    g.dma_start(out=wg_sb[:], in_=D_["wg"].rearrange("(c p) e -> p c e", p=128),
                _inc=gl.inc(16))
    g.dma_start(out=g2b[0:1, :], in_=D_["g2ln"], _inc=gl.inc(16))
    g.dma_start(out=be2b[0:1, :], in_=D_["be2ln"], _inc=gl.inc(16))
    g.dma_start(out=bgb[0:1, :], in_=D_["bgv"], _inc=gl.inc(16))
    t.wait(gl)
    t.matmul(pG[:, 0:8], C["ones_row"][:], bgb[0:1, :], start=True, stop=True,
             _inc=pe.inc())
    v.wait(pe)
    v.tensor_copy(bgb[:], pG[:, 0:8], _inc=dv.inc())
    for dst in (g2b, be2b):
        for half in range(2):
            t.wait(dv)
            t.matmul(pG[:, 0:512], C["ones_row_bf"][:],
                     dst[0:1, half * 512:(half + 1) * 512], start=True, stop=True,
                     _inc=pe.inc())
            v.wait(pe)
            v.tensor_copy(dst[:, half * 512:(half + 1) * 512], pG[:, 0:512],
                          _inc=dv.inc())

    for tt in range(8):
        slot = tt % 2
        sy.wait_ge(pe.h, x1_used[slot])
        sy.dma_start(out=x13[slot][:, :, :],
                     in_=fm_ap(D_["x1_fm"], tsl=(tt * 512, (tt + 1) * 512)),
                     _inc=x1s[slot].inc(16))
        t.wait_ge(x1s[slot].h, x1s[slot].n)
        for tc4 in range(4):
            tchunk = tt * 4 + tc4
            t.wait(dv)
            for dc in range(8):
                t.matmul(pG[:, 0:8], x13[slot][:, dc, tc4 * 128:(tc4 + 1) * 128],
                         wg3[:, dc, :], start=(dc == 0), stop=(dc == 7),
                         _inc=pe.inc() if dc == 7 else None)
            v.wait(pe)
            v.tensor_tensor(gbuf[:], pG[0:128, 0:8], bgb[:], ALU.add, _inc=dv.inc())
            v.wait(dv)
            v.reduce_max(c1[:], gbuf[:], axis=AX.X, negate=True, _inc=dv.inc())
            s.wait(dv)
            s.activation(out=e8[:], in_=gbuf[:], func=AF.Exp, bias=c1[:],
                         accum_out=c2[:], _inc=ac.inc())
            v.wait(ac)
            v.reciprocal(c2[:], c2[:], _inc=dv.inc())
            v.wait(dv)
            v.tensor_scalar(gbuf[:], e8[:], c2[:], None, ALU.mult, _inc=dv.inc())
            v.wait(dv)
            v.reduce_max(c1[:], gbuf[:], axis=AX.X, _inc=dv.inc())
            v.wait(dv)
            v.tensor_scalar(e8[:], gbuf[:], c1[:], None, ALU.is_ge, _inc=dv.inc())
            v.wait(dv)
            v.scalar_tensor_tensor(e8[:], e8[:], -1.0e9, gbuf[:], ALU.mult, ALU.add,
                                   _inc=dv.inc())
            v.wait(dv)
            v.reduce_max(c2[:], e8[:], axis=AX.X, _inc=dv.inc())
            v.wait(dv)
            v.tensor_scalar(e8[:], gbuf[:], c2[:], None, ALU.is_ge, _inc=dv.inc())
            v.tensor_tensor(c3[:], c1[:], c2[:], ALU.add, _inc=dv.inc())
            v.wait(dv)
            v.reciprocal(c3[:], c3[:], _inc=dv.inc())
            v.wait(dv)
            v.tensor_scalar(gbuf[:], gbuf[:], c3[:], None, ALU.mult, _inc=dv.inc())
            v.wait(dv)
            v.tensor_tensor(comb3[:, tchunk, :], gbuf[:], e8[:], ALU.mult,
                            _inc=dv.inc())
        x1_used[slot] = pe.n

    yacc = D_["y_acc"]
    w1r = D_["w1"].rearrange("e (c p) f -> e p c f", p=128)
    w2r = D_["w2"].rearrange("e (c p) d -> e p c d", p=128)
    w1_used = 0
    w2_used = [0, 0]
    b1_used = 0
    b2_used = 0
    ypd = {0: 0, 1: 0}
    b2s = P.sem("mob2")
    for e_ in range(8):
        g.wait_ge(pe.h, w1_used)
        g.dma_start(out=w13[:, :, :], in_=w1r[e_], _inc=w1s.inc(16))
        for half in range(2):
            g.wait_ge(pe.h, w2_used[half])
            g.dma_start(out=w23[half][:, :, :], in_=w2r[e_][:, :, half * 512:(half + 1) * 512],
                        _inc=w2s[half].inc(16))
        sy.wait_ge(ac.h, b1_used)
        sy.dma_start(out=b1c[:], in_=D_["b1"][e_], _inc=b1s.inc(16))
        g.wait_ge(pe.h, b2_used)
        g.dma_start(out=b2row[:], in_=D_["b2"][e_:e_ + 1, :], _inc=b2s.inc(16))
        t.wait_ge(w1s.h, w1s.n)
        t.wait_ge(b2s.h, b2s.n)
        s.wait_ge(b1s.h, b1s.n)
        for tt in range(8):
            slot = tt % 2
            sy.wait_ge(pe.h, x1_used[slot])
            sy.dma_start(out=x13[slot][:, :, :],
                         in_=fm_ap(D_["x1_fm"], tsl=(tt * 512, (tt + 1) * 512)),
                         _inc=x1s[slot].inc(16))
            t.wait_ge(x1s[slot].h, x1s[slot].n)
            for ffc in range(32):
                pp = pH[ffc % 2]
                if ffc >= 2:
                    t.wait_ge(ac.h, h_done[ffc - 2])
                for dc in range(8):
                    t.matmul(pp[:, 0:512], w13[:, dc, ffc * 128:(ffc + 1) * 128],
                             x13[slot][:, dc, :], start=(dc == 0), stop=(dc == 7),
                             _inc=pe.inc() if dc == 7 else None)
                s.wait(pe)
                s.activation(out=h3[:, ffc, :], in_=pp[:, 0:512], func=AF.Gelu,
                             bias=b1c[:, ffc:ffc + 1], _inc=ac.inc())
                h_done[ffc] = ac.n
            x1_used[slot] = pe.n
            b1_used = ac.n
            t.wait_ge(w2s[0].h, w2s[0].n)
            t.wait_ge(w2s[1].h, w2s[1].n)
            t.wait(ac)
            for tc4 in range(4):
                tchunk = tt * 4 + tc4
                for dt2 in range(2):
                    pp = pY[dt2]
                    slotY = (tc4 * 2 + dt2) % 2
                    t.wait_ge(dv.h, ypd[dt2])
                    for ffc in range(32):
                        t.matmul(pp[:, 0:512], h3[:, ffc, tc4 * 128:(tc4 + 1) * 128],
                                 w23[dt2][:, ffc, :], start=(ffc == 0), stop=False)
                    t.matmul(pp[:, 0:512], C["ones_row_bf"][0:1, 0:128],
                             b2row[0:1, dt2 * 512:(dt2 + 1) * 512],
                             start=False, stop=True, _inc=pe.inc())
                    v.wait(pe)
                    v.wait_ge(yts[slotY].h, yts[slotY].n)
                    v.tensor_scalar(yt[slotY][:], pp[:, 0:512],
                                    comb3[:, tchunk, e_:e_ + 1], None, ALU.mult,
                                    _inc=dv.inc())
                    ypd[dt2] = dv.n
                    g.wait_ge(dv.h, dv.n)
                    if e_ == 0:
                        g.dma_start(out=yacc[tchunk * 128:(tchunk + 1) * 128,
                                            dt2 * 512:(dt2 + 1) * 512],
                                    in_=yt[slotY][:], _inc=yts[slotY].inc(16))
                    else:
                        g.dma_start(out=yacc[tchunk * 128:(tchunk + 1) * 128,
                                            dt2 * 512:(dt2 + 1) * 512],
                                    in_=yt[slotY][:], accum_op=ALU.add,
                                    _inc=yts[slotY].inc(16))
            w2_used = [pe.n, pe.n]
        w1_used = pe.n
        b2_used = pe.n

    P.barrier()
    for nm in ("mo_h3", "mo_w2b1", "mo_w2b0", "mo_w1b"):
        P.free(nm)
    yab = [P.sbuf(f"mo_ya{i}", [128, 1024], F32) for i in range(2)]
    x1tb = [P.sbuf(f"mo_x1t{i}", [128, 1024], BF16) for i in range(2)]
    sf32 = P.sbuf("mo_sf32", [128, 1024], F32)
    cent = P.sbuf("mo_cent", [128, 1024], F32)
    stm = P.sbuf("mo_stm", [128, 1], F32)
    sts = P.sbuf("mo_sts", [128, 1], F32)
    x2t = P.sbuf("mo_x2t", [128, 1024], F32)
    sqbuf = P.sbuf("mo_sqbuf", [128, 1024], F32)
    for a_ in yts:
        sy.wait_ge(a_.h, a_.n)
    for tchunk in range(32):
        slot = tchunk % 2
        sy.wait_ge(dv.h, fl_used[slot])
        sy.dma_start(out=yab[slot][:], in_=yacc[tchunk * 128:(tchunk + 1) * 128, :],
                     _inc=fls[slot].inc(16))
        sy.dma_start(out=x1tb[slot][:],
                     in_=D_["x1_tm"][tchunk * 128:(tchunk + 1) * 128, :],
                     _inc=fls[slot].inc(16))
        v.wait_ge(fls[slot].h, fls[slot].n)
        v.wait(dv)
        v.tensor_tensor(sf32[:], yab[slot][:], x1tb[slot][:], ALU.add, _inc=dv.inc())
        fl_used[slot] = dv.n
        emit_ln_tm(P, sf32[:], x2t[:], cent, sqbuf, stm, sts, C,
                   gain_bc=g2b[:], bias_bc=be2b[:])
        s.wait(dv)
        s.dma_start(out=D_["ffn_tm"][tchunk * 128:(tchunk + 1) * 128, :], in_=x2t[:],
                    _inc=sd.inc(16))
        v.wait_ge(sd.h, sd.n)
    return {}


def ph_l1(P, D_, C):
    E = P.E
    sy, v, s, t, g = E["sync"], E["vector"], E["scalar"], E["tensor"], E["gpsimd"]
    dv, ac, pe, gp = P.sem("dv"), P.sem("ac"), P.sem("pe"), P.sem("gp")
    ld = P.sem("l1ld")
    gl = P.sem("l1gl")
    sd = P.sem("l1sd")
    cbs = [P.sem(f"l1cb{i}") for i in range(2)]
    xfs = P.sem("l1xf")
    x2s = [P.sem(f"l1x2{i}") for i in range(2)]
    wts = P.sem("l1wt")
    gts = [P.sem("l1gt0")] * 2
    fas = [P.sem("l1fa0")] * 2
    xns = P.sem("l1xn")

    stm = P.sbuf("l1_stm", [128, 1], F32)
    sts = P.sbuf("l1_sts", [128, 1], F32)
    ub = [P.sbuf(f"l1_u{i}", [128, 1024], F32) for i in range(2)]
    mb_ = [P.sbuf(f"l1_m{i}", [128, 1024], F32) for i in range(2)]
    fb = [P.sbuf(f"l1_f{i}", [128, 1024], F32) for i in range(2)]
    sf32 = P.sbuf("l1_sf32", [128, 1024], F32)
    cent = P.sbuf("l1_cent", [128, 1024], F32)
    sqbuf = P.sbuf("l1_sqbuf", [128, 1024], F32)
    xn2t_t = P.sbuf("l1_xn2t", [128, 1024], BF16)
    xft = P.sbuf("l1_xft", [128, 1024], BF16)
    pT = P.psum("l1_pT", [128, 256], BF16)
    pN = P.psum("l1_pN", [128, 512], F32)
    cb_used = [0, 0]
    for i in range(32):
        slot = i % 2
        sy.wait_ge(dv.h, cb_used[slot])
        sy.dma_start(out=ub[slot][:], in_=D_["u"][i * 128:(i + 1) * 128, :],
                     _inc=cbs[slot].inc(16))
        sy.dma_start(out=mb_[slot][:], in_=D_["mamba_tm"][i * 128:(i + 1) * 128, :],
                     _inc=cbs[slot].inc(16))
        sy.dma_start(out=fb[slot][:], in_=D_["ffn_tm"][i * 128:(i + 1) * 128, :],
                     _inc=cbs[slot].inc(16))
        v.wait_ge(cbs[slot].h, cbs[slot].n)
        v.wait(dv)
        v.tensor_tensor(sf32[:], ub[slot][:], mb_[slot][:], ALU.add, _inc=dv.inc())
        v.wait(dv)
        v.tensor_tensor(sf32[:], sf32[:], fb[slot][:], ALU.add, _inc=dv.inc())
        cb_used[slot] = dv.n
        s.wait(dv)
        s.dma_start(out=D_["xnew_tm"][i * 128:(i + 1) * 128, :], in_=sf32[:],
                    _inc=sd.inc(16))
        emit_ln_tm(P, sf32[:], xn2t_t[:], cent, sqbuf, stm, sts, C)
        s.wait(dv)
        s.dma_start(out=D_["xn2_tm"][i * 128:(i + 1) * 128, :], in_=xn2t_t[:],
                    _inc=sd.inc(16))
        for j in range(4):
            t.wait(dv)
            t.transpose(pT[:, 0:128], xn2t_t[:, (2 * j) * 128:(2 * j + 1) * 128],
                        C["ident_bf"][:])
            t.transpose(pT[:, 128:256], xn2t_t[:, (2 * j + 1) * 128:(2 * j + 2) * 128],
                        C["ident_bf"][:], _inc=pe.inc())
            v.wait(pe)
            v.tensor_copy(xft[:, (2 * j) * 128:(2 * j + 2) * 128], pT[:], _inc=dv.inc())
        s.wait(dv)
        s.dma_start(out=fm_ap(D_["xn2_fm"], tsl=(i * 128, (i + 1) * 128)),
                    in_=xft[:].rearrange("p (c t) -> p c t", t=128), _inc=sd.inc(16))
        v.wait_ge(sd.h, sd.n)

    wtf = P.sbuf("l1_wtf", [128, 8 * 1024], F32)
    wtf3 = wtf[:].rearrange("p (c d) -> p c d", d=1024)
    wtb = P.sbuf("l1_wtb", [128, 8 * 1024], BF16)
    wtb3 = wtb[:].rearrange("p (c d) -> p c d", d=1024)
    tern_a = P.sbuf("l1_terna", [128, 1024], F32)
    tern_b = P.sbuf("l1_ternb", [128, 1024], F32)
    accw = P.sbuf("l1_accw", [128, 1], F32)
    dcol = P.sbuf("l1_dcol", [128, 2], F32)
    alcol = P.sbuf("l1_alcol", [128, 1], F32)
    b1b = P.sbuf("l1_b1b", [128, 1024], F32)
    alrow = P.sbuf("l1_alrow", [1, 1], F32)
    x2tl = [P.sbuf(f"l1_x2tl{i}", [128, 8 * 512], BF16) for i in range(2)]
    x2tl3 = [x[:].rearrange("p (c t) -> p c t", t=512) for x in x2tl]
    l1f = P.sbuf("l1_l1f", [128, 1024], F32)
    pO = [P.psum(f"l1_pO{i}", [128, 512], F32) for i in range(2)]
    sy.dma_start(out=wtf[:].rearrange("p (c d) -> p c d", d=1024),
                 in_=D_["wtT"].rearrange("(c p) d -> p c d", p=128), _inc=ld.inc(16))
    sy.dma_start(out=alrow[:], in_=D_["alpha"], _inc=ld.inc(16))
    sy.dma_start(out=b1b[0:1, :], in_=D_["bias1"], _inc=ld.inc(16))
    v.wait(ld)
    for dc in range(8):
        v.wait(dv)
        if dc == 0:
            v.tensor_reduce(accw[:], wtf3[:, 0, :], axis=AX.X, op=ALU.add,
                            apply_absolute_value=True, _inc=dv.inc())
        else:
            v.tensor_reduce(sts[:], wtf3[:, dc, :], axis=AX.X, op=ALU.add,
                            apply_absolute_value=True, _inc=dv.inc())
            v.wait(dv)
            v.tensor_tensor(accw[:], accw[:], sts[:], ALU.add, _inc=dv.inc())
    t.wait(dv)
    t.matmul(pN[0:1, 0:1], C["ones_col"][:], accw[:], start=True, stop=True,
             _inc=pe.inc())
    v.wait(pe)
    v.tensor_scalar(stm[0:1, :], pN[0:1, 0:1], 0.7 / (1024.0 * 1024.0), None,
                    ALU.mult, _inc=dv.inc())
    t.wait(dv)
    t.matmul(pN[:, 1:2], C["ones_row"][:], stm[0:1, :], start=True, stop=True)
    t.matmul(pN[:, 2:3], C["ones_row"][:], alrow[:], start=True, stop=True,
             _inc=pe.inc())
    v.wait(pe)
    v.tensor_copy(dcol[:, 0:1], pN[:, 1:2])
    v.tensor_scalar_mul(dcol[:, 1:2], pN[:, 1:2], -1.0)
    v.tensor_copy(alcol[:], pN[:, 2:3], _inc=dv.inc())
    for half in range(2):
        t.wait(dv)
        t.matmul(pN[:, 0:512], C["ones_row"][:], b1b[0:1, half * 512:(half + 1) * 512],
                 start=True, stop=True, _inc=pe.inc())
        v.wait(pe)
        v.tensor_copy(b1b[:, half * 512:(half + 1) * 512], pN[:, 0:512], _inc=dv.inc())
    for dc in range(8):
        v.wait(dv)
        v.tensor_scalar(tern_a[:], wtf3[:, dc, :], dcol[:, 0:1], None, ALU.is_gt)
        v.tensor_scalar(tern_b[:], wtf3[:, dc, :], dcol[:, 1:2], None, ALU.is_lt,
                        _inc=dv.inc())
        v.wait(dv)
        v.tensor_tensor(wtb3[:, dc, :], tern_a[:], tern_b[:], ALU.subtract,
                        _inc=dv.inc())
    x2_used = [0, 0]
    for tt in range(8):
        slot = tt % 2
        sy.wait_ge(pe.h, x2_used[slot])
        sy.dma_start(out=x2tl3[slot][:, :, :],
                     in_=fm_ap(D_["xn2_fm"], tsl=(tt * 512, (tt + 1) * 512)),
                     _inc=x2s[slot].inc(16))
        t.wait_ge(x2s[slot].h, x2s[slot].n)
        t.wait(dv)
        for tc4 in range(4):
            for ot in range(2):
                pp = pO[ot]
                t.wait(dv)
                for dc in range(8):
                    t.matmul(pp[:, 0:512], x2tl3[slot][:, dc, tc4 * 128:(tc4 + 1) * 128],
                             wtb3[:, dc, ot * 512:(ot + 1) * 512],
                             start=(dc == 0), stop=(dc == 7),
                             _inc=pe.inc() if dc == 7 else None)
                v.wait(pe)
                v.scalar_tensor_tensor(l1f[:, ot * 512:(ot + 1) * 512], pp[:, 0:512],
                                       alcol[:], b1b[:, ot * 512:(ot + 1) * 512],
                                       ALU.mult, ALU.add, _inc=dv.inc())
            s.wait(dv)
            s.dma_start(out=D_["l1ffn_tm"][(tt * 4 + tc4) * 128:(tt * 4 + tc4 + 1) * 128, :],
                        in_=l1f[:], _inc=sd.inc(16))
            v.wait_ge(sd.h, sd.n)
        x2_used[slot] = pe.n

    P.barrier()
    for nm in ("l1_l1f", "l1_x2tl1", "l1_x2tl0", "l1_alrow", "l1_b1b", "l1_alcol",
               "l1_dcol", "l1_accw", "l1_ternb", "l1_terna", "l1_wtb", "l1_wtf",
               "l1_xft", "l1_xn2t", "l1_sqbuf", "l1_cent", "l1_sf32",
               "l1_f1", "l1_f0", "l1_m1", "l1_m0", "l1_u1", "l1_u0"):
        P.free(nm)
    wqb = P.sbuf("l1_wqb", [128, 8 * 1024], BF16)
    wq3 = wqb[:].rearrange("p (c e) -> p c e", e=1024)
    bqc = P.sbuf("l1_bqc", [128, 8], F32)
    bkc = P.sbuf("l1_bkc", [128, 8], F32)
    xn2f = P.sbuf("l1_xn2f", [128, 8 * 2048], BF16)
    xn2f3 = xn2f[:].rearrange("p (c t) -> p c t", t=2048)
    q1b = P.sbuf("l1_q1", [128, 8 * 2048], BF16)
    q13 = q1b[:].rearrange("p (c t) -> p c t", t=2048)
    k1b = P.sbuf("l1_k1", [128, 8 * 2048], BF16)
    k13 = k1b[:].rearrange("p (c t) -> p c t", t=2048)
    hselc = P.sbuf("l1_hselc", [128, 8 * 16], BF16)
    hs3 = hselc[:].rearrange("p (c m) -> p c m", m=16)
    id16 = P.sbuf("l1_id16", [16, 16], F32)
    negcb = P.sbuf("l1_negcb", [128, 16], F32)
    mq = P.sbuf("l1_mq", [16, 2], F32)
    crow = P.sbuf("l1_crow", [1, 16], F32)
    mn = P.sbuf("l1_mn", [128, 128], BF16)
    zbs = [P.sbuf("l1_zb0", [128, 2048], F32), P.sbuf("l1_zb1", [128, 128], F32)]
    zb = zbs[0]
    pbs = [P.sbuf(f"l1_pb{i}", [128, 2048], BF16) for i in range(2)]
    stmh = P.sbuf("l1_stmh", [128, 4], F32)
    stsh = P.sbuf("l1_stsh", [128, 2], F32)
    sqr = pbs[0]
    avg = P.sbuf("l1_avg", [128, 2048], F32)
    g2t = [P.sbuf("l1_g2t0", [128, 2048], F32)]
    g2t.append(g2t[0])
    dwn = P.sbuf("l1_dwn", [128, 2048], BF16)
    dwtc = P.sbuf("l1_dwtc", [128, 16 * 128], BF16)
    dwt3 = dwtc[:].rearrange("p (c q) -> p c q", q=128)
    xn2ts = P.sbuf("l1_xn2ts", [128, 16 * 1024], BF16)
    xn2ts3 = xn2ts[:].rearrange("p (c e) -> p c e", e=1024)
    xnb2 = [P.sbuf("l1_xnb0", [128, 1024], F32)]
    xnb2.append(xnb2[0])
    l1fb = [P.sbuf("l1_l1fb0", [128, 1024], F32)]
    l1fb.append(l1fb[0])
    pSf = P.psum("l1_pSf", [128, 2048], F32)
    pS = [pSf[:, 0:1024], pSf[:, 1024:2048]]

    g.memset(hselc[:], 0.0)
    g.memset(id16[:], 0.0, _inc=gp.inc())
    g.wait(gp)
    for oc in range(8):
        g.memset(hs3[0:64, oc, 2 * oc:2 * oc + 1], 1.0)
        g.memset(hs3[64:128, oc, 2 * oc + 1:2 * oc + 2], 1.0)
    g.affine_select(out=id16[:], in_=id16[:], compare_op=ALU.not_equal, fill=1.0,
                    base=0, pattern=[[-1, 16]], channel_multiplier=1, _inc=gp.inc())
    g.memset(mn[:], -1.0e9, _inc=gp.inc())
    g.wait(gp)
    g.affine_select(out=mn[:], in_=mn[:], compare_op=ALU.is_ge, fill=0.0,
                    base=0, pattern=[[1, 128]], channel_multiplier=-1, _inc=gp.inc())
    sy.dma_start(out=bqc[:], in_=D_["bq1"], _inc=ld.inc(16))
    sy.dma_start(out=bkc[:], in_=D_["bk1"], _inc=ld.inc(16))
    t.wait(gp)
    t.wait(ld)

    wt_used = 0
    gt_used = [0, 0]
    fa_used = [0, 0]
    for b in range(B):
        bt0 = b * T
        sy.wait_ge(pe.h, pe.n)
        sy.dma_start(out=xn2f3[:], in_=fm_ap(D_["xn2_fm"], tsl=(bt0, bt0 + T)),
                     _inc=xfs.inc(16))
        qk_dst = (q13, k13)
        for iw in range(2):
            g.wait_ge(pe.h, wt_used)
            g.dma_start(out=wq3[:, :, :],
                        in_=D_[("wq1T", "wk1T")[iw]].rearrange("(c p) e -> p c e", p=128),
                        _inc=wts.inc(16))
            t.wait_ge(wts.h, wts.n)
            t.wait_ge(xfs.h, xfs.n)
            t.wait(dv)
            dst = qk_dst[iw]
            bcol = (bqc, bkc)[iw]
            for oc in range(8):
                for qt in range(4):
                    pp = pO[(oc * 4 + qt) % 2]
                    t.wait(ac)
                    for dc in range(8):
                        t.matmul(pp[:, 0:512], wq3[:, dc, oc * 128:(oc + 1) * 128],
                                 xn2f3[:, dc, qt * 512:(qt + 1) * 512],
                                 start=(dc == 0), stop=(dc == 7),
                                 _inc=pe.inc() if dc == 7 else None)
                    s.wait(pe)
                    s.activation(out=dst[:, oc, qt * 512:(qt + 1) * 512],
                                 in_=pp[:, 0:512], func=AF.Identity,
                                 bias=bcol[:, oc:oc + 1], _inc=ac.inc())
            wt_used = pe.n
        for iqk, src3 in enumerate((q13, k13)):
            for oc in range(8):
                s.wait(dv)
                s.wait(ac)
                if oc > 0:
                    s.wait_ge(pe.h, pe.n)
                s.activation(out=sqr[:], in_=src3[:, oc, :], func=AF.Square,
                             _inc=ac.inc())
                t.wait(ac)
                for qt in range(4):
                    t.matmul(pS[qt // 2][0:16, (qt % 2) * 512:(qt % 2) * 512 + 512],
                             hs3[:, oc, :], sqr[:, qt * 512:(qt + 1) * 512],
                             start=(oc == 0), stop=(oc == 7),
                             _inc=pe.inc() if qt == 3 else None)
            for qt in range(4):
                v.wait(pe)
                if qt == 0:
                    v.reduce_max(mq[:, iqk:iqk + 1],
                                 pS[0][0:16, 0:512], axis=AX.X, _inc=dv.inc())
                else:
                    v.wait(dv)
                    v.reduce_max(sts[0:16, :],
                                 pS[qt // 2][0:16, (qt % 2) * 512:(qt % 2) * 512 + 512],
                                 axis=AX.X, _inc=dv.inc())
                    v.wait(dv)
                    v.tensor_tensor(mq[:, iqk:iqk + 1], mq[:, iqk:iqk + 1],
                                    sts[0:16, :], ALU.max, _inc=dv.inc())
        s.wait(dv)
        s.activation(out=mq[:, 0:1], in_=mq[:, 0:1], func=AF.Sqrt, _inc=ac.inc())
        s.wait(ac)
        s.activation(out=mq[:, 1:2], in_=mq[:, 1:2], func=AF.Sqrt, _inc=ac.inc())
        v.wait(ac)
        v.tensor_tensor(mq[:, 0:1], mq[:, 0:1], mq[:, 1:2], ALU.mult, _inc=dv.inc())
        t.wait(dv)
        t.matmul(pN[0:1, 0:16], mq[:, 0:1], id16[:], start=True, stop=True,
                 _inc=pe.inc())
        v.wait(pe)
        v.tensor_scalar_mul(crow[:], pN[0:1, 0:16], -0.125, _inc=dv.inc())
        t.wait(dv)
        t.matmul(pN[:, 16:32], C["ones_row"][:], crow[:], start=True, stop=True,
                 _inc=pe.inc())
        v.wait(pe)
        v.tensor_copy(negcb[:], pN[:, 16:32], _inc=dv.inc())
        sy.wait_ge(pe.h, pe.n)
        sy.dma_start(out=xn2ts3[:, :, :],
                     in_=D_["xn2_tm"][bt0:bt0 + T, :].rearrange("(c p) e -> p c e", p=128),
                     _inc=xns.inc(16))
        exp_prev = 0
        diag_prev = 0
        zb_free = [0, 0]
        pb_done = [0, 0]
        for qc in range(16):
            w = (qc + 1) * 128
            nk5 = (w + 511) // 512
            gslot = qc % 2
            sy.wait_ge(dv.h, gt_used[gslot])
            sy.dma_start(out=g2t[gslot][:, 0:w],
                         in_=D_["gum2"][bt0 + qc * 128:bt0 + (qc + 1) * 128, 0:w],
                         _inc=gts[gslot].inc(16))
            exp_ac = {}

            def emit_rcstt(h2):
                j2 = h2 % 2
                v.wait_ge(ac.h, exp_ac[h2])
                if qc == 0:
                    v.tensor_scalar(stsh[:, j2:j2 + 1], stmh[:, 2 * j2 + 1:2 * j2 + 2],
                                    16.0, 1.6e-29, ALU.mult, ALU.max, _inc=dv.inc())
                else:
                    v.tensor_tensor(stsh[:, j2:j2 + 1], stmh[:, 2 * j2:2 * j2 + 1],
                                    stmh[:, 2 * j2 + 1:2 * j2 + 2], ALU.add,
                                    _inc=dv.inc())
                    v.wait(dv)
                    v.tensor_scalar_mul(stsh[:, j2:j2 + 1], stsh[:, j2:j2 + 1], 16.0,
                                        _inc=dv.inc())
                v.wait(dv)
                v.reciprocal(stsh[:, j2:j2 + 1], stsh[:, j2:j2 + 1], _inc=dv.inc())
                v.wait(dv)
                if h2 == 0:
                    v.tensor_scalar(avg[:, 0:w], pbs[j2][:, 0:w], stsh[:, j2:j2 + 1],
                                    None, ALU.mult, _inc=dv.inc())
                else:
                    v.scalar_tensor_tensor(avg[:, 0:w], pbs[j2][:, 0:w],
                                           stsh[:, j2:j2 + 1],
                                           avg[:, 0:w], ALU.mult, ALU.add,
                                           _inc=dv.inc())
                pb_done[j2] = dv.n

            for h in range(16):
                r0 = (h % 2) * 64
                oc = h // 2
                j = h % 2
                t.wait_ge(ac.h, exp_prev)
                t.wait_ge(dv.h, diag_prev)
                for kt in range(nk5):
                    kw = min(512, w - kt * 512)
                    t.matmul(pSf[:, kt * 512:kt * 512 + kw],
                             q13[r0:r0 + 64, oc, qc * 128:(qc + 1) * 128],
                             k13[r0:r0 + 64, oc, kt * 512:kt * 512 + kw],
                             start=True, stop=True,
                             _inc=pe.inc() if kt == nk5 - 1 else None)
                spe = pe.n
                v.wait_ge(pe.h, spe)
                v.wait_ge(ac.h, zb_free[j])
                v.tensor_tensor(zbs[j][:, 0:128], pSf[:, qc * 128:qc * 128 + 128],
                                mn[:], ALU.add, _inc=dv.inc())
                diag_prev = dv.n
                s.wait_ge(dv.h, max(diag_prev, pb_done[j]))
                if qc > 0:
                    s.wait_ge(pe.h, spe)
                    s.activation(out=pbs[j][:, 0:qc * 128], in_=pSf[:, 0:qc * 128],
                                 func=AF.Exp, scale=0.125,
                                 bias=negcb[:, h:h + 1],
                                 accum_out=stmh[:, 2 * j:2 * j + 1])
                s.activation(out=pbs[j][:, qc * 128:w], in_=zbs[j][:, 0:128],
                             func=AF.Exp, scale=0.125,
                             bias=negcb[:, h:h + 1],
                             accum_out=stmh[:, 2 * j + 1:2 * j + 2], _inc=ac.inc())
                exp_ac[h] = ac.n
                exp_prev = ac.n
                zb_free[j] = ac.n
                if h >= 1:
                    emit_rcstt(h - 1)
            emit_rcstt(15)
            v.wait_ge(gts[gslot].h, gts[gslot].n)
            v.wait(dv)
            v.wait(gp)
            v.scalar_tensor_tensor(zb[:, 0:w], avg[:, 0:w], 0.5, g2t[gslot][:, 0:w],
                                   ALU.mult, ALU.add, _inc=dv.inc())
            gt_used[0] = gt_used[1] = dv.n
            v.wait(dv)
            v.reduce_max(stm[:], zb[:, 0:w], axis=AX.X, negate=True, _inc=dv.inc())
            s.wait(dv)
            s.activation(out=dwn[:, 0:w], in_=zb[:, 0:w], func=AF.Exp, bias=stm[:],
                         accum_out=sts[:], _inc=ac.inc())
            zb_free[0] = ac.n
            v.wait(ac)
            v.tensor_single_scalar(sts[:], sts[:], 1.0e-30, ALU.max, _inc=dv.inc())
            v.wait(dv)
            v.reciprocal(sts[:], sts[:], _inc=dv.inc())
            v.wait(dv)
            v.tensor_scalar(dwn[:, 0:w], dwn[:, 0:w], sts[:], None, ALU.mult,
                            _inc=dv.inc())
            for kc in range(qc + 1):
                t.wait(dv)
                t.transpose(pT[:, 0:128], dwn[:, kc * 128:(kc + 1) * 128],
                            C["ident_bf"][:], _inc=pe.inc())
                v.wait(pe)
                v.tensor_copy(dwt3[:, kc, :], pT[:, 0:128], _inc=dv.inc())
            fslot = qc % 2
            sy.wait_ge(dv.h, fa_used[fslot])
            sy.wait_ge(sd.h, sd.n)
            sy.dma_start(out=xnb2[fslot][:],
                         in_=D_["xnew_tm"][bt0 + qc * 128:bt0 + (qc + 1) * 128, :],
                         _inc=fas[fslot].inc(16))
            sy.dma_start(out=l1fb[fslot][:],
                         in_=D_["l1ffn_tm"][bt0 + qc * 128:bt0 + (qc + 1) * 128, :],
                         _inc=fas[fslot].inc(16))
            t.wait_ge(xns.h, xns.n)
            for et in range(2):
                pp = pO[et]
                t.wait(dv)
                for kc in range(qc + 1):
                    t.matmul(pp[:, 0:512], dwt3[:, kc, :],
                             xn2ts3[:, kc, et * 512:(et + 1) * 512],
                             start=(kc == 0), stop=(kc == qc),
                             _inc=pe.inc() if kc == qc else None)
                if qc == 0:
                    v.wait(pe)
                    v.memset(pp[0:1, 0:512], 0.0, _inc=dv.inc())
            v.wait(pe)
            v.wait(dv)
            v.wait_ge(fas[fslot].h, fas[fslot].n)
            for et in range(2):
                v.tensor_tensor(xnb2[fslot][:, et * 512:(et + 1) * 512],
                                xnb2[fslot][:, et * 512:(et + 1) * 512],
                                pO[et][:, 0:512], ALU.add, _inc=dv.inc())
            v.wait(dv)
            v.tensor_tensor(xnb2[fslot][:], xnb2[fslot][:], l1fb[fslot][:], ALU.add,
                            _inc=dv.inc())
            fa_used[0] = fa_used[1] = dv.n
            s.wait(dv)
            s.dma_start(out=D_["out"][bt0 + qc * 128:bt0 + (qc + 1) * 128, :],
                        in_=xnb2[fslot][:], _inc=sd.inc(16))
    return {}



PHASE_DEFS = None


def build_launch(phases, keep_outputs=None):
    nc = bass.Bass("TRN2", target_bir_lowering=False)
    needed = set()
    made_all = set()
    for name, fn, needs, makes in phases:
        for n in needs:
            if n not in made_all:
                needed.add(n)
        made_all |= set(makes)
    D_ = {}
    ext_in = []
    ext_out = []
    for n in sorted(needed):
        if n in SHARED_IN:
            sh, dt = SHARED_IN[n]
        elif n in PERCORE_IN:
            sh, dt = PERCORE_IN[n]
        elif n in SCRATCH:
            sh, dt = SCRATCH[n]
        else:
            raise KeyError(n)
        D_[n] = nc.dram_tensor(n, list(sh), dt, kind="ExternalInput").ap()
        ext_in.append(n)
    for n in sorted(made_all):
        if n == OUT_NAME:
            sh, dt = OUT_SHAPE, F32
        else:
            sh, dt = SCRATCH[n]
        if keep_outputs is None or n in keep_outputs:
            D_[n] = nc.dram_tensor(n, list(sh), dt, kind="ExternalOutput").ap()
            ext_out.append(n)
        else:
            D_[n] = nc.dram_tensor(n, list(sh), dt, kind="Internal").ap()

    P = Prog(nc)
    C = make_consts(P, D_)
    for name, fn, needs, makes in phases:
        fn(P, D_, C)
        P.barrier()
        P.free_prefix(_PHASE_PREFIX.get(name, "\x00"))
    sy = P.E["sync"]
    for snm in ("sd", "ld", "wt", "st"):
        if snm in P._sems:
            sy.wait(P.sem(snm))
    P.finish()
    return nc, ext_in, ext_out


def kernel(**inputs):
    H = _prep_host(inputs)
    u_full = H.pop("_u_full")
    gum_full = H.pop("_gum2_full")

    percore = []
    for c in range(8):
        percore.append({
            "u": np.ascontiguousarray(u_full[2 * c:2 * c + 2].reshape(NT, D)),
            "gum2": np.ascontiguousarray(gum_full[2 * c:2 * c + 2].reshape(NT, T)),
        })

    state = [dict() for _ in range(8)]
    for gi, grp in enumerate(LAUNCH_GROUPS):
        phases = [PHASE_DEFS[nm] for nm in grp]
        later_needs = {OUT_NAME}
        for g2 in LAUNCH_GROUPS[gi + 1:]:
            for nm in g2:
                later_needs |= set(PHASE_DEFS[nm][2])
        keep = {n for nm in grp for n in PHASE_DEFS[nm][3] if n in later_needs}
        nc, ext_in, ext_out = build_launch(phases, keep_outputs=keep)
        in_maps = []
        for c in range(8):
            m = {}
            for n in ext_in:
                if n in SHARED_IN:
                    m[n] = H[n]
                elif n in PERCORE_IN:
                    m[n] = percore[c][n]
                else:
                    m[n] = state[c][n]
            in_maps.append(m)
        res = run_bass_kernel_spmd(nc, in_maps, core_ids=list(range(8)))
        for c in range(8):
            for n in ext_out:
                state[c][n] = res.results[c][n]

    out = np.concatenate([np.asarray(state[c][OUT_NAME], np.float32)[None]
                          for c in range(8)], axis=0)
    return np.ascontiguousarray(out.reshape(16, 2048, 1024))


def _register_phases():
    global PHASE_DEFS
    PHASE_DEFS = {
        "prep": ("prep", ph_prep, ["u"], ["xn_tm", "xn_fm"]),
        "mamba": ("mamba", ph_mamba,
                  ["xn_fm", "inprojT", "convw", "convb", "dtb", "alog", "dp",
                   "normw", "oprojT"],
                  ["mamba_tm"]),
        "attn": ("attn", ph_attn,
                 ["xn_fm", "xn_tm", "wq", "wk", "wv", "wo", "bq", "bk", "bv",
                  "bo", "g1", "be1", "cos32", "sin32"],
                 ["x1_fm", "x1_tm"]),
        "moe": ("moe", ph_moe,
                ["x1_fm", "x1_tm", "wg", "bgv", "w1", "b1", "w2", "b2",
                 "g2ln", "be2ln"],
                ["ffn_tm", "y_acc"]),
        "l1": ("l1", ph_l1,
               ["u", "mamba_tm", "ffn_tm", "wq1T", "wk1T", "bq1", "bk1",
                "wtT", "alpha", "bias1", "gum2"],
               ["xnew_tm", "xn2_tm", "xn2_fm", "l1ffn_tm", "out"]),
    }


_register_phases()
LAUNCH_GROUPS = [["prep", "mamba", "attn", "moe", "l1"]]


# version 13
# speedup vs baseline: 1.0661x; 1.0661x over previous
import math
from contextlib import ExitStack

import numpy as np
import ml_dtypes

import concourse.bass as bass
import concourse.mybir as mybir
from concourse.bass import AP
from concourse.bass_utils import run_bass_kernel_spmd

F32 = mybir.dt.float32
BF16 = mybir.dt.bfloat16
AF = mybir.ActivationFunctionType
ALU = mybir.AluOpType
AX = mybir.AxisListType

ENGINES = ("sync", "tensor", "vector", "scalar", "gpsimd")

B = 2
T = 2048
NT = B * T
D = 1024
NCHUNK = T // 256

DEV = False



class Sem:
    __slots__ = ("h", "n")

    def __init__(self, h):
        self.h = h
        self.n = 0

    def inc(self, k=1):
        self.n += k
        return (self.h, k)


class Rec:

    def __init__(self, prog, name):
        self._p = prog
        self._name = name

    def __getattr__(self, meth):
        p, name = self._p, self._name

        def rec(*args, _inc=None, **kw):
            def run(eng, meth=meth, args=args, kw=kw, _inc=_inc):
                ins = getattr(eng, meth)(*args, **kw)
                if _inc is not None:
                    ins.then_inc(_inc[0], _inc[1])
            p.emitters[name].append(run)
        return rec

    def wait(self, sem):
        self.wait_ge(sem.h, sem.n)


class Prog:
    def __init__(self, nc):
        self.nc = nc
        self.emitters = {e: [] for e in ENGINES}
        self.E = {e: Rec(self, e) for e in ENGINES}
        self._sems = {}
        self._guards = {}
        self._order = []

    def sem(self, name):
        if name not in self._sems:
            g = self.nc.semaphore(name)
            self._sems[name] = Sem(g.__enter__())
            self._guards["sem_" + name] = g
        return self._sems[name]

    def sbuf(self, name, shape, dt):
        g = self.nc.sbuf_tensor(name, shape, dt)
        h = g.__enter__()
        self._guards[name] = g
        self._order = getattr(self, "_order", [])
        self._order.append(name)
        return h

    def psum(self, name, shape, dt=F32):
        g = self.nc.psum_tensor(name, shape, dt)
        h = g.__enter__()
        self._guards[name] = g
        self._order = getattr(self, "_order", [])
        self._order.append(name)
        return h

    def free(self, name):
        g = self._guards.pop(name)
        self._order.remove(name)
        g.__exit__(None, None, None)

    def free_prefix(self, prefix):
        for name in [n for n in reversed(self._order) if n.startswith(prefix)]:
            self.free(name)

    def barrier(self):
        for e in ENGINES:
            for s in self._sems.values():
                self.E[e].wait_ge(s.h, s.n)

    def finish(self):
        def runner(name):
            fns = self.emitters[name]

            def run(eng):
                for f in fns:
                    f(eng)
            return run
        with self.nc.Block() as block:
            block.sync(runner("sync"))
            block.tensor(runner("tensor"))
            block.vector(runner("vector"))
            block.scalar(runner("scalar"))
            block.gpsimd(runner("gpsimd"))
        return self.nc



def _prep_host(inputs):
    H = {}
    f32 = np.float32

    u = np.ascontiguousarray(np.asarray(inputs["L0_attn_u"], f32))
    H["_u_full"] = u

    H["inprojT"] = np.ascontiguousarray(np.asarray(inputs["L0_attn_in_proj_w"], f32).T)
    cw = np.asarray(inputs["L0_attn_conv_w"], f32)
    H["convw"] = np.ascontiguousarray(cw.reshape(17, 128, 4).transpose(1, 0, 2).reshape(128, 68))
    cb = np.asarray(inputs["L0_attn_conv_b"], f32)
    H["convb"] = np.ascontiguousarray(cb.reshape(17, 128).T)
    H["dtb"] = np.asarray(inputs["L0_attn_dt_bias"], f32).reshape(16, 1)
    H["alog"] = np.asarray(inputs["L0_attn_A_log"], f32).reshape(16, 1)
    H["dp"] = np.asarray(inputs["L0_attn_Dp"], f32).reshape(1, 16)
    H["normw"] = np.asarray(inputs["L0_attn_norm_w"], f32).reshape(1, 2048)
    H["oprojT"] = np.ascontiguousarray(np.asarray(inputs["L0_attn_out_proj_w"], f32).T)

    for nm, key in (("wq", "L0_ffn_Wq"), ("wk", "L0_ffn_Wk"),
                    ("wv", "L0_ffn_Wv"), ("wo", "L0_ffn_Wo")):
        H[nm] = np.ascontiguousarray(np.asarray(inputs[key], f32))
    for nm, key in (("bq", "L0_ffn_bq"), ("bk", "L0_ffn_bk")):
        H[nm] = np.ascontiguousarray(np.asarray(inputs[key], f32).reshape(8, 128).T)
    H["bv"] = np.asarray(inputs["L0_ffn_bv"], f32).reshape(1, 1024)
    H["bo"] = np.asarray(inputs["L0_ffn_bo"], f32).reshape(1, 1024)
    for nm, key in (("g1", "L0_ffn_g1"), ("be1", "L0_ffn_be1"),
                    ("g2ln", "L0_ffn_g2"), ("be2ln", "L0_ffn_be2")):
        H[nm] = np.asarray(inputs[key], f32).reshape(1, 1024)

    H["wg"] = np.ascontiguousarray(np.asarray(inputs["L0_ffn_Wg"], f32))
    H["bgv"] = np.asarray(inputs["L0_ffn_bg"], f32).reshape(1, 8)
    H["w1"] = np.ascontiguousarray(np.asarray(inputs["L0_ffn_W1"], f32))
    H["b1"] = np.ascontiguousarray(
        np.asarray(inputs["L0_ffn_b1"], f32).reshape(8, 32, 128).transpose(0, 2, 1))
    H["w2"] = np.ascontiguousarray(np.asarray(inputs["L0_ffn_W2"], f32))
    H["b2"] = np.ascontiguousarray(np.asarray(inputs["L0_ffn_b2"], f32))

    ipw = np.asarray(inputs["L1_attn_in_proj_weight"], f32)
    ipb = np.asarray(inputs["L1_attn_in_proj_bias"], f32)
    H["wq1T"] = np.ascontiguousarray(ipw[0:1024].T)
    H["wk1T"] = np.ascontiguousarray(ipw[1024:2048].T)
    H["bq1"] = np.ascontiguousarray(ipb[0:1024].reshape(8, 128).T)
    H["bk1"] = np.ascontiguousarray(ipb[1024:2048].reshape(8, 128).T)

    H["wtT"] = np.ascontiguousarray(np.asarray(inputs["L1_ffn_weight"], f32).T)
    H["alpha"] = np.asarray(inputs["L1_ffn_alpha"], f32).reshape(1, 1)
    H["bias1"] = np.asarray(inputs["L1_ffn_bias"], f32).reshape(1, 1024)

    inv_freq = (1.0 / 10000.0 ** (np.arange(0, 64, 2, dtype=f32) / 64.0)).astype(f32)
    t = np.arange(T, dtype=f32)
    freqs = t[:, None] * inv_freq[None, :]
    H["cos32"] = np.ascontiguousarray(np.cos(freqs).T.astype(f32))
    H["sin32"] = np.ascontiguousarray(np.sin(freqs).T.astype(f32))

    import jax
    import jax.numpy as jnp
    try:
        cpu = jax.local_devices(backend="cpu")[0]
    except Exception:
        cpu = None

    def _mk_gum():
        uu = jax.random.uniform(jax.random.key(42), (16, T, T), jnp.float32,
                                1e-06, 1.0 - 1e-06)
        return -jnp.log(-jnp.log(uu))
    if cpu is not None:
        with jax.default_device(cpu):
            gum = np.asarray(jax.jit(_mk_gum, backend="cpu")())
    else:
        gum = np.asarray(_mk_gum())
    idx = np.arange(T)
    invalid = (idx[None, :] >= idx[:, None]).astype(f32) * (-1.0e9)
    g2 = (gum + invalid[None]) * 0.5
    H["_gum2_full"] = np.ascontiguousarray(g2.astype(f32))
    return H


SHARED_IN = {
    "inprojT": ((1024, 4240), F32), "convw": ((128, 68), F32),
    "convb": ((128, 17), F32), "dtb": ((16, 1), F32), "alog": ((16, 1), F32),
    "dp": ((1, 16), F32), "normw": ((1, 2048), F32), "oprojT": ((2048, 1024), F32),
    "wq": ((1024, 1024), F32), "wk": ((1024, 1024), F32),
    "wv": ((1024, 1024), F32), "wo": ((1024, 1024), F32),
    "bq": ((128, 8), F32), "bk": ((128, 8), F32), "bv": ((1, 1024), F32),
    "bo": ((1, 1024), F32),
    "g1": ((1, 1024), F32), "be1": ((1, 1024), F32),
    "g2ln": ((1, 1024), F32), "be2ln": ((1, 1024), F32),
    "wg": ((1024, 8), F32), "bgv": ((1, 8), F32),
    "w1": ((8, 1024, 4096), F32), "b1": ((8, 128, 32), F32),
    "w2": ((8, 4096, 1024), F32), "b2": ((8, 1024), F32),
    "wq1T": ((1024, 1024), F32), "wk1T": ((1024, 1024), F32),
    "bq1": ((128, 8), F32), "bk1": ((128, 8), F32),
    "wtT": ((1024, 1024), F32), "alpha": ((1, 1), F32), "bias1": ((1, 1024), F32),
    "cos32": ((32, T), F32), "sin32": ((32, T), F32),
}
PERCORE_IN = {
    "u": ((NT, D), F32),
    "gum2": ((NT, T), F32),
}
SCRATCH = {
    "xn_tm": ((NT, D), BF16), "xn_fm": ((D, NT), BF16),
    "mamba_tm": ((NT, D), F32),
    "x1_fm": ((D, NT), BF16), "x1_tm": ((NT, D), BF16),
    "y_acc": ((NT, D), F32),
    "ffn_tm": ((NT, D), F32),
    "xnew_tm": ((NT, D), F32),
    "xn2_tm": ((NT, D), BF16), "xn2_fm": ((D, NT), BF16),
    "l1ffn_tm": ((NT, D), F32),
    "dwT": ((B * T, T), BF16),
}
OUT_NAME = "out"
OUT_SHAPE = (NT, D)
_PHASE_PREFIX = {"prep": "prep_", "mamba": "mb_", "attn": "at_", "moe": "mo_",
                 "l1": "l1_"}



def fm_ap(dram_ap, dc=None, tsl=None):
    r = dram_ap.rearrange("(c p) t -> p c t", p=128)
    if dc is not None:
        r = r[:, dc:dc + 1, :]
    if tsl is not None:
        r = r[:, :, tsl[0]:tsl[1]]
    return r


def make_consts(P, D_):
    nc = P.nc
    C = {}
    C["ident_bf"] = P.sbuf("ident_bf", [128, 128], BF16)
    C["ones_row_bf"] = P.sbuf("ones_row_bf", [1, 128], BF16)
    C["ones_row"] = P.sbuf("ones_row", [1, 128], F32)
    C["ones_col"] = P.sbuf("ones_col", [128, 1], F32)
    C["one1"] = P.sbuf("one1", [1, 1], F32)
    C["eps_col"] = P.sbuf("eps_col", [128, 1], F32)
    g = P.E["gpsimd"]
    gp = P.sem("gp")
    g.memset(C["ident_bf"][:], 0.0)
    g.memset(C["ones_row_bf"][:], 1.0)
    g.memset(C["ones_row"][:], 1.0)
    g.memset(C["ones_col"][:], 1.0)
    g.memset(C["eps_col"][:], 1.0e-5)
    g.memset(C["one1"][:], 1.0, _inc=gp.inc())
    g.wait(gp)
    g.affine_select(out=C["ident_bf"][:], in_=C["ident_bf"][:],
                    compare_op=ALU.not_equal, fill=1.0, base=0,
                    pattern=[[-1, 128]], channel_multiplier=1, _inc=gp.inc())
    for e in ("tensor", "vector", "scalar"):
        P.E[e].wait(gp)
    return C


def ph_prep(P, D_, C):
    E = P.E
    sy, v, s, t, g = E["sync"], E["vector"], E["scalar"], E["tensor"], E["gpsimd"]
    dv = P.sem("dv")
    ac = P.sem("ac")
    pe = P.sem("pe")
    u = D_["u"]
    xn_tm = D_["xn_tm"]
    xn_fm = D_["xn_fm"]

    nbuf = 3
    lds = [P.sem(f"prep_ld{i}") for i in range(nbuf)]
    sds = [P.sem(f"prep_sd{i}") for i in range(nbuf)]
    ut = [P.sbuf(f"prep_ut{i}", [128, 1024], F32) for i in range(nbuf)]
    cent = [P.sbuf(f"prep_cent{i}", [128, 1024], F32) for i in range(nbuf)]
    xnt = [P.sbuf(f"prep_xnt{i}", [128, 1024], BF16) for i in range(nbuf)]
    xft = [P.sbuf(f"prep_xft{i}", [128, 1024], BF16) for i in range(nbuf)]
    sq = P.sbuf("prep_sq", [128, 1024], F32)
    stm = [P.sbuf(f"prep_stm{i}", [128, 1], F32) for i in range(nbuf)]
    sts = [P.sbuf(f"prep_sts{i}", [128, 1], F32) for i in range(nbuf)]
    ptr = P.psum("prep_ptr", [128, 256], BF16)

    done_iter = []
    for i in range(32):
        buf = i % nbuf
        if i >= nbuf:
            dvn, pen, acn = done_iter[i - nbuf]
            sy.wait_ge(dv.h, dvn)
            sy.wait_ge(ac.h, acn)
            v.wait_ge(sds[buf].h, sds[buf].n)
            v.wait_ge(pe.h, pen)
        sy.dma_start(out=ut[buf][:], in_=u[i * 128:(i + 1) * 128, :],
                     _inc=lds[buf].inc(16))
        emit_ln_tm(P, ut[buf][:], xnt[buf][:], cent[buf][:], sq, stm[buf], sts[buf],
                   C, ld_sem=lds[buf])
        s.wait(dv)
        s.dma_start(out=xn_tm[i * 128:(i + 1) * 128, :], in_=xnt[buf][:],
                    _inc=sds[buf].inc(16))
        for j in range(4):
            t.wait(dv)
            t.transpose(ptr[:, 0:128], xnt[buf][:, (2 * j) * 128:(2 * j + 1) * 128], C["ident_bf"][:])
            t.transpose(ptr[:, 128:256], xnt[buf][:, (2 * j + 1) * 128:(2 * j + 2) * 128],
                        C["ident_bf"][:], _inc=pe.inc())
            v.wait(pe)
            v.tensor_copy(xft[buf][:, (2 * j) * 128:(2 * j + 2) * 128], ptr[:], _inc=dv.inc())
        s.wait(dv)
        s.dma_start(out=fm_ap(xn_fm, tsl=(i * 128, (i + 1) * 128)),
                    in_=xft[buf][:].rearrange("p (c t) -> p c t", t=128),
                    _inc=sds[buf].inc(16))
        done_iter.append((dv.n, pe.n, ac.n))
    return {}


def emit_ln_tm(P, x_ap, out_ap, cent, sqbuf, stm, sts, C, ld_sem=None,
               gain_bc=None, bias_bc=None, nfeat=1024):
    E = P.E
    v, s = E["vector"], E["scalar"]
    dv, ac = P.sem("dv"), P.sem("ac")
    if ld_sem is not None:
        v.wait(ld_sem)
    v.wait(dv)
    v.reduce_sum(stm[:], x_ap, axis=AX.X, _inc=dv.inc())
    s.wait(dv)
    s.mul(sts[:], stm[:], 1.0 / nfeat, _inc=ac.inc())
    v.wait(ac)
    v.tensor_scalar(cent[:], x_ap, sts[:], None, ALU.subtract, _inc=dv.inc())
    s.wait(dv)
    s.activation(out=sqbuf[:, 0:nfeat], in_=cent[:], func=AF.Square,
                 accum_out=stm[:], _inc=ac.inc())
    s.wait(ac)
    s.activation(out=sts[:], in_=stm[:], func=AF.Sqrt, scale=1.0 / nfeat,
                 bias=C["eps_col"][:], _inc=ac.inc())
    v.wait(ac)
    v.reciprocal(stm[:], sts[:], _inc=dv.inc())
    v.wait(dv)
    if gain_bc is None:
        v.tensor_scalar(out_ap, cent[:], stm[:], None, ALU.mult, _inc=dv.inc())
    else:
        v.tensor_scalar(cent[:], cent[:], stm[:], None, ALU.mult, _inc=dv.inc())
        v.wait(dv)
        v.tensor_tensor(cent[:], cent[:], gain_bc, ALU.mult, _inc=dv.inc())
        v.wait(dv)
        v.tensor_tensor(out_ap, cent[:], bias_bc, ALU.add, _inc=dv.inc())


def ph_mamba(P, D_, C):
    E = P.E
    sy, v, s, t, g = E["sync"], E["vector"], E["scalar"], E["tensor"], E["gpsimd"]
    ld = P.sem("mbld")
    wt = P.sem("mbwt")
    dv = P.sem("dv")
    ac = P.sem("ac")
    pe = P.sem("pe")
    sd = P.sem("mbsd")
    gp = P.sem("gp")
    xfs = [P.sem(f"mbxf{i}") for i in range(2)]
    wqs = [P.sem(f"mbwq{i}") for i in range(2)]

    QEC = (9, 9, 9, 7)
    QE0 = (0, 9, 18, 27)
    wqb = [P.sbuf(f"mb_wq{i}", [128, 8 * 9 * 128], BF16) for i in range(2)]
    opT = P.sbuf("mb_opT", [128, 16 * 1024], BF16)
    cw = P.sbuf("mb_cw", [128, 68], F32)
    cb = P.sbuf("mb_cb", [128, 17], F32)
    dtb = P.sbuf("mb_dtb", [16, 1], F32)
    acol = P.sbuf("mb_acol", [16, 1], F32)
    nw = P.sbuf("mb_nw", [1, 2048], F32)
    dpc = P.sbuf("mb_dpc", [128, 16], F32)
    u0m = P.sbuf("mb_u0", [128, 256], F32)
    u1m = P.sbuf("mb_u1", [128, 256], F32)
    z16 = P.sbuf("mb_z16", [16, 256], F32)
    id16 = P.sbuf("mb_id16", [16, 16], F32)
    hrow = P.sbuf("mb_hrow", [16, 16 * 128], F32)

    g.dma_start(out=opT[:].rearrange("p (c e) -> p c e", e=1024),
                in_=D_["oprojT"].rearrange("(c p) e -> p c e", p=128), _inc=wt.inc(16))
    dprow = P.sbuf("mb_dprow", [1, 16], F32)
    sy.dma_start(out=cw[:], in_=D_["convw"], _inc=ld.inc(16))
    sy.dma_start(out=cb[:], in_=D_["convb"], _inc=ld.inc(16))
    sy.dma_start(out=dtb[:], in_=D_["dtb"], _inc=ld.inc(16))
    sy.dma_start(out=acol[:], in_=D_["alog"], _inc=ld.inc(16))
    sy.dma_start(out=nw[:], in_=D_["normw"], _inc=ld.inc(16))
    sy.dma_start(out=dprow[:], in_=D_["dp"], _inc=ld.inc(16))
    s.wait(ld)
    s.activation(out=acol[:], in_=acol[:], func=AF.Exp, _inc=ac.inc())
    v.wait(ac)
    v.tensor_scalar_mul(acol[:], acol[:], -1.0, _inc=dv.inc())
    sel64 = P.sbuf("mb_sel64", [128, 64], BF16)
    g.memset(u0m[:], 0.0)
    g.memset(u1m[:], 0.0)
    g.memset(z16[:], 0.0)
    g.memset(id16[:], 0.0)
    g.memset(sel64[:], 0.0)
    g.memset(hrow[:], 0.0, _inc=gp.inc())
    g.wait(gp)
    g.affine_select(out=u0m[:], in_=u0m[:], compare_op=ALU.is_gt, fill=1.0,
                    base=0, pattern=[[-1, 256]], channel_multiplier=1)
    g.affine_select(out=u1m[:], in_=u1m[:], compare_op=ALU.is_gt, fill=1.0,
                    base=128, pattern=[[-1, 256]], channel_multiplier=1)
    g.affine_select(out=id16[:], in_=id16[:], compare_op=ALU.not_equal, fill=1.0,
                    base=0, pattern=[[-1, 16]], channel_multiplier=1)
    g.affine_select(out=sel64[:], in_=sel64[:], compare_op=ALU.not_equal,
                    fill=1.0, base=-64, pattern=[[-1, 64]], channel_multiplier=1)
    hr3 = hrow[:].rearrange("p (h k) -> p h k", k=128)
    for h in range(16):
        g.affine_select(out=hr3[:, h, :], in_=hr3[:, h, :],
                        compare_op=ALU.not_equal, fill=1.0,
                        base=-h, pattern=[[0, 128]], channel_multiplier=1,
                        _inc=gp.inc() if h == 15 else None)

    pbank_a = P.psum("mb_pa", [128, 512], F32)
    pbank_b = P.psum("mb_pb", [128, 512], F32)
    psc = P.psum("mb_psc", [128, 512], F32)
    pbq = P.psum("mb_pbq", [128, 512], F32)
    paux = P.psum("mb_paux", [128, 512], F32)
    py = P.psum("mb_py", [128, 256], F32)
    pnb = P.psum("mb_pnb", [128, 512], F32)
    ptx = P.psum("mb_ptx", [128, 256], BF16)
    t.wait(ld)
    t.wait(gp)
    t.matmul(pbq[:, 0:16], C["ones_row"][:], dprow[:], start=True, stop=True, _inc=pe.inc())
    v.wait(pe)
    v.tensor_copy(dpc[:], pbq[:, 0:16], _inc=dv.inc())

    xnfb = [P.sbuf(f"mb_xnf{i}", [128, 8 * 256], BF16) for i in range(2)]
    xbcraw = P.sbuf("mb_raw", [128, 17 * 259], F32)
    raw3 = xbcraw[:].rearrange("p (c w) -> p c w", w=259)
    xbcs = P.sbuf("mb_xbcs", [128, 17 * 256], BF16)
    xbcs3 = xbcs[:].rearrange("p (c w) -> p c w", w=256)
    sz = P.sbuf("mb_sz", [128, 16 * 256], BF16)
    sz3 = sz[:].rearrange("p (c w) -> p c w", w=256)
    yz = P.sbuf("mb_yz", [128, 16 * 256], BF16)
    yz3 = yz[:].rearrange("p (c w) -> p c w", w=256)
    ynb = P.sbuf("mb_ynb", [128, 16 * 256], BF16)
    ynb3 = ynb[:].rearrange("p (c w) -> p c w", w=256)
    convo = P.sbuf("mb_convo", [128, 256], F32)
    dtrow = P.sbuf("mb_dtrow", [16, 256], F32)
    dtc = P.sbuf("mb_dtc", [16, 256], F32)
    acum = P.sbuf("mb_acum", [16, 256], F32)
    dsr = P.sbuf("mb_dsr", [16, 256], F32)
    dtds = P.sbuf("mb_dtds", [16, 256], F32)
    eacum = P.sbuf("mb_eacum", [16, 256], F32)
    blrow = P.sbuf("mb_blrow", [16, 256], F32)
    cdcol = P.sbuf("mb_cdcol", [16, 1], F32)
    tm16a = P.sbuf("mb_tm16a", [16, 256], F32)
    tm16b = P.sbuf("mb_tm16b", [16, 256], F32)
    sU = P.sbuf("mb_sU", [128, 2 * 256], F32)
    sU3 = sU[:].rearrange("p (k w) -> p k w", w=256)
    btm = P.sbuf("mb_btm", [128, 2 * 64], BF16)
    btm3 = btm[:].rearrange("p (k w) -> p k w", w=64)
    xtm3d = []
    xw3d = []
    argbd = []
    expdd = []
    msb3d = []
    cwtd = []
    for i in range(2):
        xtm3d.append(P.sbuf(f"mb_xtm{i}", [128, 2 * 128], BF16)[:]
                     .rearrange("p (k w) -> p k w", w=128))
        xw3d.append(P.sbuf(f"mb_xw{i}", [128, 2 * 128], BF16)[:]
                    .rearrange("p (k w) -> p k w", w=128))
        if i == 0:
            argbd.append(P.sbuf("mb_argb0", [128, 512], F32))
            expdd.append(P.sbuf("mb_expd0", [128, 512], BF16))
        else:
            argbd.append(argbd[0])
            expdd.append(expdd[0])
        msb3d.append(P.sbuf(f"mb_msb{i}", [128, 2 * 256], BF16)[:]
                     .rearrange("p (k w) -> p k w", w=256))
        cwtd.append(P.sbuf(f"mb_cwt{i}", [64, 256], BF16))
    ctile = P.sbuf("mb_ctile", [64, 256], BF16)
    blc = P.sbuf("mb_blc", [128, 32], F32)
    ddc = P.sbuf("mb_ddc", [128, 32], F32)
    cdc = P.sbuf("mb_cdc", [64, 16], F32)
    cdr = P.sbuf("mb_cdr", [1, 16], F32)
    yfull = P.sbuf("mb_yfull", [128, 256], F32)
    Sf = P.sbuf("mb_Sf", [64, 16 * 128], F32)
    Sf3 = Sf[:].rearrange("p (h w) -> p h w", w=128)
    Sb = P.sbuf("mb_Sb", [64, 16 * 128], BF16)
    Sb3 = Sb[:].rearrange("p (h w) -> p h w", w=128)
    sqb = [P.sbuf(f"mb_sqb{i}", [128, 256], F32) for i in range(2)]
    rsd = P.sbuf("mb_rsd", [1, 256], F32)
    mo = P.sbuf("mb_mo", [128, 512], F32)

    wq3 = [w[:].rearrange("p (c e) -> p c e", e=9 * 128) for w in wqb]
    ipTr = D_["inprojT"].rearrange("(c p) e -> p c e", p=128)
    opT3 = opT[:].rearrange("p (c e) -> p c e", e=1024)
    t.wait(wt)
    v.wait(dv)

    ipg = {}
    sq_done = {}
    nb_done = {}
    wq_loaded = {}
    wq_used = {}
    xbcsd = [xbcs, P.sbuf("mb_xbcs2", [128, 17 * 256], BF16)]
    xbcs3d = [xx[:].rearrange("p (c w) -> p c w", w=256) for xx in xbcsd]
    szd = [sz, P.sbuf("mb_sz2", [128, 16 * 256], BF16)]
    sz3d = [xx[:].rearrange("p (c w) -> p c w", w=256) for xx in szd]
    acumd = [acum, P.sbuf("mb_acum2", [16, 256], F32)]
    eacumd = [eacum, P.sbuf("mb_eacum2", [16, 256], F32)]
    dtdsd = [dtds, P.sbuf("mb_dtds2", [16, 256], F32)]
    blrowd = [blrow, P.sbuf("mb_blrow2", [16, 256], F32)]
    cdcold = [cdcol, P.sbuf("mb_cdcol2", [16, 1], F32)]
    back_pe = [0, 0]
    back_dv = [0, 0]
    back_ac = [0, 0]

    def front_gen(gc):
        b = gc // NCHUNK
        cl = gc % NCHUNK
        gtok0 = b * T + cl * 256
        par = gc % 2
        xbcs3 = xbcs3d[par]
        sz3 = sz3d[par]
        acum_, eacum_, dtds_, blrow_, cdcol_ = (acumd[par], eacumd[par],
                                                dtdsd[par], blrowd[par],
                                                cdcold[par])
        if gc >= 2:
            sy.wait_ge(pe.h, ipg[gc - 2])
        sy.dma_start(out=xnfb[par][:].rearrange("p (c w) -> p c w", w=256),
                     in_=fm_ap(D_["xn_fm"], tsl=(gtok0, gtok0 + 256)),
                     _inc=xfs[par].inc(16))
        xnf3 = xnfb[par][:].rearrange("p (c w) -> p c w", w=256)
        v.wait(dv)
        if cl == 0:
            v.memset(raw3[:, :, 0:3], 0.0, _inc=dv.inc())
        else:
            v.tensor_copy(raw3[:, :, 0:3], raw3[:, :, 256:259], _inc=dv.inc())
        xf_gate = (xfs[par], xfs[par].n)

        def issue_wq(q):
            key = (gc, q - 2) if q >= 2 else (gc - 1, q + 2)
            if key in wq_used:
                g.wait_ge(pe.h, wq_used[key])
            ewid = min((QE0[q] + QEC[q]) * 128, 4240) - QE0[q] * 128
            g.dma_start(out=wq3[q % 2][:, :, 0:ewid],
                        in_=ipTr[:, :, QE0[q] * 128:QE0[q] * 128 + ewid],
                        _inc=wqs[q % 2].inc(16))
            wq_loaded[(gc, q)] = (wqs[q % 2], wqs[q % 2].n)

        issue_wq(0)
        issue_wq(1)
        t.wait_ge(xf_gate[0].h, xf_gate[1])
        for ec in range(34):
            pp = pbank_a if ec % 2 == 0 else pbank_b
            width = 128 if ec < 33 else 16
            q = min(ec // 9, 3)
            eo = (ec - QE0[q]) * 128
            if ec == 9:
                issue_wq(2)
            if ec == 18:
                issue_wq(3)
            if ec == QE0[q]:
                wsem, wcnt = wq_loaded[(gc, q)]
                t.wait_ge(wsem.h, wcnt)
            if (gc, ec - 2) in evac_done:
                sem, cnt = evac_done[(gc, ec - 2)]
                t.wait_ge(sem.h, cnt)
            for dc in range(8):
                t.matmul(pp[0:width, 0:256], wq3[q % 2][:, dc, eo:eo + width],
                         xnf3[:, dc, :], start=(dc == 0), stop=(dc == 7),
                         _inc=pe.inc() if dc == 7 else None)
            if ec == QE0[q] + QEC[q] - 1:
                wq_used[(gc, q)] = pe.n
            if ec < 16:
                s.wait(pe)
                s.wait_ge(dv.h, back_dv[par])
                s.activation(out=sz3[:, ec, :], in_=pp[:, 0:256], func=AF.Silu,
                             _inc=ac.inc())
                evac_done[(gc, ec)] = (ac, ac.n)
            elif ec < 33:
                v.wait(pe)
                v.tensor_copy(raw3[:, ec - 16, 3:259], pp[:, 0:256], _inc=dv.inc())
                evac_done[(gc, ec)] = (dv, dv.n)
            else:
                v.wait(pe)
                v.tensor_scalar(dtrow[:], pp[0:16, 0:256], dtb[:], None, ALU.add,
                                _inc=dv.inc())
                evac_done[(gc, ec)] = (dv, dv.n)
            if ec % 4 == 3:
                yield
        ipg[gc] = pe.n
        for ec in range(17):
            v.wait(dv)
            v.wait(ac)
            v.tensor_scalar(convo[:], raw3[:, ec, 0:256], cw[:, 4 * ec:4 * ec + 1],
                            cb[:, ec:ec + 1], ALU.mult, ALU.add, _inc=dv.inc())
            for w in range(1, 4):
                v.wait(dv)
                v.scalar_tensor_tensor(convo[:], raw3[:, ec, w:w + 256],
                                       cw[:, 4 * ec + w:4 * ec + w + 1],
                                       convo[:], ALU.mult, ALU.add, _inc=dv.inc())
            s.wait(dv)
            s.wait_ge(pe.h, back_pe[par])
            s.wait_ge(dv.h, back_dv[par])
            s.activation(out=xbcs3[:, ec, :], in_=convo[:], func=AF.Silu,
                         _inc=ac.inc())
            if ec % 4 == 3:
                yield
        s.wait(dv)
        s.activation(out=tm16a[:], in_=dtrow[:], func=AF.Abs, _inc=ac.inc())
        s.wait(ac)
        s.activation(out=tm16a[:], in_=tm16a[:], func=AF.Exp, scale=-1.0, _inc=ac.inc())
        s.wait(ac)
        s.activation(out=tm16a[:], in_=tm16a[:], func=AF.Ln,
                     bias=C["ones_col"][0:16, :], _inc=ac.inc())
        v.wait(ac)
        v.tensor_relu(tm16b[:], dtrow[:], _inc=dv.inc())
        v.wait(dv)
        v.tensor_tensor(dtc[:], tm16a[:], tm16b[:], ALU.add, _inc=dv.inc())
        yield
        v.wait(dv)
        v.wait_ge(pe.h, back_pe[par])
        v.tensor_scalar(tm16a[:], dtc[:], acol[:], None, ALU.mult, _inc=dv.inc())
        v.wait(dv)
        v.tensor_tensor_scan(acum_[:], tm16a[:], z16[:], 0.0, ALU.add, ALU.add,
                             _inc=dv.inc())
        s.wait(dv)
        s.activation(out=dsr[:], in_=acum_[:], func=AF.Exp, scale=-1.0,
                     bias=acum_[:, 255:256], _inc=ac.inc())
        s.activation(out=eacum_[:], in_=acum_[:], func=AF.Exp, _inc=ac.inc())
        s.activation(out=cdcol_[:], in_=acum_[:, 255:256], func=AF.Exp, _inc=ac.inc())
        s.activation(out=tm16b[:], in_=dtc[:], func=AF.Ln, _inc=ac.inc())
        v.wait(ac)
        v.tensor_tensor(dtds_[:], dtc[:], dsr[:], ALU.mult, _inc=dv.inc())
        v.tensor_tensor(blrow_[:], tm16b[:], acum_[:], ALU.subtract, _inc=dv.inc())
        yield

    def back_gen(gc):
        b = gc // NCHUNK
        cl = gc % NCHUNK
        gtok0 = b * T + cl * 256
        par = gc % 2
        xbcs3 = xbcs3d[par]
        sz3 = sz3d[par]
        acum_, eacum_, dtds_, blrow_, cdcol_ = (acumd[par], eacumd[par],
                                                dtdsd[par], blrowd[par],
                                                cdcold[par])
        if cl == 0:
            g.wait_ge(dv.h, dv.n)
            g.wait_ge(pe.h, pe.n)
            g.memset(Sf[:], 0.0)
            g.memset(Sb[:], 0.0, _inc=gp.inc())
            v.wait(gp)
            t.wait(gp)
        t.wait(ac)
        t.wait(dv)
        t.matmul(paux[0:64, 0:256], sel64[:], xbcs3[:, 16, :],
                 start=True, stop=True, _inc=pe.inc())
        v.wait(pe)
        v.tensor_copy(ctile[:], paux[0:64, 0:256], _inc=dv.inc())
        t.wait(dv)
        t.matmul(psc[:, 0:256], xbcs3[0:64, 16, 0:128], ctile[:],
                 start=True, stop=True)
        t.matmul(psc[:, 256:512], xbcs3[0:64, 16, 128:256], ctile[:],
                 start=True, stop=True, _inc=pe.inc())
        v.wait(pe)
        v.tensor_tensor(sU3[:, 0, :], psc[:, 0:256], u0m[:], ALU.mult, _inc=dv.inc())
        v.tensor_tensor(sU3[:, 1, :], psc[:, 256:512], u1m[:], ALU.mult, _inc=dv.inc())
        t.transpose(ptx[:, 0:64], xbcs3[0:64, 16, 0:128], C["ident_bf"][0:64, 0:64])
        t.transpose(ptx[:, 64:128], xbcs3[0:64, 16, 128:256],
                    C["ident_bf"][0:64, 0:64], _inc=pe.inc())
        v.wait(pe)
        v.tensor_copy(btm3[:, 0, :], ptx[:, 0:64])
        v.tensor_copy(btm3[:, 1, :], ptx[:, 64:128], _inc=dv.inc())
        t.wait(dv)
        t.wait(ac)
        for kt in range(2):
            t.matmul(pbq[:, 256 + kt * 16:256 + (kt + 1) * 16],
                     blrow_[:, kt * 128:(kt + 1) * 128], id16[:],
                     start=True, stop=True)
            t.matmul(pbq[:, 288 + kt * 16:288 + (kt + 1) * 16],
                     dtds_[:, kt * 128:(kt + 1) * 128], id16[:],
                     start=True, stop=True)
        t.matmul(paux[0:1, 400:416], cdcol_[:], id16[:], start=True, stop=True,
                 _inc=pe.inc())
        v.wait(pe)
        v.tensor_copy(blc[:], pbq[:, 256:288])
        v.tensor_copy(ddc[:], pbq[:, 288:320])
        v.tensor_copy(cdr[:], paux[0:1, 400:416], _inc=dv.inc())
        t.wait(dv)
        t.matmul(paux[0:64, 384:400], C["ones_row"][0:1, 0:64], cdr[:],
                 start=True, stop=True, _inc=pe.inc())
        v.wait(pe)
        v.tensor_copy(cdc[:], paux[0:64, 384:400], _inc=dv.inc())
        yield
        base_dv = dv.n
        base_pe = pe.n
        ptxf = [base_dv, base_dv]
        bqf = [base_dv, base_dv]
        eabf = [base_dv, base_dv]
        pyf = base_dv
        stf = [base_dv, base_dv]
        ymm = [base_pe, base_pe]
        for h in range(16):
            p2 = h % 2
            hsel = hr3[:, h, :][0:16, :]
            po = p2 * 128
            bq_bank = pbq if p2 == 0 else pnb
            eab_reg = (paux if p2 == 0 else psc)[0:64, 0:256]
            st_reg = (paux if p2 == 0 else psc)[0:64, 256:384]
            py_bank = py
            xtm_p = xtm3d[p2]
            xw_p = xw3d[p2]
            argb_p = argbd[p2]
            expd_p = expdd[p2]
            msb_p = msb3d[p2]
            cwt_p = cwtd[p2]
            t.wait_ge(dv.h, ptxf[p2])
            t.wait_ge(pe.h, ymm[p2])
            t.transpose(ptx[:, po:po + 128], xbcs3[:, h, 0:128], C["ident_bf"][:],
                        _inc=pe.inc())
            tr0 = pe.n
            v.wait_ge(pe.h, tr0)
            v.tensor_copy(xtm_p[:, 0, :], ptx[:, po:po + 128], _inc=dv.inc())
            ev0 = dv.n
            t.wait_ge(dv.h, ev0)
            t.transpose(ptx[:, po:po + 128], xbcs3[:, h, 128:256], C["ident_bf"][:],
                        _inc=pe.inc())
            tr1 = pe.n
            v.wait_ge(pe.h, tr1)
            v.tensor_copy(xtm_p[:, 1, :], ptx[:, po:po + 128], _inc=dv.inc())
            ptxf[p2] = dv.n
            t.wait_ge(dv.h, max(bqf[p2], eabf[p2]))
            t.matmul(bq_bank[:, 0:256], hsel, acum_[:], start=True, stop=True)
            t.matmul(eab_reg, hsel[:, 0:64], eacum_[:], start=True, stop=True,
                     _inc=pe.inc())
            bcpe = pe.n
            v.wait_ge(pe.h, bcpe)
            v.tensor_tensor(cwt_p[:], ctile[:], eab_reg, ALU.mult, _inc=dv.inc())
            eabf[p2] = dv.n
            v.wait(dv)
            ddc_bc = AP(ddc[:].tensor, h, [ddc[:].ap[0], [16, 2], [0, 128]])
            v.tensor_tensor(xw_p[:, :, :], xtm_p[:, :, :], ddc_bc, ALU.mult,
                            _inc=dv.inc())
            for kt in range(2):
                v.tensor_scalar(argb_p[:, kt * 256:kt * 256 + 256],
                                bq_bank[:, 0:256],
                                blc[:, kt * 16 + h:kt * 16 + h + 1],
                                30.0, ALU.add, ALU.min, _inc=dv.inc())
            bqf[p2] = dv.n
            s.wait_ge(dv.h, dv.n)
            s.activation(out=expd_p[:], in_=argb_p[:], func=AF.Exp, _inc=ac.inc())
            v.wait_ge(ac.h, ac.n)
            v.tensor_tensor(msb_p[:, :, :], sU3[:, :, :],
                            expd_p[:].rearrange("p (k w) -> p k w", w=256),
                            ALU.mult, _inc=dv.inc())
            msbdv = dv.n
            t.wait_ge(dv.h, max(msbdv, pyf))
            t.matmul(py_bank[:], xtm_p[:, 0, :], msb_p[:, 0, :], start=True, stop=False)
            t.matmul(py_bank[:], xtm_p[:, 1, :], msb_p[:, 1, :], start=False, stop=False)
            t.matmul(py_bank[:], Sb3[:, h, :], cwt_p[:], start=False, stop=True,
                     _inc=pe.inc())
            t.matmul(st_reg, btm3[:, 0, :], xw_p[:, 0, :], start=True, stop=False)
            t.matmul(st_reg, btm3[:, 1, :], xw_p[:, 1, :], start=False, stop=True,
                     _inc=pe.inc())
            ymm[p2] = pe.n
            v.wait_ge(pe.h, ymm[p2])
            v.scalar_tensor_tensor(yfull[:], xbcs3[:, h, :], dpc[:, h:h + 1],
                                   py_bank[:], ALU.mult, ALU.add, _inc=dv.inc())
            pyf = dv.n
            v.wait(dv)
            v.tensor_tensor(yz3[:, h, :], yfull[:], sz3[:, h, :], ALU.mult,
                            _inc=dv.inc())
            v.scalar_tensor_tensor(Sf3[:, h, :], Sf3[:, h, :], cdc[:, h:h + 1],
                                   st_reg, ALU.mult, ALU.add, _inc=dv.inc())
            stf[p2] = dv.n
            v.wait(dv)
            v.tensor_copy(Sb3[:, h, :], Sf3[:, h, :], _inc=dv.inc())
            yield
        for ec in range(16):
            sq = sqb[ec % 2]
            if ec >= 2:
                s.wait_ge(pe.h, sq_done[ec - 2])
            s.wait(dv)
            s.activation(out=sq[:], in_=yz3[:, ec, :], func=AF.Square, _inc=ac.inc())
            t.wait(ac)
            t.matmul(pnb[0:1, 256:512], C["ones_col"][:], sq[:],
                     start=(ec == 0), stop=(ec == 15),
                     _inc=pe.inc())
            sq_done[ec] = pe.n
        s.wait(pe)
        s.activation(out=rsd[:], in_=pnb[0:1, 256:512], func=AF.Sqrt,
                     scale=1.0 / 2048.0, bias=C["eps_col"][0:1, :], _inc=ac.inc())
        v.wait(ac)
        v.reciprocal(rsd[:], rsd[:], _inc=dv.inc())
        yield
        for ec in range(16):
            slot = ec % 2
            dst = pnb[:, 0:256] if slot == 0 else pnb[:, 256:512]
            t.wait(dv)
            t.matmul(dst, nw[:, ec * 128:(ec + 1) * 128], rsd[:],
                     start=True, stop=True, _inc=pe.inc())
            v.wait(pe)
            v.tensor_tensor(ynb3[:, ec, :], yz3[:, ec, :], dst, ALU.mult,
                            _inc=dv.inc())
            if ec == 7:
                yield
        yield
        for tc in range(2):
            pp = psc if tc == 0 else pnb
            for dh in range(2):
                t.wait(dv)
                for ec in range(16):
                    t.matmul(pp[:, 0:512], ynb3[:, ec, tc * 128:tc * 128 + 128],
                             opT3[:, ec, dh * 512:(dh + 1) * 512],
                             start=(ec == 0), stop=(ec == 15),
                             _inc=pe.inc() if ec == 15 else None)
                v.wait(pe)
                v.wait(sd)
                v.tensor_copy(mo[:], pp[:, 0:512], _inc=dv.inc())
                s.wait(dv)
                s.dma_start(out=D_["mamba_tm"][gtok0 + tc * 128:gtok0 + tc * 128 + 128,
                                               dh * 512:(dh + 1) * 512],
                            in_=mo[:], _inc=sd.inc(16))
            yield
        back_pe[par] = pe.n
        back_dv[par] = dv.n
        back_ac[par] = ac.n

    evac_done = {}
    NG = B * NCHUNK
    for _ in front_gen(0):
        pass
    fg = None
    for gc in range(NG):
        fg = front_gen(gc + 1) if gc + 1 < NG else None
        for _ in back_gen(gc):
            if fg is not None:
                next(fg, None)
                next(fg, None)
        if fg is not None:
            for _ in fg:
                pass
    return {}


def ph_attn(P, D_, C):
    E = P.E
    sy, v, s, t, g = E["sync"], E["vector"], E["scalar"], E["tensor"], E["gpsimd"]
    dv, ac, pe, gp = P.sem("dv"), P.sem("ac"), P.sem("pe"), P.sem("gp")
    ld = P.sem("atld")
    gl = P.sem("atgl")
    sd = P.sem("atsd")
    xfs = P.sem("atxf")
    wbs = [P.sem(f"atwb{i}") for i in range(2)]
    xts = [P.sem(f"atxt{i}") for i in range(2)]
    aos = [P.sem(f"atao{i}") for i in range(2)]

    wbuf = [P.sbuf("at_wb0", [128, 8 * 1024], BF16)]
    wbuf.append(wbuf[0])
    wb3 = [wbuf[0][:].rearrange("p (c e) -> p c e", e=1024)] * 2
    bq = P.sbuf("at_bq", [128, 8], F32)
    bk = P.sbuf("at_bk", [128, 8], F32)
    cosr = P.sbuf("at_cosr", [128, 2048], BF16)
    sinr = P.sbuf("at_sinr", [128, 2048], BF16)
    bvb = P.sbuf("at_bvb", [128, 1024], BF16)
    bob = P.sbuf("at_bob", [128, 1024], BF16)
    g1b = P.sbuf("at_g1b", [128, 1024], BF16)
    be1b = P.sbuf("at_be1b", [128, 1024], BF16)
    hselc = P.sbuf("at_hselc", [128, 8 * 16], BF16)
    hs3 = hselc[:].rearrange("p (c m) -> p c m", m=16)
    id16 = P.sbuf("at_id16", [16, 16], F32)
    negcb = P.sbuf("at_negcb", [128, 16], F32)
    mq = P.sbuf("at_mq", [16, 2], F32)
    crow = P.sbuf("at_crow", [1, 16], F32)

    sy.dma_start(out=bq[:], in_=D_["bq"], _inc=ld.inc(16))
    sy.dma_start(out=bk[:], in_=D_["bk"], _inc=ld.inc(16))
    g.dma_start(out=bvb[0:1, :], in_=D_["bv"], _inc=gl.inc(16))
    g.dma_start(out=bob[0:1, :], in_=D_["bo"], _inc=gl.inc(16))
    g.dma_start(out=g1b[0:1, :], in_=D_["g1"], _inc=gl.inc(16))
    g.dma_start(out=be1b[0:1, :], in_=D_["be1"], _inc=gl.inc(16))
    for j in range(4):
        g.dma_start(out=cosr[32 * j:32 * (j + 1), :], in_=D_["cos32"], _inc=gl.inc(16))
        g.dma_start(out=sinr[32 * j:32 * (j + 1), :], in_=D_["sin32"], _inc=gl.inc(16))
    g.memset(hselc[:], 0.0)
    g.memset(id16[:], 0.0, _inc=gp.inc())
    g.wait(gp)
    for oc in range(8):
        g.memset(hs3[0:64, oc, 2 * oc:2 * oc + 1], 1.0)
        g.memset(hs3[64:128, oc, 2 * oc + 1:2 * oc + 2], 1.0)
    g.affine_select(out=id16[:], in_=id16[:], compare_op=ALU.not_equal, fill=1.0,
                    base=0, pattern=[[-1, 16]], channel_multiplier=1, _inc=gp.inc())

    pA = [P.psum(f"at_pA{i}", [128, 512], F32) for i in range(2)]
    pA2 = [P.psum(f"at_pA2{i}", [128, 512], F32) for i in range(2)]
    pav = P.psum("at_pav", [128, 1024], F32)
    pav3 = pav[:].rearrange("p (q w) -> p q w", w=512)
    pN = P.psum("at_pN", [128, 512], F32)
    pT = P.psum("at_pT", [128, 256], BF16)

    t.wait(ld)
    t.wait(gl)
    t.wait(gp)
    for dst in (bvb, bob, g1b, be1b):
        for half in range(2):
            t.matmul(pN[:, 0:512], C["ones_row_bf"][:],
                     dst[0:1, half * 512:(half + 1) * 512],
                     start=True, stop=True, _inc=pe.inc())
            v.wait(pe)
            v.tensor_copy(dst[:, half * 512:(half + 1) * 512], pN[:, 0:512],
                          _inc=dv.inc())
            t.wait(dv)

    xnf = P.sbuf("at_xnf", [128, 8 * 2048], BF16)
    xnf3 = xnf[:].rearrange("p (c t) -> p c t", t=2048)
    ao3 = xnf3
    q3b = P.sbuf("at_q3", [128, 8 * 2048], BF16)
    q3 = q3b[:].rearrange("p (c t) -> p c t", t=2048)
    k3b = P.sbuf("at_k3", [128, 8 * 2048], BF16)
    k3 = k3b[:].rearrange("p (c t) -> p c t", t=2048)
    vtm = P.sbuf("at_vtm", [128, 16 * 1040], BF16)
    vtm3 = vtm[:].rearrange("p (c w) -> p c w", w=1040)
    ra = P.sbuf("at_ra", [128, 2048], BF16)
    rb = P.sbuf("at_rb", [128, 2048], BF16)
    rs = P.sbuf("at_rs", [128, 2048], BF16)
    pt_ = [P.sbuf(f"at_pt{i}", [128, 512], BF16) for i in range(2)]
    pt2_ = [P.sbuf(f"at_pt2{i}", [128, 512], BF16) for i in range(2)]
    drow = P.sbuf("at_drow", [1, 512], F32)
    rbsb = P.sbuf("at_rbsb", [64, 512], F32)
    stg = [P.sbuf(f"at_stg{i}", [64, 512], BF16) for i in range(2)]
    xnt = [P.sbuf(f"at_xnt{i}", [128, 1024], BF16) for i in range(2)]
    sf32 = P.sbuf("at_sf32", [128, 1024], F32)
    cent = P.sbuf("at_cent", [128, 1024], F32)
    sqd = pav[:, 0:1024]
    stm = P.sbuf("at_stm", [128, 1], F32)
    sts = P.sbuf("at_sts", [128, 1], F32)
    x1t = P.sbuf("at_x1t", [128, 1024], BF16)
    xft = P.sbuf("at_xft", [128, 1024], BF16)

    wb_used = [0, 0]
    xnt_used = [0, 0]
    exp_done = [0, 0]
    av_done = [0, 0]
    for b in range(B):
        bt0 = b * T
        sy.wait_ge(pe.h, pe.n)
        sy.dma_start(out=xnf3[:], in_=fm_ap(D_["xn_fm"], tsl=(bt0, bt0 + T)),
                     _inc=xfs.inc(16))
        g.wait_ge(pe.h, pe.n)
        ones_ap = AP(vtm[:].tensor, 64,
                     [vtm[:].ap[0], [1040, 16], [65, 16], [1, 1]])
        g.memset(ones_ap, 1.0, _inc=gp.inc())
        t.wait(gp)

        def load_w(wname, slot):
            g.wait_ge(pe.h, pe.n)
            g.dma_start(out=wb3[0][:, :, :],
                        in_=D_[wname].rearrange("(c p) e -> p c e", p=128),
                        _inc=wbs[0].inc(16))
        qk_dst = (q3, k3)
        for iw in range(2):
            slot = 0
            load_w(("wq", "wk")[iw], slot)
            t.wait_ge(wbs[slot].h, wbs[slot].n)
            t.wait_ge(xfs.h, xfs.n)
            dst = qk_dst[iw]
            bcol = (bq, bk)[iw]
            for oc in range(8):
                for qt in range(4):
                    pp = pA[(oc * 4 + qt) % 2]
                    t.wait(ac)
                    for dc in range(8):
                        t.matmul(pp[:, 0:512], wb3[slot][:, dc, oc * 128:(oc + 1) * 128],
                                 xnf3[:, dc, qt * 512:(qt + 1) * 512],
                                 start=(dc == 0), stop=(dc == 7),
                                 _inc=pe.inc() if dc == 7 else None)
                    s.wait(pe)
                    s.activation(out=dst[:, oc, qt * 512:(qt + 1) * 512], in_=pp[:, 0:512],
                                 func=AF.Identity, bias=bcol[:, oc:oc + 1], _inc=ac.inc())
            wb_used[slot] = pe.n
        load_w("wv", 0)
        t.wait_ge(wbs[0].h, wbs[0].n)
        t.wait_ge(xfs.h, xfs.n)
        for tc in range(16):
            for ot in range(2):
                pp = pA[(tc * 2 + ot) % 2]
                t.wait(dv)
                t.wait(ac)
                for dc in range(8):
                    t.matmul(pp[:, 0:512], xnf3[:, dc, tc * 128:(tc + 1) * 128],
                             wb3[0][:, dc, ot * 512:(ot + 1) * 512],
                             start=(dc == 0), stop=(dc == 7),
                             _inc=pe.inc() if dc == 7 else None)
                v.wait(pe)
                dst = AP(vtm[:].tensor, tc * 1040 + ot * 8 * 65,
                         [vtm[:].ap[0], [65, 8], [1, 64]])
                v.tensor_tensor(dst, pp[:, 0:512].rearrange("p (h w) -> p h w", w=64),
                                bvb[:, ot * 512:(ot + 1) * 512]
                                .rearrange("p (h w) -> p h w", w=64),
                                ALU.add, _inc=dv.inc())
        wb_used[0] = pe.n
        for dst3 in (q3, k3):
            for oc in range(8):
                v.wait(ac)
                v.wait(dv)
                v.tensor_tensor(ra[:], dst3[:, oc, :], cosr[:], ALU.mult)
                v.tensor_tensor(rb[:], dst3[:, oc, :], sinr[:], ALU.mult, _inc=dv.inc())
                v.wait(dv)
                v.tensor_scalar_mul(rs[0:32, :], rb[32:64, :], -1.0)
                v.tensor_copy(rs[32:64, :], rb[0:32, :])
                v.tensor_scalar_mul(rs[64:96, :], rb[96:128, :], -1.0)
                v.tensor_copy(rs[96:128, :], rb[64:96, :], _inc=dv.inc())
                v.wait(dv)
                v.tensor_tensor(dst3[:, oc, :], ra[:], rs[:], ALU.add, _inc=dv.inc())
        for iqk, src3 in enumerate((q3, k3)):
            for oc in range(8):
                s.wait(dv)
                s.wait(ac)
                if oc > 0:
                    s.wait_ge(pe.h, pe.n)
                s.activation(out=ra[:], in_=src3[:, oc, :], func=AF.Square, _inc=ac.inc())
                t.wait(ac)
                for qt in range(4):
                    ndst = pav3[0:16, qt, :] if qt < 2 else pA2[qt - 2][0:16, 0:512]
                    t.matmul(ndst, hs3[:, oc, :], ra[:, qt * 512:(qt + 1) * 512],
                             start=(oc == 0), stop=(oc == 7),
                             _inc=pe.inc() if qt == 3 else None)
            for qt in range(4):
                nsrc = pav3[0:16, qt, :] if qt < 2 else pA2[qt - 2][0:16, 0:512]
                v.wait(pe)
                if qt == 0:
                    v.reduce_max(mq[:, iqk:iqk + 1], nsrc, axis=AX.X,
                                 _inc=dv.inc())
                else:
                    v.wait(dv)
                    v.reduce_max(sts[0:16, :], nsrc,
                                 axis=AX.X, _inc=dv.inc())
                    v.wait(dv)
                    v.tensor_tensor(mq[:, iqk:iqk + 1], mq[:, iqk:iqk + 1],
                                    sts[0:16, :], ALU.max, _inc=dv.inc())
        s.wait(dv)
        s.activation(out=mq[:, 0:1], in_=mq[:, 0:1], func=AF.Sqrt, _inc=ac.inc())
        s.wait(ac)
        s.activation(out=mq[:, 1:2], in_=mq[:, 1:2], func=AF.Sqrt, _inc=ac.inc())
        v.wait(ac)
        v.tensor_tensor(mq[:, 0:1], mq[:, 0:1], mq[:, 1:2], ALU.mult, _inc=dv.inc())
        t.wait(dv)
        t.matmul(pN[0:1, 0:16], mq[:, 0:1], id16[:], start=True, stop=True, _inc=pe.inc())
        v.wait(pe)
        v.tensor_scalar_mul(crow[:], pN[0:1, 0:16], -0.125, _inc=dv.inc())
        t.wait(dv)
        t.matmul(pN[:, 16:32], C["ones_row"][:], crow[:], start=True, stop=True,
                 _inc=pe.inc())
        v.wait(pe)
        v.tensor_copy(negcb[:], pN[:, 16:32], _inc=dv.inc())
        for hp in range(8):
            oc = hp
            h0, h1 = 2 * hp, 2 * hp + 1
            for qt in range(4):
                t.wait(dv)
                s_pe = {}

                def emit_S(kc_):
                    j_ = kc_ % 2
                    t.wait_ge(ac.h, exp_done[j_])
                    t.matmul(pA[j_][:, 0:512],
                             k3[0:64, oc, kc_ * 128:(kc_ + 1) * 128],
                             q3[0:64, oc, qt * 512:(qt + 1) * 512],
                             start=True, stop=True, tile_position=(0, 0))
                    t.matmul(pA2[j_][:, 0:512],
                             k3[64:128, oc, kc_ * 128:(kc_ + 1) * 128],
                             q3[64:128, oc, qt * 512:(qt + 1) * 512],
                             start=True, stop=True, tile_position=(64, 0),
                             _inc=pe.inc())
                    s_pe[kc_] = pe.n

                emit_S(0)
                for kc in range(16):
                    j = kc % 2
                    if kc + 1 < 16:
                        emit_S(kc + 1)
                    s.wait_ge(pe.h, max(s_pe[kc], av_done[j]))
                    s.activation(out=pt_[j][:], in_=pA[j][:, 0:512], func=AF.Exp,
                                 scale=0.125, bias=negcb[:, h0:h0 + 1])
                    s.activation(out=pt2_[j][:], in_=pA2[j][:, 0:512], func=AF.Exp,
                                 scale=0.125, bias=negcb[:, h1:h1 + 1], _inc=ac.inc())
                    exp_done[j] = ac.n
                    t.wait_ge(ac.h, exp_done[j])
                    t.matmul(pav3[0:65, 0, :], vtm3[:, kc, h0 * 65:h0 * 65 + 65],
                             pt_[j][:], start=(kc == 0), stop=(kc == 15))
                    t.matmul(pav3[0:65, 1, :], vtm3[:, kc, h1 * 65:h1 * 65 + 65],
                             pt2_[j][:], start=(kc == 0), stop=(kc == 15),
                             _inc=pe.inc())
                    av_done[j] = pe.n
                for hi in range(2):
                    v.wait(pe)
                    v.wait(dv)
                    v.tensor_copy(drow[:], pav3[64:65, hi, :], _inc=dv.inc())
                    v.wait(dv)
                    v.reciprocal(drow[:], drow[:], _inc=dv.inc())
                    t.wait(dv)
                    t.matmul(pN[0:64, 0:512], C["ones_row"][0:1, 0:64], drow[:],
                             start=True, stop=True, _inc=pe.inc())
                    v.wait(pe)
                    v.tensor_copy(rbsb[:], pN[0:64, 0:512], _inc=dv.inc())
                    v.wait(dv)
                    if hi == 0:
                        v.tensor_tensor(ao3[0:64, oc, qt * 512:(qt + 1) * 512],
                                        pav3[0:64, 0, :], rbsb[:], ALU.mult,
                                        _inc=dv.inc())
                    else:
                        sg = stg[qt % 2]
                        v.wait_ge(aos[qt % 2].h, aos[qt % 2].n)
                        v.tensor_tensor(sg[:], pav3[0:64, 1, :], rbsb[:], ALU.mult,
                                        _inc=dv.inc())
                        sy.wait_ge(dv.h, dv.n)
                        sy.dma_start(out=ao3[64:128, oc, qt * 512:(qt + 1) * 512],
                                     in_=sg[:], _inc=aos[qt % 2].inc(16))
        if DEV and b == 0:
            for nm, ap_ in (("dbg_q3", q3b[:]), ("dbg_k3", k3b[:]),
                            ("dbg_vtm", vtm[:]), ("dbg_ao", xnf[:]),
                            ("dbg_negcb", negcb[:])):
                d = P.nc.dram_tensor(nm, list(ap_.shape), ap_.tensor.dtype,
                                     kind="ExternalOutput").ap()
                s.wait(dv)
                s.wait(ac)
                for a_ in aos:
                    s.wait_ge(a_.h, a_.n)
                s.dma_start(out=d, in_=ap_, _inc=sd.inc(16))
        load_w("wo", 1)
        t.wait_ge(wbs[0].h, wbs[0].n)
        for a_ in aos:
            t.wait_ge(a_.h, a_.n)
        t.wait(dv)
        for tc in range(16):
            xslot = tc % 2
            sy.wait_ge(dv.h, xnt_used[xslot])
            sy.dma_start(out=xnt[xslot][:],
                         in_=D_["xn_tm"][bt0 + tc * 128:bt0 + (tc + 1) * 128, :],
                         _inc=xts[xslot].inc(16))
            for dh in range(2):
                pp = pA[dh]
                t.wait(dv)
                for oc in range(8):
                    t.matmul(pp[:, 0:512], ao3[:, oc, tc * 128:(tc + 1) * 128],
                             wb3[1][:, oc, dh * 512:(dh + 1) * 512],
                             start=(oc == 0), stop=(oc == 7),
                             _inc=pe.inc() if oc == 7 else None)
                v.wait(pe)
                v.wait_ge(xts[xslot].h, xts[xslot].n)
                v.tensor_tensor(sf32[:, dh * 512:(dh + 1) * 512], pp[:, 0:512],
                                bob[:, dh * 512:(dh + 1) * 512], ALU.add, _inc=dv.inc())
                v.wait(dv)
                v.tensor_tensor(sf32[:, dh * 512:(dh + 1) * 512],
                                sf32[:, dh * 512:(dh + 1) * 512],
                                xnt[xslot][:, dh * 512:(dh + 1) * 512], ALU.add,
                                _inc=dv.inc())
            xnt_used[xslot] = dv.n
            emit_ln_tm(P, sf32[:], x1t[:], cent, sqd, stm, sts, C,
                       gain_bc=g1b[:], bias_bc=be1b[:])
            s.wait(dv)
            s.dma_start(out=D_["x1_tm"][bt0 + tc * 128:bt0 + (tc + 1) * 128, :],
                        in_=x1t[:], _inc=sd.inc(16))
            for j in range(4):
                t.wait(dv)
                t.transpose(pT[:, 0:128], x1t[:, (2 * j) * 128:(2 * j + 1) * 128],
                            C["ident_bf"][:])
                t.transpose(pT[:, 128:256], x1t[:, (2 * j + 1) * 128:(2 * j + 2) * 128],
                            C["ident_bf"][:], _inc=pe.inc())
                v.wait(pe)
                v.tensor_copy(xft[:, (2 * j) * 128:(2 * j + 2) * 128], pT[:],
                              _inc=dv.inc())
            s.wait(dv)
            s.dma_start(out=fm_ap(D_["x1_fm"], tsl=(bt0 + tc * 128, bt0 + (tc + 1) * 128)),
                        in_=xft[:].rearrange("p (c t) -> p c t", t=128), _inc=sd.inc(16))
            v.wait_ge(sd.h, sd.n)
    return {}


def ph_moe(P, D_, C):
    E = P.E
    sy, v, s, t, g = E["sync"], E["vector"], E["scalar"], E["tensor"], E["gpsimd"]
    dv, ac, pe, gp = P.sem("dv"), P.sem("ac"), P.sem("pe"), P.sem("gp")
    ld = P.sem("mold")
    gl = P.sem("mogl")
    sd = P.sem("mosd")
    x1s = [P.sem(f"mox1{i}") for i in range(2)]
    w1s = P.sem("mow1")
    w2s = [P.sem(f"mow2{i}") for i in range(2)]
    yts = [P.sem(f"moyt{i}") for i in range(2)]
    b1s = P.sem("mob1")
    fls = [P.sem(f"mofl{i}") for i in range(2)]

    wg_sb = P.sbuf("mo_wg", [128, 8 * 8], BF16)
    wg3 = wg_sb[:].rearrange("p (c e) -> p c e", e=8)
    bgb = P.sbuf("mo_bgb", [128, 8], F32)
    g2b = P.sbuf("mo_g2b", [128, 1024], BF16)
    be2b = P.sbuf("mo_be2b", [128, 1024], BF16)
    b2row = P.sbuf("mo_b2row", [1, 1024], BF16)
    comb = P.sbuf("mo_comb", [128, 32 * 8], F32)
    comb3 = comb[:].rearrange("p (c e) -> p c e", e=8)
    gbuf = P.sbuf("mo_gbuf", [128, 8], F32)
    e8 = P.sbuf("mo_e8", [128, 8], F32)
    c1 = P.sbuf("mo_c1", [128, 1], F32)
    c2 = P.sbuf("mo_c2", [128, 1], F32)
    c3 = P.sbuf("mo_c3", [128, 1], F32)
    x1b = [P.sbuf(f"mo_x1{i}", [128, 8 * 512], BF16) for i in range(2)]
    x13 = [x[:].rearrange("p (c t) -> p c t", t=512) for x in x1b]
    b1c = P.sbuf("mo_b1c", [128, 32], F32)
    yt = [P.sbuf(f"mo_yt{i}", [128, 512], F32) for i in range(2)]
    pH = [P.psum(f"mo_pH{i}", [128, 512], F32) for i in range(2)]
    pY = [P.psum(f"mo_pY{i}", [128, 512], F32) for i in range(2)]
    pG = P.psum("mo_pG", [128, 512], F32)
    w1b = P.sbuf("mo_w1b", [128, 8 * 4096], BF16)
    w13 = w1b[:].rearrange("p (c f) -> p c f", f=4096)
    w2b = [P.sbuf(f"mo_w2b{i}", [128, 32 * 512], BF16) for i in range(2)]
    w23 = [w[:].rearrange("p (c d) -> p c d", d=512) for w in w2b]
    h3b = P.sbuf("mo_h3", [128, 32 * 512], BF16)
    h3 = h3b[:].rearrange("p (c t) -> p c t", t=512)
    x1_used = [0, 0]
    h_done = {}
    fl_used = [0, 0]

    g.dma_start(out=wg_sb[:], in_=D_["wg"].rearrange("(c p) e -> p c e", p=128),
                _inc=gl.inc(16))
    g.dma_start(out=g2b[0:1, :], in_=D_["g2ln"], _inc=gl.inc(16))
    g.dma_start(out=be2b[0:1, :], in_=D_["be2ln"], _inc=gl.inc(16))
    g.dma_start(out=bgb[0:1, :], in_=D_["bgv"], _inc=gl.inc(16))
    t.wait(gl)
    t.matmul(pG[:, 0:8], C["ones_row"][:], bgb[0:1, :], start=True, stop=True,
             _inc=pe.inc())
    v.wait(pe)
    v.tensor_copy(bgb[:], pG[:, 0:8], _inc=dv.inc())
    for dst in (g2b, be2b):
        for half in range(2):
            t.wait(dv)
            t.matmul(pG[:, 0:512], C["ones_row_bf"][:],
                     dst[0:1, half * 512:(half + 1) * 512], start=True, stop=True,
                     _inc=pe.inc())
            v.wait(pe)
            v.tensor_copy(dst[:, half * 512:(half + 1) * 512], pG[:, 0:512],
                          _inc=dv.inc())

    for tt in range(8):
        slot = tt % 2
        sy.wait_ge(pe.h, x1_used[slot])
        sy.dma_start(out=x13[slot][:, :, :],
                     in_=fm_ap(D_["x1_fm"], tsl=(tt * 512, (tt + 1) * 512)),
                     _inc=x1s[slot].inc(16))
        t.wait_ge(x1s[slot].h, x1s[slot].n)
        for tc4 in range(4):
            tchunk = tt * 4 + tc4
            t.wait(dv)
            for dc in range(8):
                t.matmul(pG[:, 0:8], x13[slot][:, dc, tc4 * 128:(tc4 + 1) * 128],
                         wg3[:, dc, :], start=(dc == 0), stop=(dc == 7),
                         _inc=pe.inc() if dc == 7 else None)
            v.wait(pe)
            v.tensor_tensor(gbuf[:], pG[0:128, 0:8], bgb[:], ALU.add, _inc=dv.inc())
            v.wait(dv)
            v.reduce_max(c1[:], gbuf[:], axis=AX.X, negate=True, _inc=dv.inc())
            s.wait(dv)
            s.activation(out=e8[:], in_=gbuf[:], func=AF.Exp, bias=c1[:],
                         accum_out=c2[:], _inc=ac.inc())
            v.wait(ac)
            v.reciprocal(c2[:], c2[:], _inc=dv.inc())
            v.wait(dv)
            v.tensor_scalar(gbuf[:], e8[:], c2[:], None, ALU.mult, _inc=dv.inc())
            v.wait(dv)
            v.reduce_max(c1[:], gbuf[:], axis=AX.X, _inc=dv.inc())
            v.wait(dv)
            v.tensor_scalar(e8[:], gbuf[:], c1[:], None, ALU.is_ge, _inc=dv.inc())
            v.wait(dv)
            v.scalar_tensor_tensor(e8[:], e8[:], -1.0e9, gbuf[:], ALU.mult, ALU.add,
                                   _inc=dv.inc())
            v.wait(dv)
            v.reduce_max(c2[:], e8[:], axis=AX.X, _inc=dv.inc())
            v.wait(dv)
            v.tensor_scalar(e8[:], gbuf[:], c2[:], None, ALU.is_ge, _inc=dv.inc())
            v.tensor_tensor(c3[:], c1[:], c2[:], ALU.add, _inc=dv.inc())
            v.wait(dv)
            v.reciprocal(c3[:], c3[:], _inc=dv.inc())
            v.wait(dv)
            v.tensor_scalar(gbuf[:], gbuf[:], c3[:], None, ALU.mult, _inc=dv.inc())
            v.wait(dv)
            v.tensor_tensor(comb3[:, tchunk, :], gbuf[:], e8[:], ALU.mult,
                            _inc=dv.inc())
        x1_used[slot] = pe.n

    yacc = D_["y_acc"]
    w1r = D_["w1"].rearrange("e (c p) f -> e p c f", p=128)
    w2r = D_["w2"].rearrange("e (c p) d -> e p c d", p=128)
    w1_used = 0
    w2_used = [0, 0]
    b1_used = 0
    b2_used = 0
    ypd = {0: 0, 1: 0}
    b2s = P.sem("mob2")
    for e_ in range(8):
        g.wait_ge(pe.h, w1_used)
        g.dma_start(out=w13[:, :, :], in_=w1r[e_], _inc=w1s.inc(16))
        for half in range(2):
            g.wait_ge(pe.h, w2_used[half])
            g.dma_start(out=w23[half][:, :, :], in_=w2r[e_][:, :, half * 512:(half + 1) * 512],
                        _inc=w2s[half].inc(16))
        sy.wait_ge(ac.h, b1_used)
        sy.dma_start(out=b1c[:], in_=D_["b1"][e_], _inc=b1s.inc(16))
        g.wait_ge(pe.h, b2_used)
        g.dma_start(out=b2row[:], in_=D_["b2"][e_:e_ + 1, :], _inc=b2s.inc(16))
        t.wait_ge(w1s.h, w1s.n)
        t.wait_ge(b2s.h, b2s.n)
        s.wait_ge(b1s.h, b1s.n)
        for tt in range(8):
            slot = tt % 2
            sy.wait_ge(pe.h, x1_used[slot])
            sy.dma_start(out=x13[slot][:, :, :],
                         in_=fm_ap(D_["x1_fm"], tsl=(tt * 512, (tt + 1) * 512)),
                         _inc=x1s[slot].inc(16))
            t.wait_ge(x1s[slot].h, x1s[slot].n)
            for ffc in range(32):
                pp = pH[ffc % 2]
                if ffc >= 2:
                    t.wait_ge(ac.h, h_done[ffc - 2])
                for dc in range(8):
                    t.matmul(pp[:, 0:512], w13[:, dc, ffc * 128:(ffc + 1) * 128],
                             x13[slot][:, dc, :], start=(dc == 0), stop=(dc == 7),
                             _inc=pe.inc() if dc == 7 else None)
                s.wait(pe)
                s.activation(out=h3[:, ffc, :], in_=pp[:, 0:512], func=AF.Gelu,
                             bias=b1c[:, ffc:ffc + 1], _inc=ac.inc())
                h_done[ffc] = ac.n
            x1_used[slot] = pe.n
            b1_used = ac.n
            t.wait_ge(w2s[0].h, w2s[0].n)
            t.wait_ge(w2s[1].h, w2s[1].n)
            t.wait(ac)
            for tc4 in range(4):
                tchunk = tt * 4 + tc4
                for dt2 in range(2):
                    pp = pY[dt2]
                    slotY = (tc4 * 2 + dt2) % 2
                    t.wait_ge(dv.h, ypd[dt2])
                    for ffc in range(32):
                        t.matmul(pp[:, 0:512], h3[:, ffc, tc4 * 128:(tc4 + 1) * 128],
                                 w23[dt2][:, ffc, :], start=(ffc == 0), stop=False)
                    t.matmul(pp[:, 0:512], C["ones_row_bf"][0:1, 0:128],
                             b2row[0:1, dt2 * 512:(dt2 + 1) * 512],
                             start=False, stop=True, _inc=pe.inc())
                    v.wait(pe)
                    v.wait_ge(yts[slotY].h, yts[slotY].n)
                    v.tensor_scalar(yt[slotY][:], pp[:, 0:512],
                                    comb3[:, tchunk, e_:e_ + 1], None, ALU.mult,
                                    _inc=dv.inc())
                    ypd[dt2] = dv.n
                    g.wait_ge(dv.h, dv.n)
                    if e_ == 0:
                        g.dma_start(out=yacc[tchunk * 128:(tchunk + 1) * 128,
                                            dt2 * 512:(dt2 + 1) * 512],
                                    in_=yt[slotY][:], _inc=yts[slotY].inc(16))
                    else:
                        g.dma_start(out=yacc[tchunk * 128:(tchunk + 1) * 128,
                                            dt2 * 512:(dt2 + 1) * 512],
                                    in_=yt[slotY][:], accum_op=ALU.add,
                                    _inc=yts[slotY].inc(16))
            w2_used = [pe.n, pe.n]
        w1_used = pe.n
        b2_used = pe.n

    P.barrier()
    for nm in ("mo_h3", "mo_w2b1", "mo_w2b0", "mo_w1b"):
        P.free(nm)
    yab = [P.sbuf(f"mo_ya{i}", [128, 1024], F32) for i in range(2)]
    x1tb = [P.sbuf(f"mo_x1t{i}", [128, 1024], BF16) for i in range(2)]
    sf32 = P.sbuf("mo_sf32", [128, 1024], F32)
    cent = P.sbuf("mo_cent", [128, 1024], F32)
    stm = P.sbuf("mo_stm", [128, 1], F32)
    sts = P.sbuf("mo_sts", [128, 1], F32)
    x2t = P.sbuf("mo_x2t", [128, 1024], F32)
    sqbuf = P.sbuf("mo_sqbuf", [128, 1024], F32)
    for a_ in yts:
        sy.wait_ge(a_.h, a_.n)
    for tchunk in range(32):
        slot = tchunk % 2
        sy.wait_ge(dv.h, fl_used[slot])
        sy.dma_start(out=yab[slot][:], in_=yacc[tchunk * 128:(tchunk + 1) * 128, :],
                     _inc=fls[slot].inc(16))
        sy.dma_start(out=x1tb[slot][:],
                     in_=D_["x1_tm"][tchunk * 128:(tchunk + 1) * 128, :],
                     _inc=fls[slot].inc(16))
        v.wait_ge(fls[slot].h, fls[slot].n)
        v.wait(dv)
        v.tensor_tensor(sf32[:], yab[slot][:], x1tb[slot][:], ALU.add, _inc=dv.inc())
        fl_used[slot] = dv.n
        emit_ln_tm(P, sf32[:], x2t[:], cent, sqbuf, stm, sts, C,
                   gain_bc=g2b[:], bias_bc=be2b[:])
        s.wait(dv)
        s.dma_start(out=D_["ffn_tm"][tchunk * 128:(tchunk + 1) * 128, :], in_=x2t[:],
                    _inc=sd.inc(16))
        v.wait_ge(sd.h, sd.n)
    return {}


def ph_l1(P, D_, C):
    E = P.E
    sy, v, s, t, g = E["sync"], E["vector"], E["scalar"], E["tensor"], E["gpsimd"]
    dv, ac, pe, gp = P.sem("dv"), P.sem("ac"), P.sem("pe"), P.sem("gp")
    ld = P.sem("l1ld")
    gl = P.sem("l1gl")
    sd = P.sem("l1sd")
    cbs = [P.sem(f"l1cb{i}") for i in range(2)]
    xfs = P.sem("l1xf")
    x2s = [P.sem(f"l1x2{i}") for i in range(2)]
    wts = P.sem("l1wt")
    gts = [P.sem("l1gt0")] * 2
    fas = [P.sem("l1fa0")] * 2
    xns = P.sem("l1xn")

    stm = P.sbuf("l1_stm", [128, 1], F32)
    sts = P.sbuf("l1_sts", [128, 1], F32)
    ub = [P.sbuf(f"l1_u{i}", [128, 1024], F32) for i in range(2)]
    mb_ = [P.sbuf(f"l1_m{i}", [128, 1024], F32) for i in range(2)]
    fb = [P.sbuf(f"l1_f{i}", [128, 1024], F32) for i in range(2)]
    sf32d = [P.sbuf(f"l1_sf32{i}", [128, 1024], F32) for i in range(2)]
    cent = P.sbuf("l1_cent", [128, 1024], F32)
    sqbuf = P.sbuf("l1_sqbuf", [128, 1024], F32)
    xn2td = [P.sbuf(f"l1_xn2t{i}", [128, 1024], BF16) for i in range(2)]
    xftd = [P.sbuf(f"l1_xft{i}", [128, 1024], BF16) for i in range(2)]
    wtf = P.sbuf("l1_wtf", [128, 8 * 1024], F32)
    wtf3 = wtf[:].rearrange("p (c d) -> p c d", d=1024)
    alrow = P.sbuf("l1_alrow", [1, 1], F32)
    b1b = P.sbuf("l1_b1b", [128, 1024], F32)
    pT = P.psum("l1_pT", [128, 256], BF16)
    pN = P.psum("l1_pN", [128, 512], F32)
    sds = [P.sem(f"l1sa{i}") for i in range(2)]
    sy.dma_start(out=wtf[:].rearrange("p (c d) -> p c d", d=1024),
                 in_=D_["wtT"].rearrange("(c p) d -> p c d", p=128), _inc=ld.inc(16))
    sy.dma_start(out=alrow[:], in_=D_["alpha"], _inc=ld.inc(16))
    sy.dma_start(out=b1b[0:1, :], in_=D_["bias1"], _inc=ld.inc(16))
    cb_used = [0, 0]
    xnt_pe = [0, 0]
    for i in range(32):
        slot = i % 2
        sf32 = sf32d[slot]
        xn2t_t = xn2td[slot]
        xft = xftd[slot]
        sy.wait_ge(dv.h, cb_used[slot])
        sy.dma_start(out=ub[slot][:], in_=D_["u"][i * 128:(i + 1) * 128, :],
                     _inc=cbs[slot].inc(16))
        sy.dma_start(out=mb_[slot][:], in_=D_["mamba_tm"][i * 128:(i + 1) * 128, :],
                     _inc=cbs[slot].inc(16))
        sy.dma_start(out=fb[slot][:], in_=D_["ffn_tm"][i * 128:(i + 1) * 128, :],
                     _inc=cbs[slot].inc(16))
        v.wait_ge(cbs[slot].h, cbs[slot].n)
        v.wait_ge(sds[slot].h, sds[slot].n)
        v.wait_ge(pe.h, xnt_pe[slot])
        v.wait(dv)
        v.tensor_tensor(sf32[:], ub[slot][:], mb_[slot][:], ALU.add, _inc=dv.inc())
        v.wait(dv)
        v.tensor_tensor(sf32[:], sf32[:], fb[slot][:], ALU.add, _inc=dv.inc())
        cb_used[slot] = dv.n
        s.wait(dv)
        s.dma_start(out=D_["xnew_tm"][i * 128:(i + 1) * 128, :], in_=sf32[:],
                    _inc=sds[slot].inc(16))
        emit_ln_tm(P, sf32[:], xn2t_t[:], cent, sqbuf, stm, sts, C)
        s.wait(dv)
        s.dma_start(out=D_["xn2_tm"][i * 128:(i + 1) * 128, :], in_=xn2t_t[:],
                    _inc=sds[slot].inc(16))
        for j in range(4):
            t.wait(dv)
            t.transpose(pT[:, 0:128], xn2t_t[:, (2 * j) * 128:(2 * j + 1) * 128],
                        C["ident_bf"][:])
            t.transpose(pT[:, 128:256], xn2t_t[:, (2 * j + 1) * 128:(2 * j + 2) * 128],
                        C["ident_bf"][:], _inc=pe.inc())
            v.wait(pe)
            v.tensor_copy(xft[:, (2 * j) * 128:(2 * j + 2) * 128], pT[:], _inc=dv.inc())
        xnt_pe[slot] = pe.n
        s.wait(dv)
        s.dma_start(out=fm_ap(D_["xn2_fm"], tsl=(i * 128, (i + 1) * 128)),
                    in_=xft[:].rearrange("p (c t) -> p c t", t=128),
                    _inc=sds[slot].inc(16))

    wtb = P.sbuf("l1_wtb", [128, 8 * 1024], BF16)
    wtb3 = wtb[:].rearrange("p (c d) -> p c d", d=1024)
    tern_a = P.sbuf("l1_terna", [128, 1024], F32)
    tern_b = P.sbuf("l1_ternb", [128, 1024], F32)
    accw = P.sbuf("l1_accw", [128, 1], F32)
    dcol = P.sbuf("l1_dcol", [128, 2], F32)
    alcol = P.sbuf("l1_alcol", [128, 1], F32)
    x2tl = [P.sbuf(f"l1_x2tl{i}", [128, 8 * 512], BF16) for i in range(2)]
    x2tl3 = [x[:].rearrange("p (c t) -> p c t", t=512) for x in x2tl]
    l1f = P.sbuf("l1_l1f", [128, 1024], F32)
    pO = [P.psum(f"l1_pO{i}", [128, 512], F32) for i in range(2)]
    sy.wait_ge(sds[0].h, sds[0].n)
    sy.wait_ge(sds[1].h, sds[1].n)
    v.wait(ld)
    for dc in range(8):
        v.wait(dv)
        if dc == 0:
            v.tensor_reduce(accw[:], wtf3[:, 0, :], axis=AX.X, op=ALU.add,
                            apply_absolute_value=True, _inc=dv.inc())
        else:
            v.tensor_reduce(sts[:], wtf3[:, dc, :], axis=AX.X, op=ALU.add,
                            apply_absolute_value=True, _inc=dv.inc())
            v.wait(dv)
            v.tensor_tensor(accw[:], accw[:], sts[:], ALU.add, _inc=dv.inc())
    t.wait(dv)
    t.matmul(pN[0:1, 0:1], C["ones_col"][:], accw[:], start=True, stop=True,
             _inc=pe.inc())
    v.wait(pe)
    v.tensor_scalar(stm[0:1, :], pN[0:1, 0:1], 0.7 / (1024.0 * 1024.0), None,
                    ALU.mult, _inc=dv.inc())
    t.wait(dv)
    t.matmul(pN[:, 1:2], C["ones_row"][:], stm[0:1, :], start=True, stop=True)
    t.matmul(pN[:, 2:3], C["ones_row"][:], alrow[:], start=True, stop=True,
             _inc=pe.inc())
    v.wait(pe)
    v.tensor_copy(dcol[:, 0:1], pN[:, 1:2])
    v.tensor_scalar_mul(dcol[:, 1:2], pN[:, 1:2], -1.0)
    v.tensor_copy(alcol[:], pN[:, 2:3], _inc=dv.inc())
    for half in range(2):
        t.wait(dv)
        t.matmul(pN[:, 0:512], C["ones_row"][:], b1b[0:1, half * 512:(half + 1) * 512],
                 start=True, stop=True, _inc=pe.inc())
        v.wait(pe)
        v.tensor_copy(b1b[:, half * 512:(half + 1) * 512], pN[:, 0:512], _inc=dv.inc())
    for dc in range(8):
        v.wait(dv)
        v.tensor_scalar(tern_a[:], wtf3[:, dc, :], dcol[:, 0:1], None, ALU.is_gt)
        v.tensor_scalar(tern_b[:], wtf3[:, dc, :], dcol[:, 1:2], None, ALU.is_lt,
                        _inc=dv.inc())
        v.wait(dv)
        v.tensor_tensor(wtb3[:, dc, :], tern_a[:], tern_b[:], ALU.subtract,
                        _inc=dv.inc())
    x2_used = [0, 0]
    for tt in range(8):
        slot = tt % 2
        sy.wait_ge(pe.h, x2_used[slot])
        sy.dma_start(out=x2tl3[slot][:, :, :],
                     in_=fm_ap(D_["xn2_fm"], tsl=(tt * 512, (tt + 1) * 512)),
                     _inc=x2s[slot].inc(16))
        t.wait_ge(x2s[slot].h, x2s[slot].n)
        t.wait(dv)
        for tc4 in range(4):
            for ot in range(2):
                pp = pO[ot]
                t.wait(dv)
                for dc in range(8):
                    t.matmul(pp[:, 0:512], x2tl3[slot][:, dc, tc4 * 128:(tc4 + 1) * 128],
                             wtb3[:, dc, ot * 512:(ot + 1) * 512],
                             start=(dc == 0), stop=(dc == 7),
                             _inc=pe.inc() if dc == 7 else None)
                v.wait(pe)
                v.scalar_tensor_tensor(l1f[:, ot * 512:(ot + 1) * 512], pp[:, 0:512],
                                       alcol[:], b1b[:, ot * 512:(ot + 1) * 512],
                                       ALU.mult, ALU.add, _inc=dv.inc())
            s.wait(dv)
            s.dma_start(out=D_["l1ffn_tm"][(tt * 4 + tc4) * 128:(tt * 4 + tc4 + 1) * 128, :],
                        in_=l1f[:], _inc=sd.inc(16))
            v.wait_ge(sd.h, sd.n)
        x2_used[slot] = pe.n

    P.barrier()
    for nm in ("l1_l1f", "l1_x2tl1", "l1_x2tl0", "l1_alcol", "l1_dcol", "l1_accw",
               "l1_ternb", "l1_terna", "l1_wtb", "l1_b1b", "l1_alrow", "l1_wtf",
               "l1_xft1", "l1_xft0", "l1_xn2t1", "l1_xn2t0", "l1_sqbuf", "l1_cent",
               "l1_sf321", "l1_sf320",
               "l1_f1", "l1_f0", "l1_m1", "l1_m0", "l1_u1", "l1_u0"):
        P.free(nm)
    wqb = P.sbuf("l1_wqb", [128, 8 * 1024], BF16)
    wq3 = wqb[:].rearrange("p (c e) -> p c e", e=1024)
    bqc = P.sbuf("l1_bqc", [128, 8], F32)
    bkc = P.sbuf("l1_bkc", [128, 8], F32)
    xn2f = P.sbuf("l1_xn2f", [128, 8 * 2048], BF16)
    xn2f3 = xn2f[:].rearrange("p (c t) -> p c t", t=2048)
    q1b = P.sbuf("l1_q1", [128, 8 * 2048], BF16)
    q13 = q1b[:].rearrange("p (c t) -> p c t", t=2048)
    k1b = P.sbuf("l1_k1", [128, 8 * 2048], BF16)
    k13 = k1b[:].rearrange("p (c t) -> p c t", t=2048)
    hselc = P.sbuf("l1_hselc", [128, 8 * 16], BF16)
    hs3 = hselc[:].rearrange("p (c m) -> p c m", m=16)
    id16 = P.sbuf("l1_id16", [16, 16], F32)
    negcb = P.sbuf("l1_negcb", [128, 16], F32)
    mq = P.sbuf("l1_mq", [16, 2], F32)
    crow = P.sbuf("l1_crow", [1, 16], F32)
    mn = P.sbuf("l1_mn", [128, 2048], BF16)
    zbs = [P.sbuf(f"l1_zb{i}", [128, 2048], F32) for i in range(2)]
    zb = zbs[0]
    pbs = [P.sbuf(f"l1_pb{i}", [128, 2048], BF16) for i in range(2)]
    stmh = P.sbuf("l1_stmh", [128, 2], F32)
    stsh = P.sbuf("l1_stsh", [128, 2], F32)
    sqr = pbs[0]
    avg = P.sbuf("l1_avg", [128, 2048], F32)
    g2t = [P.sbuf("l1_g2t0", [128, 2048], F32)]
    g2t.append(g2t[0])
    dwn = P.sbuf("l1_dwn", [128, 2048], BF16)
    dwtc = P.sbuf("l1_dwtc", [128, 16 * 128], BF16)
    dwt3 = dwtc[:].rearrange("p (c q) -> p c q", q=128)
    xn2ts = P.sbuf("l1_xn2ts", [128, 16 * 1024], BF16)
    xn2ts3 = xn2ts[:].rearrange("p (c e) -> p c e", e=1024)
    xnb2 = [P.sbuf("l1_xnb0", [128, 1024], F32)]
    xnb2.append(xnb2[0])
    l1fb = [P.sbuf("l1_l1fb0", [128, 1024], F32)]
    l1fb.append(l1fb[0])
    pSf = P.psum("l1_pSf", [128, 2048], F32)
    pS = [pSf[:, 0:1024], pSf[:, 1024:2048]]

    g.memset(hselc[:], 0.0)
    g.memset(id16[:], 0.0, _inc=gp.inc())
    g.wait(gp)
    for oc in range(8):
        g.memset(hs3[0:64, oc, 2 * oc:2 * oc + 1], 1.0)
        g.memset(hs3[64:128, oc, 2 * oc + 1:2 * oc + 2], 1.0)
    g.affine_select(out=id16[:], in_=id16[:], compare_op=ALU.not_equal, fill=1.0,
                    base=0, pattern=[[-1, 16]], channel_multiplier=1, _inc=gp.inc())
    sy.dma_start(out=bqc[:], in_=D_["bq1"], _inc=ld.inc(16))
    sy.dma_start(out=bkc[:], in_=D_["bk1"], _inc=ld.inc(16))
    t.wait(gp)
    t.wait(ld)

    wt_used = 0
    gt_used = [0, 0]
    fa_used = [0, 0]
    for b in range(B):
        bt0 = b * T
        sy.wait_ge(pe.h, pe.n)
        sy.dma_start(out=xn2f3[:], in_=fm_ap(D_["xn2_fm"], tsl=(bt0, bt0 + T)),
                     _inc=xfs.inc(16))
        qk_dst = (q13, k13)
        for iw in range(2):
            g.wait_ge(pe.h, wt_used)
            g.dma_start(out=wq3[:, :, :],
                        in_=D_[("wq1T", "wk1T")[iw]].rearrange("(c p) e -> p c e", p=128),
                        _inc=wts.inc(16))
            t.wait_ge(wts.h, wts.n)
            t.wait_ge(xfs.h, xfs.n)
            t.wait(dv)
            dst = qk_dst[iw]
            bcol = (bqc, bkc)[iw]
            for oc in range(8):
                for qt in range(4):
                    pp = pO[(oc * 4 + qt) % 2]
                    t.wait(ac)
                    for dc in range(8):
                        t.matmul(pp[:, 0:512], wq3[:, dc, oc * 128:(oc + 1) * 128],
                                 xn2f3[:, dc, qt * 512:(qt + 1) * 512],
                                 start=(dc == 0), stop=(dc == 7),
                                 _inc=pe.inc() if dc == 7 else None)
                    s.wait(pe)
                    s.activation(out=dst[:, oc, qt * 512:(qt + 1) * 512],
                                 in_=pp[:, 0:512], func=AF.Identity,
                                 bias=bcol[:, oc:oc + 1], _inc=ac.inc())
            wt_used = pe.n
        for iqk, src3 in enumerate((q13, k13)):
            for oc in range(8):
                s.wait(dv)
                s.wait(ac)
                if oc > 0:
                    s.wait_ge(pe.h, pe.n)
                s.activation(out=sqr[:], in_=src3[:, oc, :], func=AF.Square,
                             _inc=ac.inc())
                t.wait(ac)
                for qt in range(4):
                    t.matmul(pS[qt // 2][0:16, (qt % 2) * 512:(qt % 2) * 512 + 512],
                             hs3[:, oc, :], sqr[:, qt * 512:(qt + 1) * 512],
                             start=(oc == 0), stop=(oc == 7),
                             _inc=pe.inc() if qt == 3 else None)
            for qt in range(4):
                v.wait(pe)
                if qt == 0:
                    v.reduce_max(mq[:, iqk:iqk + 1],
                                 pS[0][0:16, 0:512], axis=AX.X, _inc=dv.inc())
                else:
                    v.wait(dv)
                    v.reduce_max(sts[0:16, :],
                                 pS[qt // 2][0:16, (qt % 2) * 512:(qt % 2) * 512 + 512],
                                 axis=AX.X, _inc=dv.inc())
                    v.wait(dv)
                    v.tensor_tensor(mq[:, iqk:iqk + 1], mq[:, iqk:iqk + 1],
                                    sts[0:16, :], ALU.max, _inc=dv.inc())
        s.wait(dv)
        s.activation(out=mq[:, 0:1], in_=mq[:, 0:1], func=AF.Sqrt, _inc=ac.inc())
        s.wait(ac)
        s.activation(out=mq[:, 1:2], in_=mq[:, 1:2], func=AF.Sqrt, _inc=ac.inc())
        v.wait(ac)
        v.tensor_tensor(mq[:, 0:1], mq[:, 0:1], mq[:, 1:2], ALU.mult, _inc=dv.inc())
        t.wait(dv)
        t.matmul(pN[0:1, 0:16], mq[:, 0:1], id16[:], start=True, stop=True,
                 _inc=pe.inc())
        v.wait(pe)
        v.tensor_scalar_mul(crow[:], pN[0:1, 0:16], -0.125, _inc=dv.inc())
        t.wait(dv)
        t.matmul(pN[:, 16:32], C["ones_row"][:], crow[:], start=True, stop=True,
                 _inc=pe.inc())
        v.wait(pe)
        v.tensor_copy(negcb[:], pN[:, 16:32], _inc=dv.inc())
        sy.wait_ge(pe.h, pe.n)
        sy.dma_start(out=xn2ts3[:, :, :],
                     in_=D_["xn2_tm"][bt0:bt0 + T, :].rearrange("(c p) e -> p c e", p=128),
                     _inc=xns.inc(16))
        zt_done = 0
        zb_free = [0, 0]
        pb_done = [0, 0]
        for qc in range(16):
            w = (qc + 1) * 128
            nk5 = (w + 511) // 512
            g.wait_ge(dv.h, dv.n)
            g.memset(mn[:, 0:w], -1.0e9, _inc=gp.inc())
            g.wait(gp)
            g.affine_select(out=mn[:, 0:w], in_=mn[:, 0:w], compare_op=ALU.is_ge,
                            fill=0.0, base=-qc * 128, pattern=[[1, w]],
                            channel_multiplier=-1, _inc=gp.inc())
            mn_done = gp.n
            gslot = qc % 2
            sy.wait_ge(dv.h, gt_used[gslot])
            sy.dma_start(out=g2t[gslot][:, 0:w],
                         in_=D_["gum2"][bt0 + qc * 128:bt0 + (qc + 1) * 128, 0:w],
                         _inc=gts[gslot].inc(16))
            exp_ac = {}

            def emit_rcstt(h2):
                j2 = h2 % 2
                v.wait_ge(ac.h, exp_ac[h2])
                if qc == 0:
                    v.tensor_scalar(stsh[:, j2:j2 + 1], stmh[:, j2:j2 + 1], 16.0,
                                    1.6e-29, ALU.mult, ALU.max, _inc=dv.inc())
                else:
                    v.tensor_scalar_mul(stsh[:, j2:j2 + 1], stmh[:, j2:j2 + 1], 16.0,
                                        _inc=dv.inc())
                v.wait(dv)
                v.reciprocal(stsh[:, j2:j2 + 1], stsh[:, j2:j2 + 1], _inc=dv.inc())
                v.wait(dv)
                if h2 == 0:
                    v.tensor_scalar(avg[:, 0:w], pbs[j2][:, 0:w], stsh[:, j2:j2 + 1],
                                    None, ALU.mult, _inc=dv.inc())
                else:
                    v.scalar_tensor_tensor(avg[:, 0:w], pbs[j2][:, 0:w],
                                           stsh[:, j2:j2 + 1],
                                           avg[:, 0:w], ALU.mult, ALU.add,
                                           _inc=dv.inc())
                pb_done[j2] = dv.n

            for h in range(16):
                r0 = (h % 2) * 64
                oc = h // 2
                j = h % 2
                t.wait_ge(dv.h, zt_done)
                for kt in range(nk5):
                    kw = min(512, w - kt * 512)
                    t.matmul(pSf[:, kt * 512:kt * 512 + kw],
                             q13[r0:r0 + 64, oc, qc * 128:(qc + 1) * 128],
                             k13[r0:r0 + 64, oc, kt * 512:kt * 512 + kw],
                             start=True, stop=True,
                             _inc=pe.inc() if kt == nk5 - 1 else None)
                spe = pe.n
                v.wait_ge(pe.h, spe)
                v.wait_ge(ac.h, zb_free[j])
                v.wait_ge(gp.h, mn_done)
                v.tensor_tensor(zbs[j][:, 0:w], pSf[:, 0:w], mn[:, 0:w], ALU.add,
                                _inc=dv.inc())
                zt_done = dv.n
                s.wait_ge(dv.h, max(zt_done, pb_done[j]))
                s.activation(out=pbs[j][:, 0:w], in_=zbs[j][:, 0:w], func=AF.Exp,
                             scale=0.125,
                             bias=negcb[:, h:h + 1], accum_out=stmh[:, j:j + 1],
                             _inc=ac.inc())
                exp_ac[h] = ac.n
                zb_free[j] = ac.n
                if h >= 1:
                    emit_rcstt(h - 1)
            emit_rcstt(15)
            v.wait_ge(gts[gslot].h, gts[gslot].n)
            v.wait(dv)
            v.wait(gp)
            v.scalar_tensor_tensor(zb[:, 0:w], avg[:, 0:w], 0.5, g2t[gslot][:, 0:w],
                                   ALU.mult, ALU.add, _inc=dv.inc())
            gt_used[0] = gt_used[1] = dv.n
            v.wait(dv)
            v.reduce_max(stm[:], zb[:, 0:w], axis=AX.X, negate=True, _inc=dv.inc())
            s.wait(dv)
            s.activation(out=dwn[:, 0:w], in_=zb[:, 0:w], func=AF.Exp, bias=stm[:],
                         accum_out=sts[:], _inc=ac.inc())
            zb_free[0] = ac.n
            v.wait(ac)
            v.tensor_single_scalar(sts[:], sts[:], 1.0e-30, ALU.max, _inc=dv.inc())
            v.wait(dv)
            v.reciprocal(sts[:], sts[:], _inc=dv.inc())
            v.wait(dv)
            v.tensor_scalar(dwn[:, 0:w], dwn[:, 0:w], sts[:], None, ALU.mult,
                            _inc=dv.inc())
            for kc in range(qc + 1):
                t.wait(dv)
                t.transpose(pT[:, 0:128], dwn[:, kc * 128:(kc + 1) * 128],
                            C["ident_bf"][:], _inc=pe.inc())
                v.wait(pe)
                v.tensor_copy(dwt3[:, kc, :], pT[:, 0:128], _inc=dv.inc())
            fslot = qc % 2
            sy.wait_ge(dv.h, fa_used[fslot])
            sy.wait_ge(sd.h, sd.n)
            sy.dma_start(out=xnb2[fslot][:],
                         in_=D_["xnew_tm"][bt0 + qc * 128:bt0 + (qc + 1) * 128, :],
                         _inc=fas[fslot].inc(16))
            sy.dma_start(out=l1fb[fslot][:],
                         in_=D_["l1ffn_tm"][bt0 + qc * 128:bt0 + (qc + 1) * 128, :],
                         _inc=fas[fslot].inc(16))
            t.wait_ge(xns.h, xns.n)
            for et in range(2):
                pp = pO[et]
                t.wait(dv)
                for kc in range(qc + 1):
                    t.matmul(pp[:, 0:512], dwt3[:, kc, :],
                             xn2ts3[:, kc, et * 512:(et + 1) * 512],
                             start=(kc == 0), stop=(kc == qc),
                             _inc=pe.inc() if kc == qc else None)
                if qc == 0:
                    v.wait(pe)
                    v.memset(pp[0:1, 0:512], 0.0, _inc=dv.inc())
            v.wait(pe)
            v.wait(dv)
            v.wait_ge(fas[fslot].h, fas[fslot].n)
            for et in range(2):
                v.tensor_tensor(xnb2[fslot][:, et * 512:(et + 1) * 512],
                                xnb2[fslot][:, et * 512:(et + 1) * 512],
                                pO[et][:, 0:512], ALU.add, _inc=dv.inc())
            v.wait(dv)
            v.tensor_tensor(xnb2[fslot][:], xnb2[fslot][:], l1fb[fslot][:], ALU.add,
                            _inc=dv.inc())
            fa_used[0] = fa_used[1] = dv.n
            s.wait(dv)
            s.dma_start(out=D_["out"][bt0 + qc * 128:bt0 + (qc + 1) * 128, :],
                        in_=xnb2[fslot][:], _inc=sd.inc(16))
    return {}



PHASE_DEFS = None


def build_launch(phases, keep_outputs=None):
    nc = bass.Bass("TRN2", target_bir_lowering=False)
    needed = set()
    made_all = set()
    for name, fn, needs, makes in phases:
        for n in needs:
            if n not in made_all:
                needed.add(n)
        made_all |= set(makes)
    D_ = {}
    ext_in = []
    ext_out = []
    for n in sorted(needed):
        if n in SHARED_IN:
            sh, dt = SHARED_IN[n]
        elif n in PERCORE_IN:
            sh, dt = PERCORE_IN[n]
        elif n in SCRATCH:
            sh, dt = SCRATCH[n]
        else:
            raise KeyError(n)
        D_[n] = nc.dram_tensor(n, list(sh), dt, kind="ExternalInput").ap()
        ext_in.append(n)
    for n in sorted(made_all):
        if n == OUT_NAME:
            sh, dt = OUT_SHAPE, F32
        else:
            sh, dt = SCRATCH[n]
        if keep_outputs is None or n in keep_outputs:
            D_[n] = nc.dram_tensor(n, list(sh), dt, kind="ExternalOutput").ap()
            ext_out.append(n)
        else:
            D_[n] = nc.dram_tensor(n, list(sh), dt, kind="Internal").ap()

    P = Prog(nc)
    C = make_consts(P, D_)
    for name, fn, needs, makes in phases:
        fn(P, D_, C)
        P.barrier()
        P.free_prefix(_PHASE_PREFIX.get(name, "\x00"))
    sy = P.E["sync"]
    for snm in ("sd", "ld", "wt", "st"):
        if snm in P._sems:
            sy.wait(P.sem(snm))
    P.finish()
    return nc, ext_in, ext_out


def kernel(**inputs):
    H = _prep_host(inputs)
    u_full = H.pop("_u_full")
    gum_full = H.pop("_gum2_full")

    percore = []
    for c in range(8):
        percore.append({
            "u": np.ascontiguousarray(u_full[2 * c:2 * c + 2].reshape(NT, D)),
            "gum2": np.ascontiguousarray(gum_full[2 * c:2 * c + 2].reshape(NT, T)),
        })

    state = [dict() for _ in range(8)]
    for gi, grp in enumerate(LAUNCH_GROUPS):
        phases = [PHASE_DEFS[nm] for nm in grp]
        later_needs = {OUT_NAME}
        for g2 in LAUNCH_GROUPS[gi + 1:]:
            for nm in g2:
                later_needs |= set(PHASE_DEFS[nm][2])
        keep = {n for nm in grp for n in PHASE_DEFS[nm][3] if n in later_needs}
        nc, ext_in, ext_out = build_launch(phases, keep_outputs=keep)
        in_maps = []
        for c in range(8):
            m = {}
            for n in ext_in:
                if n in SHARED_IN:
                    m[n] = H[n]
                elif n in PERCORE_IN:
                    m[n] = percore[c][n]
                else:
                    m[n] = state[c][n]
            in_maps.append(m)
        res = run_bass_kernel_spmd(nc, in_maps, core_ids=list(range(8)))
        for c in range(8):
            for n in ext_out:
                state[c][n] = res.results[c][n]

    out = np.concatenate([np.asarray(state[c][OUT_NAME], np.float32)[None]
                          for c in range(8)], axis=0)
    return np.ascontiguousarray(out.reshape(16, 2048, 1024))


def _register_phases():
    global PHASE_DEFS
    PHASE_DEFS = {
        "prep": ("prep", ph_prep, ["u"], ["xn_tm", "xn_fm"]),
        "mamba": ("mamba", ph_mamba,
                  ["xn_fm", "inprojT", "convw", "convb", "dtb", "alog", "dp",
                   "normw", "oprojT"],
                  ["mamba_tm"]),
        "attn": ("attn", ph_attn,
                 ["xn_fm", "xn_tm", "wq", "wk", "wv", "wo", "bq", "bk", "bv",
                  "bo", "g1", "be1", "cos32", "sin32"],
                 ["x1_fm", "x1_tm"]),
        "moe": ("moe", ph_moe,
                ["x1_fm", "x1_tm", "wg", "bgv", "w1", "b1", "w2", "b2",
                 "g2ln", "be2ln"],
                ["ffn_tm", "y_acc"]),
        "l1": ("l1", ph_l1,
               ["u", "mamba_tm", "ffn_tm", "wq1T", "wk1T", "bq1", "bk1",
                "wtT", "alpha", "bias1", "gum2"],
               ["xnew_tm", "xn2_tm", "xn2_fm", "l1ffn_tm", "out"]),
    }


_register_phases()
LAUNCH_GROUPS = [["prep", "mamba", "attn", "moe", "l1"]]


# version 15 (final)
# speedup vs baseline: 1.0767x; 1.0100x over previous
import math
from contextlib import ExitStack

import numpy as np
import ml_dtypes

import concourse.bass as bass
import concourse.mybir as mybir
from concourse.bass import AP
from concourse.bass_utils import run_bass_kernel_spmd

F32 = mybir.dt.float32
BF16 = mybir.dt.bfloat16
AF = mybir.ActivationFunctionType
ALU = mybir.AluOpType
AX = mybir.AxisListType

ENGINES = ("sync", "tensor", "vector", "scalar", "gpsimd")

B = 2
T = 2048
NT = B * T
D = 1024
NCHUNK = T // 256

DEV = False



class Sem:
    __slots__ = ("h", "n")

    def __init__(self, h):
        self.h = h
        self.n = 0

    def inc(self, k=1):
        self.n += k
        return (self.h, k)


class Rec:

    def __init__(self, prog, name):
        self._p = prog
        self._name = name

    def __getattr__(self, meth):
        p, name = self._p, self._name

        def rec(*args, _inc=None, **kw):
            def run(eng, meth=meth, args=args, kw=kw, _inc=_inc):
                ins = getattr(eng, meth)(*args, **kw)
                if _inc is not None:
                    ins.then_inc(_inc[0], _inc[1])
            p.emitters[name].append(run)
        return rec

    def wait(self, sem):
        self.wait_ge(sem.h, sem.n)


class Prog:
    def __init__(self, nc):
        self.nc = nc
        self.emitters = {e: [] for e in ENGINES}
        self.E = {e: Rec(self, e) for e in ENGINES}
        self._sems = {}
        self._guards = {}
        self._order = []

    def sem(self, name):
        if name not in self._sems:
            g = self.nc.semaphore(name)
            self._sems[name] = Sem(g.__enter__())
            self._guards["sem_" + name] = g
        return self._sems[name]

    def sbuf(self, name, shape, dt):
        g = self.nc.sbuf_tensor(name, shape, dt)
        h = g.__enter__()
        self._guards[name] = g
        self._order = getattr(self, "_order", [])
        self._order.append(name)
        return h

    def psum(self, name, shape, dt=F32):
        g = self.nc.psum_tensor(name, shape, dt)
        h = g.__enter__()
        self._guards[name] = g
        self._order = getattr(self, "_order", [])
        self._order.append(name)
        return h

    def free(self, name):
        g = self._guards.pop(name)
        self._order.remove(name)
        g.__exit__(None, None, None)

    def free_prefix(self, prefix):
        for name in [n for n in reversed(self._order) if n.startswith(prefix)]:
            self.free(name)

    def barrier(self):
        for e in ENGINES:
            for s in self._sems.values():
                self.E[e].wait_ge(s.h, s.n)

    def finish(self):
        def runner(name):
            fns = self.emitters[name]

            def run(eng):
                for f in fns:
                    f(eng)
            return run
        with self.nc.Block() as block:
            block.sync(runner("sync"))
            block.tensor(runner("tensor"))
            block.vector(runner("vector"))
            block.scalar(runner("scalar"))
            block.gpsimd(runner("gpsimd"))
        return self.nc



def _prep_host(inputs):
    H = {}
    f32 = np.float32

    u = np.ascontiguousarray(np.asarray(inputs["L0_attn_u"], f32))
    H["_u_full"] = u

    H["inprojT"] = np.ascontiguousarray(np.asarray(inputs["L0_attn_in_proj_w"], f32).T)
    cw = np.asarray(inputs["L0_attn_conv_w"], f32)
    H["convw"] = np.ascontiguousarray(cw.reshape(17, 128, 4).transpose(1, 0, 2).reshape(128, 68))
    cb = np.asarray(inputs["L0_attn_conv_b"], f32)
    H["convb"] = np.ascontiguousarray(cb.reshape(17, 128).T)
    H["dtb"] = np.asarray(inputs["L0_attn_dt_bias"], f32).reshape(16, 1)
    H["alog"] = np.asarray(inputs["L0_attn_A_log"], f32).reshape(16, 1)
    H["dp"] = np.asarray(inputs["L0_attn_Dp"], f32).reshape(1, 16)
    H["normw"] = np.asarray(inputs["L0_attn_norm_w"], f32).reshape(1, 2048)
    H["oprojT"] = np.ascontiguousarray(np.asarray(inputs["L0_attn_out_proj_w"], f32).T)

    for nm, key in (("wq", "L0_ffn_Wq"), ("wk", "L0_ffn_Wk"),
                    ("wv", "L0_ffn_Wv"), ("wo", "L0_ffn_Wo")):
        H[nm] = np.ascontiguousarray(np.asarray(inputs[key], f32))
    for nm, key in (("bq", "L0_ffn_bq"), ("bk", "L0_ffn_bk")):
        H[nm] = np.ascontiguousarray(np.asarray(inputs[key], f32).reshape(8, 128).T)
    H["bv"] = np.asarray(inputs["L0_ffn_bv"], f32).reshape(1, 1024)
    H["bo"] = np.asarray(inputs["L0_ffn_bo"], f32).reshape(1, 1024)
    for nm, key in (("g1", "L0_ffn_g1"), ("be1", "L0_ffn_be1"),
                    ("g2ln", "L0_ffn_g2"), ("be2ln", "L0_ffn_be2")):
        H[nm] = np.asarray(inputs[key], f32).reshape(1, 1024)

    H["wg"] = np.ascontiguousarray(np.asarray(inputs["L0_ffn_Wg"], f32))
    H["bgv"] = np.asarray(inputs["L0_ffn_bg"], f32).reshape(1, 8)
    H["w1"] = np.ascontiguousarray(np.asarray(inputs["L0_ffn_W1"], f32))
    H["b1"] = np.ascontiguousarray(
        np.asarray(inputs["L0_ffn_b1"], f32).reshape(8, 32, 128).transpose(0, 2, 1))
    H["w2"] = np.ascontiguousarray(np.asarray(inputs["L0_ffn_W2"], f32))
    H["b2"] = np.ascontiguousarray(np.asarray(inputs["L0_ffn_b2"], f32))

    ipw = np.asarray(inputs["L1_attn_in_proj_weight"], f32)
    ipb = np.asarray(inputs["L1_attn_in_proj_bias"], f32)
    H["wq1T"] = np.ascontiguousarray(ipw[0:1024].T)
    H["wk1T"] = np.ascontiguousarray(ipw[1024:2048].T)
    H["bq1"] = np.ascontiguousarray(ipb[0:1024].reshape(8, 128).T)
    H["bk1"] = np.ascontiguousarray(ipb[1024:2048].reshape(8, 128).T)

    H["wtT"] = np.ascontiguousarray(np.asarray(inputs["L1_ffn_weight"], f32).T)
    H["alpha"] = np.asarray(inputs["L1_ffn_alpha"], f32).reshape(1, 1)
    H["bias1"] = np.asarray(inputs["L1_ffn_bias"], f32).reshape(1, 1024)

    inv_freq = (1.0 / 10000.0 ** (np.arange(0, 64, 2, dtype=f32) / 64.0)).astype(f32)
    t = np.arange(T, dtype=f32)
    freqs = t[:, None] * inv_freq[None, :]
    H["cos32"] = np.ascontiguousarray(np.cos(freqs).T.astype(f32))
    H["sin32"] = np.ascontiguousarray(np.sin(freqs).T.astype(f32))

    import jax
    import jax.numpy as jnp
    try:
        cpu = jax.local_devices(backend="cpu")[0]
    except Exception:
        cpu = None

    def _mk_gum():
        uu = jax.random.uniform(jax.random.key(42), (16, T, T), jnp.float32,
                                1e-06, 1.0 - 1e-06)
        return -jnp.log(-jnp.log(uu))
    if cpu is not None:
        with jax.default_device(cpu):
            gum = np.asarray(jax.jit(_mk_gum, backend="cpu")())
    else:
        gum = np.asarray(_mk_gum())
    idx = np.arange(T)
    invalid = (idx[None, :] >= idx[:, None]).astype(f32) * (-1.0e9)
    g2 = (gum + invalid[None]) * 0.5
    H["_gum2_full"] = np.ascontiguousarray(g2.astype(f32))
    return H


SHARED_IN = {
    "inprojT": ((1024, 4240), F32), "convw": ((128, 68), F32),
    "convb": ((128, 17), F32), "dtb": ((16, 1), F32), "alog": ((16, 1), F32),
    "dp": ((1, 16), F32), "normw": ((1, 2048), F32), "oprojT": ((2048, 1024), F32),
    "wq": ((1024, 1024), F32), "wk": ((1024, 1024), F32),
    "wv": ((1024, 1024), F32), "wo": ((1024, 1024), F32),
    "bq": ((128, 8), F32), "bk": ((128, 8), F32), "bv": ((1, 1024), F32),
    "bo": ((1, 1024), F32),
    "g1": ((1, 1024), F32), "be1": ((1, 1024), F32),
    "g2ln": ((1, 1024), F32), "be2ln": ((1, 1024), F32),
    "wg": ((1024, 8), F32), "bgv": ((1, 8), F32),
    "w1": ((8, 1024, 4096), F32), "b1": ((8, 128, 32), F32),
    "w2": ((8, 4096, 1024), F32), "b2": ((8, 1024), F32),
    "wq1T": ((1024, 1024), F32), "wk1T": ((1024, 1024), F32),
    "bq1": ((128, 8), F32), "bk1": ((128, 8), F32),
    "wtT": ((1024, 1024), F32), "alpha": ((1, 1), F32), "bias1": ((1, 1024), F32),
    "cos32": ((32, T), F32), "sin32": ((32, T), F32),
}
PERCORE_IN = {
    "u": ((NT, D), F32),
    "gum2": ((NT, T), F32),
}
SCRATCH = {
    "xn_tm": ((NT, D), BF16), "xn_fm": ((D, NT), BF16),
    "mamba_tm": ((NT, D), F32),
    "x1_fm": ((D, NT), BF16), "x1_tm": ((NT, D), BF16),
    "y_acc": ((NT, D), F32),
    "ffn_tm": ((NT, D), F32),
    "xnew_tm": ((NT, D), F32),
    "xn2_tm": ((NT, D), BF16), "xn2_fm": ((D, NT), BF16),
    "l1ffn_tm": ((NT, D), F32),
    "dwT": ((B * T, T), BF16),
}
OUT_NAME = "out"
OUT_SHAPE = (NT, D)
_PHASE_PREFIX = {"prep": "prep_", "mamba": "mb_", "attn": "at_", "moe": "mo_",
                 "l1": "l1_"}



def fm_ap(dram_ap, dc=None, tsl=None):
    r = dram_ap.rearrange("(c p) t -> p c t", p=128)
    if dc is not None:
        r = r[:, dc:dc + 1, :]
    if tsl is not None:
        r = r[:, :, tsl[0]:tsl[1]]
    return r


def make_consts(P, D_):
    nc = P.nc
    C = {}
    C["ident_bf"] = P.sbuf("ident_bf", [128, 128], BF16)
    C["ones_row_bf"] = P.sbuf("ones_row_bf", [1, 128], BF16)
    C["ones_row"] = P.sbuf("ones_row", [1, 128], F32)
    C["ones_col"] = P.sbuf("ones_col", [128, 1], F32)
    C["one1"] = P.sbuf("one1", [1, 1], F32)
    C["eps_col"] = P.sbuf("eps_col", [128, 1], F32)
    g = P.E["gpsimd"]
    gp = P.sem("gp")
    g.memset(C["ident_bf"][:], 0.0)
    g.memset(C["ones_row_bf"][:], 1.0)
    g.memset(C["ones_row"][:], 1.0)
    g.memset(C["ones_col"][:], 1.0)
    g.memset(C["eps_col"][:], 1.0e-5)
    g.memset(C["one1"][:], 1.0, _inc=gp.inc())
    g.wait(gp)
    g.affine_select(out=C["ident_bf"][:], in_=C["ident_bf"][:],
                    compare_op=ALU.not_equal, fill=1.0, base=0,
                    pattern=[[-1, 128]], channel_multiplier=1, _inc=gp.inc())
    for e in ("tensor", "vector", "scalar"):
        P.E[e].wait(gp)
    return C


def ph_prep(P, D_, C):
    E = P.E
    sy, v, s, t, g = E["sync"], E["vector"], E["scalar"], E["tensor"], E["gpsimd"]
    dv = P.sem("dv")
    ac = P.sem("ac")
    pe = P.sem("pe")
    u = D_["u"]
    xn_tm = D_["xn_tm"]
    xn_fm = D_["xn_fm"]

    nbuf = 3
    lds = [P.sem(f"prep_ld{i}") for i in range(nbuf)]
    sds = [P.sem(f"prep_sd{i}") for i in range(nbuf)]
    ut = [P.sbuf(f"prep_ut{i}", [128, 1024], F32) for i in range(nbuf)]
    cent = [P.sbuf(f"prep_cent{i}", [128, 1024], F32) for i in range(nbuf)]
    xnt = [P.sbuf(f"prep_xnt{i}", [128, 1024], BF16) for i in range(nbuf)]
    xft = [P.sbuf(f"prep_xft{i}", [128, 1024], BF16) for i in range(nbuf)]
    sq = P.sbuf("prep_sq", [128, 1024], F32)
    stm = [P.sbuf(f"prep_stm{i}", [128, 1], F32) for i in range(nbuf)]
    sts = [P.sbuf(f"prep_sts{i}", [128, 1], F32) for i in range(nbuf)]
    ptr = P.psum("prep_ptr", [128, 256], BF16)

    done_iter = []
    for i in range(32):
        buf = i % nbuf
        if i >= nbuf:
            dvn, pen, acn = done_iter[i - nbuf]
            sy.wait_ge(dv.h, dvn)
            sy.wait_ge(ac.h, acn)
            v.wait_ge(sds[buf].h, sds[buf].n)
            v.wait_ge(pe.h, pen)
        sy.dma_start(out=ut[buf][:], in_=u[i * 128:(i + 1) * 128, :],
                     _inc=lds[buf].inc(16))
        emit_ln_tm(P, ut[buf][:], xnt[buf][:], cent[buf][:], sq, stm[buf], sts[buf],
                   C, ld_sem=lds[buf])
        s.wait(dv)
        s.dma_start(out=xn_tm[i * 128:(i + 1) * 128, :], in_=xnt[buf][:],
                    _inc=sds[buf].inc(16))
        for j in range(4):
            t.wait(dv)
            t.transpose(ptr[:, 0:128], xnt[buf][:, (2 * j) * 128:(2 * j + 1) * 128], C["ident_bf"][:])
            t.transpose(ptr[:, 128:256], xnt[buf][:, (2 * j + 1) * 128:(2 * j + 2) * 128],
                        C["ident_bf"][:], _inc=pe.inc())
            v.wait(pe)
            v.tensor_copy(xft[buf][:, (2 * j) * 128:(2 * j + 2) * 128], ptr[:], _inc=dv.inc())
        s.wait(dv)
        s.dma_start(out=fm_ap(xn_fm, tsl=(i * 128, (i + 1) * 128)),
                    in_=xft[buf][:].rearrange("p (c t) -> p c t", t=128),
                    _inc=sds[buf].inc(16))
        done_iter.append((dv.n, pe.n, ac.n))
    return {}


def emit_ln_tm(P, x_ap, out_ap, cent, sqbuf, stm, sts, C, ld_sem=None,
               gain_bc=None, bias_bc=None, nfeat=1024):
    E = P.E
    v, s = E["vector"], E["scalar"]
    dv, ac = P.sem("dv"), P.sem("ac")
    if ld_sem is not None:
        v.wait(ld_sem)
    v.wait(dv)
    v.reduce_sum(stm[:], x_ap, axis=AX.X, _inc=dv.inc())
    s.wait(dv)
    s.mul(sts[:], stm[:], 1.0 / nfeat, _inc=ac.inc())
    v.wait(ac)
    v.tensor_scalar(cent[:], x_ap, sts[:], None, ALU.subtract, _inc=dv.inc())
    s.wait(dv)
    s.activation(out=sqbuf[:, 0:nfeat], in_=cent[:], func=AF.Square,
                 accum_out=stm[:], _inc=ac.inc())
    s.wait(ac)
    s.activation(out=sts[:], in_=stm[:], func=AF.Sqrt, scale=1.0 / nfeat,
                 bias=C["eps_col"][:], _inc=ac.inc())
    v.wait(ac)
    v.reciprocal(stm[:], sts[:], _inc=dv.inc())
    v.wait(dv)
    if gain_bc is None:
        v.tensor_scalar(out_ap, cent[:], stm[:], None, ALU.mult, _inc=dv.inc())
    else:
        v.tensor_scalar(cent[:], cent[:], stm[:], None, ALU.mult, _inc=dv.inc())
        v.wait(dv)
        v.tensor_tensor(cent[:], cent[:], gain_bc, ALU.mult, _inc=dv.inc())
        v.wait(dv)
        v.tensor_tensor(out_ap, cent[:], bias_bc, ALU.add, _inc=dv.inc())


def ph_mamba(P, D_, C):
    E = P.E
    sy, v, s, t, g = E["sync"], E["vector"], E["scalar"], E["tensor"], E["gpsimd"]
    ld = P.sem("mbld")
    wt = P.sem("mbwt")
    dv = P.sem("dv")
    ac = P.sem("ac")
    pe = P.sem("pe")
    sd = P.sem("mbsd")
    gp = P.sem("gp")
    xfs = [P.sem(f"mbxf{i}") for i in range(2)]
    wqs = [P.sem(f"mbwq{i}") for i in range(2)]

    QEC = (9, 9, 9, 7)
    QE0 = (0, 9, 18, 27)
    wqb = [P.sbuf(f"mb_wq{i}", [128, 8 * 9 * 128], BF16) for i in range(2)]
    opT = P.sbuf("mb_opT", [128, 16 * 1024], BF16)
    cw = P.sbuf("mb_cw", [128, 68], F32)
    cb = P.sbuf("mb_cb", [128, 17], F32)
    dtb = P.sbuf("mb_dtb", [16, 1], F32)
    acol = P.sbuf("mb_acol", [16, 1], F32)
    nw = P.sbuf("mb_nw", [1, 2048], F32)
    dpc = P.sbuf("mb_dpc", [128, 16], F32)
    u0m = P.sbuf("mb_u0", [128, 256], F32)
    u1m = P.sbuf("mb_u1", [128, 256], F32)
    z16 = P.sbuf("mb_z16", [16, 256], F32)
    id16 = P.sbuf("mb_id16", [16, 16], F32)
    hrow = P.sbuf("mb_hrow", [16, 16 * 128], F32)

    g.dma_start(out=opT[:].rearrange("p (c e) -> p c e", e=1024),
                in_=D_["oprojT"].rearrange("(c p) e -> p c e", p=128), _inc=wt.inc(16))
    dprow = P.sbuf("mb_dprow", [1, 16], F32)
    sy.dma_start(out=cw[:], in_=D_["convw"], _inc=ld.inc(16))
    sy.dma_start(out=cb[:], in_=D_["convb"], _inc=ld.inc(16))
    sy.dma_start(out=dtb[:], in_=D_["dtb"], _inc=ld.inc(16))
    sy.dma_start(out=acol[:], in_=D_["alog"], _inc=ld.inc(16))
    sy.dma_start(out=nw[:], in_=D_["normw"], _inc=ld.inc(16))
    sy.dma_start(out=dprow[:], in_=D_["dp"], _inc=ld.inc(16))
    s.wait(ld)
    s.activation(out=acol[:], in_=acol[:], func=AF.Exp, _inc=ac.inc())
    v.wait(ac)
    v.tensor_scalar_mul(acol[:], acol[:], -1.0, _inc=dv.inc())
    sel64 = P.sbuf("mb_sel64", [128, 64], BF16)
    g.memset(u0m[:], 0.0)
    g.memset(u1m[:], 0.0)
    g.memset(z16[:], 0.0)
    g.memset(id16[:], 0.0)
    g.memset(sel64[:], 0.0)
    g.memset(hrow[:], 0.0, _inc=gp.inc())
    g.wait(gp)
    g.affine_select(out=u0m[:], in_=u0m[:], compare_op=ALU.is_gt, fill=1.0,
                    base=0, pattern=[[-1, 256]], channel_multiplier=1)
    g.affine_select(out=u1m[:], in_=u1m[:], compare_op=ALU.is_gt, fill=1.0,
                    base=128, pattern=[[-1, 256]], channel_multiplier=1)
    g.affine_select(out=id16[:], in_=id16[:], compare_op=ALU.not_equal, fill=1.0,
                    base=0, pattern=[[-1, 16]], channel_multiplier=1)
    g.affine_select(out=sel64[:], in_=sel64[:], compare_op=ALU.not_equal,
                    fill=1.0, base=-64, pattern=[[-1, 64]], channel_multiplier=1)
    hr3 = hrow[:].rearrange("p (h k) -> p h k", k=128)
    for h in range(16):
        g.affine_select(out=hr3[:, h, :], in_=hr3[:, h, :],
                        compare_op=ALU.not_equal, fill=1.0,
                        base=-h, pattern=[[0, 128]], channel_multiplier=1,
                        _inc=gp.inc() if h == 15 else None)

    pbank_a = P.psum("mb_pa", [128, 512], F32)
    pbank_b = P.psum("mb_pb", [128, 512], F32)
    psc = P.psum("mb_psc", [128, 512], F32)
    pbq = P.psum("mb_pbq", [128, 512], F32)
    paux = P.psum("mb_paux", [128, 512], F32)
    py = P.psum("mb_py", [128, 256], F32)
    pnb = P.psum("mb_pnb", [128, 512], F32)
    ptx = P.psum("mb_ptx", [128, 256], BF16)
    t.wait(ld)
    t.wait(gp)
    t.matmul(pbq[:, 0:16], C["ones_row"][:], dprow[:], start=True, stop=True, _inc=pe.inc())
    v.wait(pe)
    v.tensor_copy(dpc[:], pbq[:, 0:16], _inc=dv.inc())

    xnfb = [P.sbuf(f"mb_xnf{i}", [128, 8 * 256], BF16) for i in range(2)]
    xbcraw = P.sbuf("mb_raw", [128, 17 * 259], F32)
    raw3 = xbcraw[:].rearrange("p (c w) -> p c w", w=259)
    xbcs = P.sbuf("mb_xbcs", [128, 17 * 256], BF16)
    xbcs3 = xbcs[:].rearrange("p (c w) -> p c w", w=256)
    sz = P.sbuf("mb_sz", [128, 16 * 256], BF16)
    sz3 = sz[:].rearrange("p (c w) -> p c w", w=256)
    yz = P.sbuf("mb_yz", [128, 16 * 256], BF16)
    yz3 = yz[:].rearrange("p (c w) -> p c w", w=256)
    ynb = P.sbuf("mb_ynb", [128, 16 * 256], BF16)
    ynb3 = ynb[:].rearrange("p (c w) -> p c w", w=256)
    convo = P.sbuf("mb_convo", [128, 256], F32)
    dtrow = P.sbuf("mb_dtrow", [16, 256], F32)
    dtc = P.sbuf("mb_dtc", [16, 256], F32)
    acum = P.sbuf("mb_acum", [16, 256], F32)
    dsr = P.sbuf("mb_dsr", [16, 256], F32)
    dtds = P.sbuf("mb_dtds", [16, 256], F32)
    eacum = P.sbuf("mb_eacum", [16, 256], F32)
    blrow = P.sbuf("mb_blrow", [16, 256], F32)
    cdcol = P.sbuf("mb_cdcol", [16, 1], F32)
    tm16a = P.sbuf("mb_tm16a", [16, 256], F32)
    tm16b = P.sbuf("mb_tm16b", [16, 256], F32)
    sU = P.sbuf("mb_sU", [128, 2 * 256], F32)
    sU3 = sU[:].rearrange("p (k w) -> p k w", w=256)
    btm = P.sbuf("mb_btm", [128, 2 * 64], BF16)
    btm3 = btm[:].rearrange("p (k w) -> p k w", w=64)
    xtm3d = []
    xw3d = []
    argbd = []
    expdd = []
    msb3d = []
    cwtd = []
    for i in range(2):
        xtm3d.append(P.sbuf(f"mb_xtm{i}", [128, 2 * 128], BF16)[:]
                     .rearrange("p (k w) -> p k w", w=128))
        xw3d.append(P.sbuf(f"mb_xw{i}", [128, 2 * 128], BF16)[:]
                    .rearrange("p (k w) -> p k w", w=128))
        if i == 0:
            argbd.append(P.sbuf("mb_argb0", [128, 512], F32))
            expdd.append(P.sbuf("mb_expd0", [128, 512], BF16))
        else:
            argbd.append(argbd[0])
            expdd.append(expdd[0])
        msb3d.append(P.sbuf(f"mb_msb{i}", [128, 2 * 256], BF16)[:]
                     .rearrange("p (k w) -> p k w", w=256))
        cwtd.append(P.sbuf(f"mb_cwt{i}", [64, 256], BF16))
    ctile = P.sbuf("mb_ctile", [64, 256], BF16)
    blc = P.sbuf("mb_blc", [128, 32], F32)
    ddc = P.sbuf("mb_ddc", [128, 32], F32)
    cdc = P.sbuf("mb_cdc", [64, 16], F32)
    cdr = P.sbuf("mb_cdr", [1, 16], F32)
    yfull = P.sbuf("mb_yfull", [128, 256], F32)
    Sf = P.sbuf("mb_Sf", [64, 16 * 128], F32)
    Sf3 = Sf[:].rearrange("p (h w) -> p h w", w=128)
    Sb = P.sbuf("mb_Sb", [64, 16 * 128], BF16)
    Sb3 = Sb[:].rearrange("p (h w) -> p h w", w=128)
    sqb = [P.sbuf(f"mb_sqb{i}", [128, 256], F32) for i in range(2)]
    rsd = P.sbuf("mb_rsd", [1, 256], F32)
    mo = P.sbuf("mb_mo", [128, 512], F32)

    wq3 = [w[:].rearrange("p (c e) -> p c e", e=9 * 128) for w in wqb]
    ipTr = D_["inprojT"].rearrange("(c p) e -> p c e", p=128)
    opT3 = opT[:].rearrange("p (c e) -> p c e", e=1024)
    t.wait(wt)
    v.wait(dv)

    ipg = {}
    sq_done = {}
    nb_done = {}
    wq_loaded = {}
    wq_used = {}
    xbcsd = [xbcs, P.sbuf("mb_xbcs2", [128, 17 * 256], BF16)]
    xbcs3d = [xx[:].rearrange("p (c w) -> p c w", w=256) for xx in xbcsd]
    szd = [sz, P.sbuf("mb_sz2", [128, 16 * 256], BF16)]
    sz3d = [xx[:].rearrange("p (c w) -> p c w", w=256) for xx in szd]
    acumd = [acum, P.sbuf("mb_acum2", [16, 256], F32)]
    eacumd = [eacum, P.sbuf("mb_eacum2", [16, 256], F32)]
    dtdsd = [dtds, P.sbuf("mb_dtds2", [16, 256], F32)]
    blrowd = [blrow, P.sbuf("mb_blrow2", [16, 256], F32)]
    cdcold = [cdcol, P.sbuf("mb_cdcol2", [16, 1], F32)]
    back_pe = [0, 0]
    back_dv = [0, 0]
    back_ac = [0, 0]

    def front_gen(gc):
        b = gc // NCHUNK
        cl = gc % NCHUNK
        gtok0 = b * T + cl * 256
        par = gc % 2
        xbcs3 = xbcs3d[par]
        sz3 = sz3d[par]
        acum_, eacum_, dtds_, blrow_, cdcol_ = (acumd[par], eacumd[par],
                                                dtdsd[par], blrowd[par],
                                                cdcold[par])
        if gc >= 2:
            sy.wait_ge(pe.h, ipg[gc - 2])
        sy.dma_start(out=xnfb[par][:].rearrange("p (c w) -> p c w", w=256),
                     in_=fm_ap(D_["xn_fm"], tsl=(gtok0, gtok0 + 256)),
                     _inc=xfs[par].inc(16))
        xnf3 = xnfb[par][:].rearrange("p (c w) -> p c w", w=256)
        v.wait(dv)
        if cl == 0:
            v.memset(raw3[:, :, 0:3], 0.0, _inc=dv.inc())
        else:
            v.tensor_copy(raw3[:, :, 0:3], raw3[:, :, 256:259], _inc=dv.inc())
        xf_gate = (xfs[par], xfs[par].n)

        def issue_wq(q):
            key = (gc, q - 2) if q >= 2 else (gc - 1, q + 2)
            if key in wq_used:
                g.wait_ge(pe.h, wq_used[key])
            ewid = min((QE0[q] + QEC[q]) * 128, 4240) - QE0[q] * 128
            g.dma_start(out=wq3[q % 2][:, :, 0:ewid],
                        in_=ipTr[:, :, QE0[q] * 128:QE0[q] * 128 + ewid],
                        _inc=wqs[q % 2].inc(16))
            wq_loaded[(gc, q)] = (wqs[q % 2], wqs[q % 2].n)

        issue_wq(0)
        issue_wq(1)
        t.wait_ge(xf_gate[0].h, xf_gate[1])
        for ec in range(34):
            pp = pbank_a if ec % 2 == 0 else pbank_b
            width = 128 if ec < 33 else 16
            q = min(ec // 9, 3)
            eo = (ec - QE0[q]) * 128
            if ec == 9:
                issue_wq(2)
            if ec == 18:
                issue_wq(3)
            if ec == QE0[q]:
                wsem, wcnt = wq_loaded[(gc, q)]
                t.wait_ge(wsem.h, wcnt)
            if (gc, ec - 2) in evac_done:
                sem, cnt = evac_done[(gc, ec - 2)]
                t.wait_ge(sem.h, cnt)
            for dc in range(8):
                t.matmul(pp[0:width, 0:256], wq3[q % 2][:, dc, eo:eo + width],
                         xnf3[:, dc, :], start=(dc == 0), stop=(dc == 7),
                         _inc=pe.inc() if dc == 7 else None)
            if ec == QE0[q] + QEC[q] - 1:
                wq_used[(gc, q)] = pe.n
            if ec < 16:
                s.wait(pe)
                s.wait_ge(dv.h, back_dv[par])
                s.activation(out=sz3[:, ec, :], in_=pp[:, 0:256], func=AF.Silu,
                             _inc=ac.inc())
                evac_done[(gc, ec)] = (ac, ac.n)
            elif ec < 33:
                v.wait(pe)
                v.tensor_copy(raw3[:, ec - 16, 3:259], pp[:, 0:256], _inc=dv.inc())
                evac_done[(gc, ec)] = (dv, dv.n)
            else:
                v.wait(pe)
                v.tensor_scalar(dtrow[:], pp[0:16, 0:256], dtb[:], None, ALU.add,
                                _inc=dv.inc())
                evac_done[(gc, ec)] = (dv, dv.n)
            if ec % 4 == 3:
                yield
        ipg[gc] = pe.n
        for ec in range(17):
            v.wait(dv)
            v.wait(ac)
            v.tensor_scalar(convo[:], raw3[:, ec, 0:256], cw[:, 4 * ec:4 * ec + 1],
                            cb[:, ec:ec + 1], ALU.mult, ALU.add, _inc=dv.inc())
            for w in range(1, 4):
                v.wait(dv)
                v.scalar_tensor_tensor(convo[:], raw3[:, ec, w:w + 256],
                                       cw[:, 4 * ec + w:4 * ec + w + 1],
                                       convo[:], ALU.mult, ALU.add, _inc=dv.inc())
            s.wait(dv)
            s.wait_ge(pe.h, back_pe[par])
            s.wait_ge(dv.h, back_dv[par])
            s.activation(out=xbcs3[:, ec, :], in_=convo[:], func=AF.Silu,
                         _inc=ac.inc())
            if ec % 4 == 3:
                yield
        s.wait(dv)
        s.activation(out=tm16a[:], in_=dtrow[:], func=AF.Abs, _inc=ac.inc())
        s.wait(ac)
        s.activation(out=tm16a[:], in_=tm16a[:], func=AF.Exp, scale=-1.0, _inc=ac.inc())
        s.wait(ac)
        s.activation(out=tm16a[:], in_=tm16a[:], func=AF.Ln,
                     bias=C["ones_col"][0:16, :], _inc=ac.inc())
        v.wait(ac)
        v.tensor_relu(tm16b[:], dtrow[:], _inc=dv.inc())
        v.wait(dv)
        v.tensor_tensor(dtc[:], tm16a[:], tm16b[:], ALU.add, _inc=dv.inc())
        yield
        v.wait(dv)
        v.wait_ge(pe.h, back_pe[par])
        v.tensor_scalar(tm16a[:], dtc[:], acol[:], None, ALU.mult, _inc=dv.inc())
        v.wait(dv)
        v.tensor_tensor_scan(acum_[:], tm16a[:], z16[:], 0.0, ALU.add, ALU.add,
                             _inc=dv.inc())
        s.wait(dv)
        s.activation(out=dsr[:], in_=acum_[:], func=AF.Exp, scale=-1.0,
                     bias=acum_[:, 255:256], _inc=ac.inc())
        s.activation(out=eacum_[:], in_=acum_[:], func=AF.Exp, _inc=ac.inc())
        s.activation(out=cdcol_[:], in_=acum_[:, 255:256], func=AF.Exp, _inc=ac.inc())
        s.activation(out=tm16b[:], in_=dtc[:], func=AF.Ln, _inc=ac.inc())
        v.wait(ac)
        v.tensor_tensor(dtds_[:], dtc[:], dsr[:], ALU.mult, _inc=dv.inc())
        v.tensor_tensor(blrow_[:], tm16b[:], acum_[:], ALU.subtract, _inc=dv.inc())
        yield

    def back_gen(gc):
        b = gc // NCHUNK
        cl = gc % NCHUNK
        gtok0 = b * T + cl * 256
        par = gc % 2
        xbcs3 = xbcs3d[par]
        sz3 = sz3d[par]
        acum_, eacum_, dtds_, blrow_, cdcol_ = (acumd[par], eacumd[par],
                                                dtdsd[par], blrowd[par],
                                                cdcold[par])
        if cl == 0:
            g.wait_ge(dv.h, dv.n)
            g.wait_ge(pe.h, pe.n)
            g.memset(Sf[:], 0.0)
            g.memset(Sb[:], 0.0, _inc=gp.inc())
            v.wait(gp)
            t.wait(gp)
        t.wait(ac)
        t.wait(dv)
        t.matmul(paux[0:64, 0:256], sel64[:], xbcs3[:, 16, :],
                 start=True, stop=True, _inc=pe.inc())
        v.wait(pe)
        v.tensor_copy(ctile[:], paux[0:64, 0:256], _inc=dv.inc())
        t.wait(dv)
        t.matmul(psc[:, 0:256], xbcs3[0:64, 16, 0:128], ctile[:],
                 start=True, stop=True)
        t.matmul(psc[:, 256:512], xbcs3[0:64, 16, 128:256], ctile[:],
                 start=True, stop=True, _inc=pe.inc())
        v.wait(pe)
        v.tensor_tensor(sU3[:, 0, :], psc[:, 0:256], u0m[:], ALU.mult, _inc=dv.inc())
        v.tensor_tensor(sU3[:, 1, :], psc[:, 256:512], u1m[:], ALU.mult, _inc=dv.inc())
        t.transpose(ptx[:, 0:64], xbcs3[0:64, 16, 0:128], C["ident_bf"][0:64, 0:64])
        t.transpose(ptx[:, 64:128], xbcs3[0:64, 16, 128:256],
                    C["ident_bf"][0:64, 0:64], _inc=pe.inc())
        v.wait(pe)
        v.tensor_copy(btm3[:, 0, :], ptx[:, 0:64])
        v.tensor_copy(btm3[:, 1, :], ptx[:, 64:128], _inc=dv.inc())
        t.wait(dv)
        t.wait(ac)
        for kt in range(2):
            t.matmul(pbq[:, 256 + kt * 16:256 + (kt + 1) * 16],
                     blrow_[:, kt * 128:(kt + 1) * 128], id16[:],
                     start=True, stop=True)
            t.matmul(pbq[:, 288 + kt * 16:288 + (kt + 1) * 16],
                     dtds_[:, kt * 128:(kt + 1) * 128], id16[:],
                     start=True, stop=True)
        t.matmul(paux[0:1, 400:416], cdcol_[:], id16[:], start=True, stop=True,
                 _inc=pe.inc())
        v.wait(pe)
        v.tensor_copy(blc[:], pbq[:, 256:288])
        v.tensor_copy(ddc[:], pbq[:, 288:320])
        v.tensor_copy(cdr[:], paux[0:1, 400:416], _inc=dv.inc())
        t.wait(dv)
        t.matmul(paux[0:64, 384:400], C["ones_row"][0:1, 0:64], cdr[:],
                 start=True, stop=True, _inc=pe.inc())
        v.wait(pe)
        v.tensor_copy(cdc[:], paux[0:64, 384:400], _inc=dv.inc())
        yield
        base_dv = dv.n
        base_pe = pe.n
        ptxf = [base_dv, base_dv]
        bqf = [base_dv, base_dv]
        eabf = [base_dv, base_dv]
        pyf = base_dv
        stf = [base_dv, base_dv]
        ymm = [base_pe, base_pe]
        for h in range(16):
            p2 = h % 2
            hsel = hr3[:, h, :][0:16, :]
            po = p2 * 128
            bq_bank = pbq if p2 == 0 else pnb
            eab_reg = (paux if p2 == 0 else psc)[0:64, 0:256]
            st_reg = (paux if p2 == 0 else psc)[0:64, 256:384]
            py_bank = py
            xtm_p = xtm3d[p2]
            xw_p = xw3d[p2]
            argb_p = argbd[p2]
            expd_p = expdd[p2]
            msb_p = msb3d[p2]
            cwt_p = cwtd[p2]
            t.wait_ge(dv.h, ptxf[p2])
            t.wait_ge(pe.h, ymm[p2])
            t.transpose(ptx[:, po:po + 128], xbcs3[:, h, 0:128], C["ident_bf"][:],
                        _inc=pe.inc())
            tr0 = pe.n
            v.wait_ge(pe.h, tr0)
            v.tensor_copy(xtm_p[:, 0, :], ptx[:, po:po + 128], _inc=dv.inc())
            ev0 = dv.n
            t.wait_ge(dv.h, ev0)
            t.transpose(ptx[:, po:po + 128], xbcs3[:, h, 128:256], C["ident_bf"][:],
                        _inc=pe.inc())
            tr1 = pe.n
            v.wait_ge(pe.h, tr1)
            v.tensor_copy(xtm_p[:, 1, :], ptx[:, po:po + 128], _inc=dv.inc())
            ptxf[p2] = dv.n
            t.wait_ge(dv.h, max(bqf[p2], eabf[p2]))
            t.matmul(bq_bank[:, 0:256], hsel, acum_[:], start=True, stop=True)
            t.matmul(eab_reg, hsel[:, 0:64], eacum_[:], start=True, stop=True,
                     _inc=pe.inc())
            bcpe = pe.n
            v.wait_ge(pe.h, bcpe)
            v.tensor_tensor(cwt_p[:], ctile[:], eab_reg, ALU.mult, _inc=dv.inc())
            eabf[p2] = dv.n
            v.wait(dv)
            ddc_bc = AP(ddc[:].tensor, h, [ddc[:].ap[0], [16, 2], [0, 128]])
            v.tensor_tensor(xw_p[:, :, :], xtm_p[:, :, :], ddc_bc, ALU.mult,
                            _inc=dv.inc())
            for kt in range(2):
                v.tensor_scalar(argb_p[:, kt * 256:kt * 256 + 256],
                                bq_bank[:, 0:256],
                                blc[:, kt * 16 + h:kt * 16 + h + 1],
                                30.0, ALU.add, ALU.min, _inc=dv.inc())
            bqf[p2] = dv.n
            s.wait_ge(dv.h, dv.n)
            s.activation(out=expd_p[:], in_=argb_p[:], func=AF.Exp, _inc=ac.inc())
            v.wait_ge(ac.h, ac.n)
            v.tensor_tensor(msb_p[:, :, :], sU3[:, :, :],
                            expd_p[:].rearrange("p (k w) -> p k w", w=256),
                            ALU.mult, _inc=dv.inc())
            msbdv = dv.n
            t.wait_ge(dv.h, max(msbdv, pyf))
            t.matmul(py_bank[:], xtm_p[:, 0, :], msb_p[:, 0, :], start=True, stop=False)
            t.matmul(py_bank[:], xtm_p[:, 1, :], msb_p[:, 1, :], start=False, stop=False)
            t.matmul(py_bank[:], Sb3[:, h, :], cwt_p[:], start=False, stop=True,
                     _inc=pe.inc())
            t.matmul(st_reg, btm3[:, 0, :], xw_p[:, 0, :], start=True, stop=False)
            t.matmul(st_reg, btm3[:, 1, :], xw_p[:, 1, :], start=False, stop=True,
                     _inc=pe.inc())
            ymm[p2] = pe.n
            v.wait_ge(pe.h, ymm[p2])
            v.scalar_tensor_tensor(yfull[:], xbcs3[:, h, :], dpc[:, h:h + 1],
                                   py_bank[:], ALU.mult, ALU.add, _inc=dv.inc())
            pyf = dv.n
            v.wait(dv)
            v.tensor_tensor(yz3[:, h, :], yfull[:], sz3[:, h, :], ALU.mult,
                            _inc=dv.inc())
            v.scalar_tensor_tensor(Sf3[:, h, :], Sf3[:, h, :], cdc[:, h:h + 1],
                                   st_reg, ALU.mult, ALU.add, _inc=dv.inc())
            stf[p2] = dv.n
            v.wait(dv)
            v.tensor_copy(Sb3[:, h, :], Sf3[:, h, :], _inc=dv.inc())
            yield
        for ec in range(16):
            sq = sqb[ec % 2]
            if ec >= 2:
                s.wait_ge(pe.h, sq_done[ec - 2])
            s.wait(dv)
            s.activation(out=sq[:], in_=yz3[:, ec, :], func=AF.Square, _inc=ac.inc())
            t.wait(ac)
            t.matmul(pnb[0:1, 256:512], C["ones_col"][:], sq[:],
                     start=(ec == 0), stop=(ec == 15),
                     _inc=pe.inc())
            sq_done[ec] = pe.n
        s.wait(pe)
        s.activation(out=rsd[:], in_=pnb[0:1, 256:512], func=AF.Sqrt,
                     scale=1.0 / 2048.0, bias=C["eps_col"][0:1, :], _inc=ac.inc())
        v.wait(ac)
        v.reciprocal(rsd[:], rsd[:], _inc=dv.inc())
        yield
        for ec in range(16):
            slot = ec % 2
            dst = pnb[:, 0:256] if slot == 0 else pnb[:, 256:512]
            t.wait(dv)
            t.matmul(dst, nw[:, ec * 128:(ec + 1) * 128], rsd[:],
                     start=True, stop=True, _inc=pe.inc())
            v.wait(pe)
            v.tensor_tensor(ynb3[:, ec, :], yz3[:, ec, :], dst, ALU.mult,
                            _inc=dv.inc())
            if ec == 7:
                yield
        yield
        for tc in range(2):
            pp = psc if tc == 0 else pnb
            for dh in range(2):
                t.wait(dv)
                for ec in range(16):
                    t.matmul(pp[:, 0:512], ynb3[:, ec, tc * 128:tc * 128 + 128],
                             opT3[:, ec, dh * 512:(dh + 1) * 512],
                             start=(ec == 0), stop=(ec == 15),
                             _inc=pe.inc() if ec == 15 else None)
                v.wait(pe)
                v.wait(sd)
                v.tensor_copy(mo[:], pp[:, 0:512], _inc=dv.inc())
                s.wait(dv)
                s.dma_start(out=D_["mamba_tm"][gtok0 + tc * 128:gtok0 + tc * 128 + 128,
                                               dh * 512:(dh + 1) * 512],
                            in_=mo[:], _inc=sd.inc(16))
            yield
        back_pe[par] = pe.n
        back_dv[par] = dv.n
        back_ac[par] = ac.n

    evac_done = {}
    NG = B * NCHUNK
    for _ in front_gen(0):
        pass
    fg = None
    for gc in range(NG):
        fg = front_gen(gc + 1) if gc + 1 < NG else None
        for _ in back_gen(gc):
            if fg is not None:
                next(fg, None)
                next(fg, None)
        if fg is not None:
            for _ in fg:
                pass
    return {}


def ph_attn(P, D_, C):
    E = P.E
    sy, v, s, t, g = E["sync"], E["vector"], E["scalar"], E["tensor"], E["gpsimd"]
    dv, ac, pe, gp = P.sem("dv"), P.sem("ac"), P.sem("pe"), P.sem("gp")
    ld = P.sem("atld")
    gl = P.sem("atgl")
    sd = P.sem("atsd")
    xfs = P.sem("atxf")
    wbs = [P.sem(f"atwb{i}") for i in range(2)]
    xts = [P.sem(f"atxt{i}") for i in range(2)]
    aos = [P.sem(f"atao{i}") for i in range(2)]

    wbuf = [P.sbuf("at_wb0", [128, 8 * 1024], BF16)]
    wbuf.append(wbuf[0])
    wb3 = [wbuf[0][:].rearrange("p (c e) -> p c e", e=1024)] * 2
    bq = P.sbuf("at_bq", [128, 8], F32)
    bk = P.sbuf("at_bk", [128, 8], F32)
    cosr = P.sbuf("at_cosr", [128, 2048], BF16)
    sinr = P.sbuf("at_sinr", [128, 2048], BF16)
    bvb = P.sbuf("at_bvb", [128, 1024], BF16)
    bob = P.sbuf("at_bob", [128, 1024], BF16)
    g1b = P.sbuf("at_g1b", [128, 1024], BF16)
    be1b = P.sbuf("at_be1b", [128, 1024], BF16)
    hselc = P.sbuf("at_hselc", [128, 8 * 16], BF16)
    hs3 = hselc[:].rearrange("p (c m) -> p c m", m=16)
    id16 = P.sbuf("at_id16", [16, 16], F32)
    negcb = P.sbuf("at_negcb", [128, 16], F32)
    mq = P.sbuf("at_mq", [16, 2], F32)
    crow = P.sbuf("at_crow", [1, 16], F32)

    sy.dma_start(out=bq[:], in_=D_["bq"], _inc=ld.inc(16))
    sy.dma_start(out=bk[:], in_=D_["bk"], _inc=ld.inc(16))
    g.dma_start(out=bvb[0:1, :], in_=D_["bv"], _inc=gl.inc(16))
    g.dma_start(out=bob[0:1, :], in_=D_["bo"], _inc=gl.inc(16))
    g.dma_start(out=g1b[0:1, :], in_=D_["g1"], _inc=gl.inc(16))
    g.dma_start(out=be1b[0:1, :], in_=D_["be1"], _inc=gl.inc(16))
    for j in range(4):
        g.dma_start(out=cosr[32 * j:32 * (j + 1), :], in_=D_["cos32"], _inc=gl.inc(16))
        g.dma_start(out=sinr[32 * j:32 * (j + 1), :], in_=D_["sin32"], _inc=gl.inc(16))
    g.memset(hselc[:], 0.0)
    g.memset(id16[:], 0.0, _inc=gp.inc())
    g.wait(gp)
    for oc in range(8):
        g.memset(hs3[0:64, oc, 2 * oc:2 * oc + 1], 1.0)
        g.memset(hs3[64:128, oc, 2 * oc + 1:2 * oc + 2], 1.0)
    g.affine_select(out=id16[:], in_=id16[:], compare_op=ALU.not_equal, fill=1.0,
                    base=0, pattern=[[-1, 16]], channel_multiplier=1, _inc=gp.inc())

    pA = [P.psum(f"at_pA{i}", [128, 512], F32) for i in range(2)]
    pA2 = [P.psum(f"at_pA2{i}", [128, 512], F32) for i in range(2)]
    pav = P.psum("at_pav", [128, 1024], F32)
    pav3 = pav[:].rearrange("p (q w) -> p q w", w=512)
    pN = P.psum("at_pN", [128, 512], F32)
    pT = P.psum("at_pT", [128, 256], BF16)

    t.wait(ld)
    t.wait(gl)
    t.wait(gp)
    for dst in (bvb, bob, g1b, be1b):
        for half in range(2):
            t.matmul(pN[:, 0:512], C["ones_row_bf"][:],
                     dst[0:1, half * 512:(half + 1) * 512],
                     start=True, stop=True, _inc=pe.inc())
            v.wait(pe)
            v.tensor_copy(dst[:, half * 512:(half + 1) * 512], pN[:, 0:512],
                          _inc=dv.inc())
            t.wait(dv)

    xnf = P.sbuf("at_xnf", [128, 8 * 2048], BF16)
    xnf3 = xnf[:].rearrange("p (c t) -> p c t", t=2048)
    ao3 = xnf3
    q3b = P.sbuf("at_q3", [128, 8 * 2048], BF16)
    q3 = q3b[:].rearrange("p (c t) -> p c t", t=2048)
    k3b = P.sbuf("at_k3", [128, 8 * 2048], BF16)
    k3 = k3b[:].rearrange("p (c t) -> p c t", t=2048)
    vtm = P.sbuf("at_vtm", [128, 16 * 1040], BF16)
    vtm3 = vtm[:].rearrange("p (c w) -> p c w", w=1040)
    ra = P.sbuf("at_ra", [128, 2048], BF16)
    rb = P.sbuf("at_rb", [128, 2048], BF16)
    rs = P.sbuf("at_rs", [128, 2048], BF16)
    pt_ = [P.sbuf(f"at_pt{i}", [128, 512], BF16) for i in range(2)]
    pt2_ = [P.sbuf(f"at_pt2{i}", [128, 512], BF16) for i in range(2)]
    drow = P.sbuf("at_drow", [1, 512], F32)
    rbsb = P.sbuf("at_rbsb", [64, 512], F32)
    stg = [P.sbuf(f"at_stg{i}", [64, 512], BF16) for i in range(2)]
    xnt = [P.sbuf(f"at_xnt{i}", [128, 1024], BF16) for i in range(2)]
    sf32 = P.sbuf("at_sf32", [128, 1024], F32)
    cent = P.sbuf("at_cent", [128, 1024], F32)
    sqd = pav[:, 0:1024]
    stm = P.sbuf("at_stm", [128, 1], F32)
    sts = P.sbuf("at_sts", [128, 1], F32)
    x1t = P.sbuf("at_x1t", [128, 1024], BF16)
    xft = P.sbuf("at_xft", [128, 1024], BF16)

    wb_used = [0, 0]
    xnt_used = [0, 0]
    exp_done = [0, 0]
    av_done = [0, 0]
    for b in range(B):
        bt0 = b * T
        sy.wait_ge(pe.h, pe.n)
        sy.dma_start(out=xnf3[:], in_=fm_ap(D_["xn_fm"], tsl=(bt0, bt0 + T)),
                     _inc=xfs.inc(16))
        g.wait_ge(pe.h, pe.n)
        ones_ap = AP(vtm[:].tensor, 64,
                     [vtm[:].ap[0], [1040, 16], [65, 16], [1, 1]])
        g.memset(ones_ap, 1.0, _inc=gp.inc())
        t.wait(gp)

        def load_w(wname, slot):
            g.wait_ge(pe.h, pe.n)
            g.dma_start(out=wb3[0][:, :, :],
                        in_=D_[wname].rearrange("(c p) e -> p c e", p=128),
                        _inc=wbs[0].inc(16))
        qk_dst = (q3, k3)
        for iw in range(2):
            slot = 0
            load_w(("wq", "wk")[iw], slot)
            t.wait_ge(wbs[slot].h, wbs[slot].n)
            t.wait_ge(xfs.h, xfs.n)
            dst = qk_dst[iw]
            bcol = (bq, bk)[iw]
            for oc in range(8):
                for qt in range(4):
                    pp = pA[(oc * 4 + qt) % 2]
                    t.wait(ac)
                    for dc in range(8):
                        t.matmul(pp[:, 0:512], wb3[slot][:, dc, oc * 128:(oc + 1) * 128],
                                 xnf3[:, dc, qt * 512:(qt + 1) * 512],
                                 start=(dc == 0), stop=(dc == 7),
                                 _inc=pe.inc() if dc == 7 else None)
                    s.wait(pe)
                    s.activation(out=dst[:, oc, qt * 512:(qt + 1) * 512], in_=pp[:, 0:512],
                                 func=AF.Identity, bias=bcol[:, oc:oc + 1], _inc=ac.inc())
            wb_used[slot] = pe.n
        load_w("wv", 0)
        t.wait_ge(wbs[0].h, wbs[0].n)
        t.wait_ge(xfs.h, xfs.n)
        for tc in range(16):
            for ot in range(2):
                pp = pA[(tc * 2 + ot) % 2]
                t.wait(dv)
                t.wait(ac)
                for dc in range(8):
                    t.matmul(pp[:, 0:512], xnf3[:, dc, tc * 128:(tc + 1) * 128],
                             wb3[0][:, dc, ot * 512:(ot + 1) * 512],
                             start=(dc == 0), stop=(dc == 7),
                             _inc=pe.inc() if dc == 7 else None)
                v.wait(pe)
                dst = AP(vtm[:].tensor, tc * 1040 + ot * 8 * 65,
                         [vtm[:].ap[0], [65, 8], [1, 64]])
                v.tensor_tensor(dst, pp[:, 0:512].rearrange("p (h w) -> p h w", w=64),
                                bvb[:, ot * 512:(ot + 1) * 512]
                                .rearrange("p (h w) -> p h w", w=64),
                                ALU.add, _inc=dv.inc())
        wb_used[0] = pe.n
        for dst3 in (q3, k3):
            for oc in range(8):
                v.wait(ac)
                v.wait(dv)
                v.tensor_tensor(ra[:], dst3[:, oc, :], cosr[:], ALU.mult)
                v.tensor_tensor(rb[:], dst3[:, oc, :], sinr[:], ALU.mult, _inc=dv.inc())
                v.wait(dv)
                v.tensor_scalar_mul(rs[0:32, :], rb[32:64, :], -1.0)
                v.tensor_copy(rs[32:64, :], rb[0:32, :])
                v.tensor_scalar_mul(rs[64:96, :], rb[96:128, :], -1.0)
                v.tensor_copy(rs[96:128, :], rb[64:96, :], _inc=dv.inc())
                v.wait(dv)
                v.tensor_tensor(dst3[:, oc, :], ra[:], rs[:], ALU.add, _inc=dv.inc())
        for iqk, src3 in enumerate((q3, k3)):
            for oc in range(8):
                s.wait(dv)
                s.wait(ac)
                if oc > 0:
                    s.wait_ge(pe.h, pe.n)
                s.activation(out=ra[:], in_=src3[:, oc, :], func=AF.Square, _inc=ac.inc())
                t.wait(ac)
                for qt in range(4):
                    ndst = pav3[0:16, qt, :] if qt < 2 else pA2[qt - 2][0:16, 0:512]
                    t.matmul(ndst, hs3[:, oc, :], ra[:, qt * 512:(qt + 1) * 512],
                             start=(oc == 0), stop=(oc == 7),
                             _inc=pe.inc() if qt == 3 else None)
            for qt in range(4):
                nsrc = pav3[0:16, qt, :] if qt < 2 else pA2[qt - 2][0:16, 0:512]
                v.wait(pe)
                if qt == 0:
                    v.reduce_max(mq[:, iqk:iqk + 1], nsrc, axis=AX.X,
                                 _inc=dv.inc())
                else:
                    v.wait(dv)
                    v.reduce_max(sts[0:16, :], nsrc,
                                 axis=AX.X, _inc=dv.inc())
                    v.wait(dv)
                    v.tensor_tensor(mq[:, iqk:iqk + 1], mq[:, iqk:iqk + 1],
                                    sts[0:16, :], ALU.max, _inc=dv.inc())
        s.wait(dv)
        s.activation(out=mq[:, 0:1], in_=mq[:, 0:1], func=AF.Sqrt, _inc=ac.inc())
        s.wait(ac)
        s.activation(out=mq[:, 1:2], in_=mq[:, 1:2], func=AF.Sqrt, _inc=ac.inc())
        v.wait(ac)
        v.tensor_tensor(mq[:, 0:1], mq[:, 0:1], mq[:, 1:2], ALU.mult, _inc=dv.inc())
        t.wait(dv)
        t.matmul(pN[0:1, 0:16], mq[:, 0:1], id16[:], start=True, stop=True, _inc=pe.inc())
        v.wait(pe)
        v.tensor_scalar_mul(crow[:], pN[0:1, 0:16], -0.125, _inc=dv.inc())
        t.wait(dv)
        t.matmul(pN[:, 16:32], C["ones_row"][:], crow[:], start=True, stop=True,
                 _inc=pe.inc())
        v.wait(pe)
        v.tensor_copy(negcb[:], pN[:, 16:32], _inc=dv.inc())
        s.wait(dv)
        for hp in range(8):
            oc = hp
            h0, h1 = 2 * hp, 2 * hp + 1
            for qt in range(4):
                s_pe = {}

                def emit_S(kc_):
                    j_ = kc_ % 2
                    t.wait_ge(ac.h, exp_done[j_])
                    t.matmul(pA[j_][:, 0:512],
                             k3[0:64, oc, kc_ * 128:(kc_ + 1) * 128],
                             q3[0:64, oc, qt * 512:(qt + 1) * 512],
                             start=True, stop=True, tile_position=(0, 0))
                    t.matmul(pA2[j_][:, 0:512],
                             k3[64:128, oc, kc_ * 128:(kc_ + 1) * 128],
                             q3[64:128, oc, qt * 512:(qt + 1) * 512],
                             start=True, stop=True, tile_position=(64, 0),
                             _inc=pe.inc())
                    s_pe[kc_] = pe.n

                emit_S(0)
                for kc in range(16):
                    j = kc % 2
                    if kc + 1 < 16:
                        emit_S(kc + 1)
                    s.wait_ge(pe.h, max(s_pe[kc], av_done[j]))
                    s.activation(out=pt_[j][:], in_=pA[j][:, 0:512], func=AF.Exp,
                                 scale=0.125, bias=negcb[:, h0:h0 + 1])
                    s.activation(out=pt2_[j][:], in_=pA2[j][:, 0:512], func=AF.Exp,
                                 scale=0.125, bias=negcb[:, h1:h1 + 1], _inc=ac.inc())
                    exp_done[j] = ac.n
                    t.wait_ge(ac.h, exp_done[j])
                    if kc == 0:
                        t.wait(dv)
                    t.matmul(pav3[0:65, 0, :], vtm3[:, kc, h0 * 65:h0 * 65 + 65],
                             pt_[j][:], start=(kc == 0), stop=(kc == 15))
                    t.matmul(pav3[0:65, 1, :], vtm3[:, kc, h1 * 65:h1 * 65 + 65],
                             pt2_[j][:], start=(kc == 0), stop=(kc == 15),
                             _inc=pe.inc())
                    av_done[j] = pe.n
                for hi in range(2):
                    v.wait(pe)
                    v.reciprocal(drow[:], pav3[64:65, hi, :], _inc=dv.inc())
                    t.wait(dv)
                    t.matmul(pN[0:64, 0:512], C["ones_row"][0:1, 0:64], drow[:],
                             start=True, stop=True, _inc=pe.inc())
                    v.wait(pe)
                    v.tensor_copy(rbsb[:], pN[0:64, 0:512], _inc=dv.inc())
                    v.wait(dv)
                    if hi == 0:
                        v.tensor_tensor(ao3[0:64, oc, qt * 512:(qt + 1) * 512],
                                        pav3[0:64, 0, :], rbsb[:], ALU.mult,
                                        _inc=dv.inc())
                    else:
                        sg = stg[qt % 2]
                        v.wait_ge(aos[qt % 2].h, aos[qt % 2].n)
                        v.tensor_tensor(sg[:], pav3[0:64, 1, :], rbsb[:], ALU.mult,
                                        _inc=dv.inc())
                        sy.wait_ge(dv.h, dv.n)
                        sy.dma_start(out=ao3[64:128, oc, qt * 512:(qt + 1) * 512],
                                     in_=sg[:], _inc=aos[qt % 2].inc(16))
        if DEV and b == 0:
            for nm, ap_ in (("dbg_q3", q3b[:]), ("dbg_k3", k3b[:]),
                            ("dbg_vtm", vtm[:]), ("dbg_ao", xnf[:]),
                            ("dbg_negcb", negcb[:])):
                d = P.nc.dram_tensor(nm, list(ap_.shape), ap_.tensor.dtype,
                                     kind="ExternalOutput").ap()
                s.wait(dv)
                s.wait(ac)
                for a_ in aos:
                    s.wait_ge(a_.h, a_.n)
                s.dma_start(out=d, in_=ap_, _inc=sd.inc(16))
        load_w("wo", 1)
        t.wait_ge(wbs[0].h, wbs[0].n)
        for a_ in aos:
            t.wait_ge(a_.h, a_.n)
        t.wait(dv)
        for tc in range(16):
            xslot = tc % 2
            sy.wait_ge(dv.h, xnt_used[xslot])
            sy.dma_start(out=xnt[xslot][:],
                         in_=D_["xn_tm"][bt0 + tc * 128:bt0 + (tc + 1) * 128, :],
                         _inc=xts[xslot].inc(16))
            for dh in range(2):
                pp = pA[dh]
                t.wait(dv)
                for oc in range(8):
                    t.matmul(pp[:, 0:512], ao3[:, oc, tc * 128:(tc + 1) * 128],
                             wb3[1][:, oc, dh * 512:(dh + 1) * 512],
                             start=(oc == 0), stop=(oc == 7),
                             _inc=pe.inc() if oc == 7 else None)
                v.wait(pe)
                v.wait_ge(xts[xslot].h, xts[xslot].n)
                v.tensor_tensor(sf32[:, dh * 512:(dh + 1) * 512], pp[:, 0:512],
                                bob[:, dh * 512:(dh + 1) * 512], ALU.add, _inc=dv.inc())
                v.wait(dv)
                v.tensor_tensor(sf32[:, dh * 512:(dh + 1) * 512],
                                sf32[:, dh * 512:(dh + 1) * 512],
                                xnt[xslot][:, dh * 512:(dh + 1) * 512], ALU.add,
                                _inc=dv.inc())
            xnt_used[xslot] = dv.n
            emit_ln_tm(P, sf32[:], x1t[:], cent, sqd, stm, sts, C,
                       gain_bc=g1b[:], bias_bc=be1b[:])
            s.wait(dv)
            s.dma_start(out=D_["x1_tm"][bt0 + tc * 128:bt0 + (tc + 1) * 128, :],
                        in_=x1t[:], _inc=sd.inc(16))
            for j in range(4):
                t.wait(dv)
                t.transpose(pT[:, 0:128], x1t[:, (2 * j) * 128:(2 * j + 1) * 128],
                            C["ident_bf"][:])
                t.transpose(pT[:, 128:256], x1t[:, (2 * j + 1) * 128:(2 * j + 2) * 128],
                            C["ident_bf"][:], _inc=pe.inc())
                v.wait(pe)
                v.tensor_copy(xft[:, (2 * j) * 128:(2 * j + 2) * 128], pT[:],
                              _inc=dv.inc())
            s.wait(dv)
            s.dma_start(out=fm_ap(D_["x1_fm"], tsl=(bt0 + tc * 128, bt0 + (tc + 1) * 128)),
                        in_=xft[:].rearrange("p (c t) -> p c t", t=128), _inc=sd.inc(16))
            v.wait_ge(sd.h, sd.n)
    return {}


def ph_moe(P, D_, C):
    E = P.E
    sy, v, s, t, g = E["sync"], E["vector"], E["scalar"], E["tensor"], E["gpsimd"]
    dv, ac, pe, gp = P.sem("dv"), P.sem("ac"), P.sem("pe"), P.sem("gp")
    ld = P.sem("mold")
    gl = P.sem("mogl")
    sd = P.sem("mosd")
    x1s = [P.sem(f"mox1{i}") for i in range(2)]
    w1s = P.sem("mow1")
    w2s = [P.sem(f"mow2{i}") for i in range(2)]
    yts = [P.sem(f"moyt{i}") for i in range(2)]
    b1s = P.sem("mob1")
    fls = [P.sem(f"mofl{i}") for i in range(2)]

    wg_sb = P.sbuf("mo_wg", [128, 8 * 8], BF16)
    wg3 = wg_sb[:].rearrange("p (c e) -> p c e", e=8)
    bgb = P.sbuf("mo_bgb", [128, 8], F32)
    g2b = P.sbuf("mo_g2b", [128, 1024], BF16)
    be2b = P.sbuf("mo_be2b", [128, 1024], BF16)
    b2row = P.sbuf("mo_b2row", [1, 1024], BF16)
    comb = P.sbuf("mo_comb", [128, 32 * 8], F32)
    comb3 = comb[:].rearrange("p (c e) -> p c e", e=8)
    gbuf = P.sbuf("mo_gbuf", [128, 8], F32)
    e8 = P.sbuf("mo_e8", [128, 8], F32)
    c1 = P.sbuf("mo_c1", [128, 1], F32)
    c2 = P.sbuf("mo_c2", [128, 1], F32)
    c3 = P.sbuf("mo_c3", [128, 1], F32)
    x1b = [P.sbuf(f"mo_x1{i}", [128, 8 * 512], BF16) for i in range(2)]
    x13 = [x[:].rearrange("p (c t) -> p c t", t=512) for x in x1b]
    b1c = P.sbuf("mo_b1c", [128, 32], F32)
    yt = [P.sbuf(f"mo_yt{i}", [128, 512], F32) for i in range(2)]
    pH = [P.psum(f"mo_pH{i}", [128, 512], F32) for i in range(2)]
    pY = [P.psum(f"mo_pY{i}", [128, 512], F32) for i in range(2)]
    pG = P.psum("mo_pG", [128, 512], F32)
    w1b = P.sbuf("mo_w1b", [128, 8 * 4096], BF16)
    w13 = w1b[:].rearrange("p (c f) -> p c f", f=4096)
    w2b = [P.sbuf(f"mo_w2b{i}", [128, 32 * 512], BF16) for i in range(2)]
    w23 = [w[:].rearrange("p (c d) -> p c d", d=512) for w in w2b]
    h3b = P.sbuf("mo_h3", [128, 32 * 512], BF16)
    h3 = h3b[:].rearrange("p (c t) -> p c t", t=512)
    x1_used = [0, 0]
    h_done = {}
    fl_used = [0, 0]

    g.dma_start(out=wg_sb[:], in_=D_["wg"].rearrange("(c p) e -> p c e", p=128),
                _inc=gl.inc(16))
    g.dma_start(out=g2b[0:1, :], in_=D_["g2ln"], _inc=gl.inc(16))
    g.dma_start(out=be2b[0:1, :], in_=D_["be2ln"], _inc=gl.inc(16))
    g.dma_start(out=bgb[0:1, :], in_=D_["bgv"], _inc=gl.inc(16))
    t.wait(gl)
    t.matmul(pG[:, 0:8], C["ones_row"][:], bgb[0:1, :], start=True, stop=True,
             _inc=pe.inc())
    v.wait(pe)
    v.tensor_copy(bgb[:], pG[:, 0:8], _inc=dv.inc())
    for dst in (g2b, be2b):
        for half in range(2):
            t.wait(dv)
            t.matmul(pG[:, 0:512], C["ones_row_bf"][:],
                     dst[0:1, half * 512:(half + 1) * 512], start=True, stop=True,
                     _inc=pe.inc())
            v.wait(pe)
            v.tensor_copy(dst[:, half * 512:(half + 1) * 512], pG[:, 0:512],
                          _inc=dv.inc())

    for tt in range(8):
        slot = tt % 2
        sy.wait_ge(pe.h, x1_used[slot])
        sy.dma_start(out=x13[slot][:, :, :],
                     in_=fm_ap(D_["x1_fm"], tsl=(tt * 512, (tt + 1) * 512)),
                     _inc=x1s[slot].inc(16))
        t.wait_ge(x1s[slot].h, x1s[slot].n)
        for tc4 in range(4):
            tchunk = tt * 4 + tc4
            t.wait(dv)
            for dc in range(8):
                t.matmul(pG[:, 0:8], x13[slot][:, dc, tc4 * 128:(tc4 + 1) * 128],
                         wg3[:, dc, :], start=(dc == 0), stop=(dc == 7),
                         _inc=pe.inc() if dc == 7 else None)
            v.wait(pe)
            v.tensor_tensor(gbuf[:], pG[0:128, 0:8], bgb[:], ALU.add, _inc=dv.inc())
            v.wait(dv)
            v.reduce_max(c1[:], gbuf[:], axis=AX.X, negate=True, _inc=dv.inc())
            s.wait(dv)
            s.activation(out=e8[:], in_=gbuf[:], func=AF.Exp, bias=c1[:],
                         accum_out=c2[:], _inc=ac.inc())
            v.wait(ac)
            v.reciprocal(c2[:], c2[:], _inc=dv.inc())
            v.wait(dv)
            v.tensor_scalar(gbuf[:], e8[:], c2[:], None, ALU.mult, _inc=dv.inc())
            v.wait(dv)
            v.reduce_max(c1[:], gbuf[:], axis=AX.X, _inc=dv.inc())
            v.wait(dv)
            v.tensor_scalar(e8[:], gbuf[:], c1[:], None, ALU.is_ge, _inc=dv.inc())
            v.wait(dv)
            v.scalar_tensor_tensor(e8[:], e8[:], -1.0e9, gbuf[:], ALU.mult, ALU.add,
                                   _inc=dv.inc())
            v.wait(dv)
            v.reduce_max(c2[:], e8[:], axis=AX.X, _inc=dv.inc())
            v.wait(dv)
            v.tensor_scalar(e8[:], gbuf[:], c2[:], None, ALU.is_ge, _inc=dv.inc())
            v.tensor_tensor(c3[:], c1[:], c2[:], ALU.add, _inc=dv.inc())
            v.wait(dv)
            v.reciprocal(c3[:], c3[:], _inc=dv.inc())
            v.wait(dv)
            v.tensor_scalar(gbuf[:], gbuf[:], c3[:], None, ALU.mult, _inc=dv.inc())
            v.wait(dv)
            v.tensor_tensor(comb3[:, tchunk, :], gbuf[:], e8[:], ALU.mult,
                            _inc=dv.inc())
        x1_used[slot] = pe.n

    yacc = D_["y_acc"]
    w1r = D_["w1"].rearrange("e (c p) f -> e p c f", p=128)
    w2r = D_["w2"].rearrange("e (c p) d -> e p c d", p=128)
    w1_used = 0
    w2_used = [0, 0]
    b1_used = 0
    b2_used = 0
    ypd = {0: 0, 1: 0}
    b2s = P.sem("mob2")
    for e_ in range(8):
        g.wait_ge(pe.h, w1_used)
        g.dma_start(out=w13[:, :, :], in_=w1r[e_], _inc=w1s.inc(16))
        for half in range(2):
            g.wait_ge(pe.h, w2_used[half])
            g.dma_start(out=w23[half][:, :, :], in_=w2r[e_][:, :, half * 512:(half + 1) * 512],
                        _inc=w2s[half].inc(16))
        sy.wait_ge(ac.h, b1_used)
        sy.dma_start(out=b1c[:], in_=D_["b1"][e_], _inc=b1s.inc(16))
        g.wait_ge(pe.h, b2_used)
        g.dma_start(out=b2row[:], in_=D_["b2"][e_:e_ + 1, :], _inc=b2s.inc(16))
        t.wait_ge(w1s.h, w1s.n)
        t.wait_ge(b2s.h, b2s.n)
        s.wait_ge(b1s.h, b1s.n)
        for tt in range(8):
            slot = tt % 2
            sy.wait_ge(pe.h, x1_used[slot])
            sy.dma_start(out=x13[slot][:, :, :],
                         in_=fm_ap(D_["x1_fm"], tsl=(tt * 512, (tt + 1) * 512)),
                         _inc=x1s[slot].inc(16))
            t.wait_ge(x1s[slot].h, x1s[slot].n)
            for ffc in range(32):
                pp = pH[ffc % 2]
                if ffc >= 2:
                    t.wait_ge(ac.h, h_done[ffc - 2])
                for dc in range(8):
                    t.matmul(pp[:, 0:512], w13[:, dc, ffc * 128:(ffc + 1) * 128],
                             x13[slot][:, dc, :], start=(dc == 0), stop=(dc == 7),
                             _inc=pe.inc() if dc == 7 else None)
                s.wait(pe)
                s.activation(out=h3[:, ffc, :], in_=pp[:, 0:512], func=AF.Gelu,
                             bias=b1c[:, ffc:ffc + 1], _inc=ac.inc())
                h_done[ffc] = ac.n
            x1_used[slot] = pe.n
            b1_used = ac.n
            t.wait_ge(w2s[0].h, w2s[0].n)
            t.wait_ge(w2s[1].h, w2s[1].n)
            t.wait(ac)
            for tc4 in range(4):
                tchunk = tt * 4 + tc4
                for dt2 in range(2):
                    pp = pY[dt2]
                    slotY = (tc4 * 2 + dt2) % 2
                    t.wait_ge(dv.h, ypd[dt2])
                    for ffc in range(32):
                        t.matmul(pp[:, 0:512], h3[:, ffc, tc4 * 128:(tc4 + 1) * 128],
                                 w23[dt2][:, ffc, :], start=(ffc == 0), stop=False)
                    t.matmul(pp[:, 0:512], C["ones_row_bf"][0:1, 0:128],
                             b2row[0:1, dt2 * 512:(dt2 + 1) * 512],
                             start=False, stop=True, _inc=pe.inc())
                    v.wait(pe)
                    v.wait_ge(yts[slotY].h, yts[slotY].n)
                    v.tensor_scalar(yt[slotY][:], pp[:, 0:512],
                                    comb3[:, tchunk, e_:e_ + 1], None, ALU.mult,
                                    _inc=dv.inc())
                    ypd[dt2] = dv.n
                    g.wait_ge(dv.h, dv.n)
                    if e_ == 0:
                        g.dma_start(out=yacc[tchunk * 128:(tchunk + 1) * 128,
                                            dt2 * 512:(dt2 + 1) * 512],
                                    in_=yt[slotY][:], _inc=yts[slotY].inc(16))
                    else:
                        g.dma_start(out=yacc[tchunk * 128:(tchunk + 1) * 128,
                                            dt2 * 512:(dt2 + 1) * 512],
                                    in_=yt[slotY][:], accum_op=ALU.add,
                                    _inc=yts[slotY].inc(16))
            w2_used = [pe.n, pe.n]
        w1_used = pe.n
        b2_used = pe.n

    P.barrier()
    for nm in ("mo_h3", "mo_w2b1", "mo_w2b0", "mo_w1b"):
        P.free(nm)
    yab = [P.sbuf(f"mo_ya{i}", [128, 1024], F32) for i in range(2)]
    x1tb = [P.sbuf(f"mo_x1t{i}", [128, 1024], BF16) for i in range(2)]
    sf32 = P.sbuf("mo_sf32", [128, 1024], F32)
    cent = P.sbuf("mo_cent", [128, 1024], F32)
    stm = P.sbuf("mo_stm", [128, 1], F32)
    sts = P.sbuf("mo_sts", [128, 1], F32)
    x2t = P.sbuf("mo_x2t", [128, 1024], F32)
    sqbuf = P.sbuf("mo_sqbuf", [128, 1024], F32)
    for a_ in yts:
        sy.wait_ge(a_.h, a_.n)
    for tchunk in range(32):
        slot = tchunk % 2
        sy.wait_ge(dv.h, fl_used[slot])
        sy.dma_start(out=yab[slot][:], in_=yacc[tchunk * 128:(tchunk + 1) * 128, :],
                     _inc=fls[slot].inc(16))
        sy.dma_start(out=x1tb[slot][:],
                     in_=D_["x1_tm"][tchunk * 128:(tchunk + 1) * 128, :],
                     _inc=fls[slot].inc(16))
        v.wait_ge(fls[slot].h, fls[slot].n)
        v.wait(dv)
        v.tensor_tensor(sf32[:], yab[slot][:], x1tb[slot][:], ALU.add, _inc=dv.inc())
        fl_used[slot] = dv.n
        emit_ln_tm(P, sf32[:], x2t[:], cent, sqbuf, stm, sts, C,
                   gain_bc=g2b[:], bias_bc=be2b[:])
        s.wait(dv)
        s.dma_start(out=D_["ffn_tm"][tchunk * 128:(tchunk + 1) * 128, :], in_=x2t[:],
                    _inc=sd.inc(16))
        v.wait_ge(sd.h, sd.n)
    return {}


def ph_l1(P, D_, C):
    E = P.E
    sy, v, s, t, g = E["sync"], E["vector"], E["scalar"], E["tensor"], E["gpsimd"]
    dv, ac, pe, gp = P.sem("dv"), P.sem("ac"), P.sem("pe"), P.sem("gp")
    ld = P.sem("l1ld")
    gl = P.sem("l1gl")
    sd = P.sem("l1sd")
    cbs = [P.sem(f"l1cb{i}") for i in range(2)]
    xfs = P.sem("l1xf")
    x2s = [P.sem(f"l1x2{i}") for i in range(2)]
    wts = P.sem("l1wt")
    gts = [P.sem("l1gt0")] * 2
    fas = [P.sem("l1fa0")] * 2
    xns = P.sem("l1xn")

    stm = P.sbuf("l1_stm", [128, 1], F32)
    sts = P.sbuf("l1_sts", [128, 1], F32)
    ub = [P.sbuf(f"l1_u{i}", [128, 1024], F32) for i in range(2)]
    mb_ = [P.sbuf(f"l1_m{i}", [128, 1024], F32) for i in range(2)]
    fb = [P.sbuf(f"l1_f{i}", [128, 1024], F32) for i in range(2)]
    sf32d = [P.sbuf(f"l1_sf32{i}", [128, 1024], F32) for i in range(2)]
    cent = P.sbuf("l1_cent", [128, 1024], F32)
    sqbuf = P.sbuf("l1_sqbuf", [128, 1024], F32)
    xn2td = [P.sbuf(f"l1_xn2t{i}", [128, 1024], BF16) for i in range(2)]
    xftd = [P.sbuf(f"l1_xft{i}", [128, 1024], BF16) for i in range(2)]
    wtf = P.sbuf("l1_wtf", [128, 8 * 1024], F32)
    wtf3 = wtf[:].rearrange("p (c d) -> p c d", d=1024)
    alrow = P.sbuf("l1_alrow", [1, 1], F32)
    b1b = P.sbuf("l1_b1b", [128, 1024], F32)
    pT = P.psum("l1_pT", [128, 256], BF16)
    pN = P.psum("l1_pN", [128, 512], F32)
    sds = [P.sem(f"l1sa{i}") for i in range(2)]
    sy.dma_start(out=wtf[:].rearrange("p (c d) -> p c d", d=1024),
                 in_=D_["wtT"].rearrange("(c p) d -> p c d", p=128), _inc=ld.inc(16))
    sy.dma_start(out=alrow[:], in_=D_["alpha"], _inc=ld.inc(16))
    sy.dma_start(out=b1b[0:1, :], in_=D_["bias1"], _inc=ld.inc(16))
    cb_used = [0, 0]
    xnt_pe = [0, 0]
    for i in range(32):
        slot = i % 2
        sf32 = sf32d[slot]
        xn2t_t = xn2td[slot]
        xft = xftd[slot]
        sy.wait_ge(dv.h, cb_used[slot])
        sy.dma_start(out=ub[slot][:], in_=D_["u"][i * 128:(i + 1) * 128, :],
                     _inc=cbs[slot].inc(16))
        sy.dma_start(out=mb_[slot][:], in_=D_["mamba_tm"][i * 128:(i + 1) * 128, :],
                     _inc=cbs[slot].inc(16))
        sy.dma_start(out=fb[slot][:], in_=D_["ffn_tm"][i * 128:(i + 1) * 128, :],
                     _inc=cbs[slot].inc(16))
        v.wait_ge(cbs[slot].h, cbs[slot].n)
        v.wait_ge(sds[slot].h, sds[slot].n)
        v.wait_ge(pe.h, xnt_pe[slot])
        v.wait(dv)
        v.tensor_tensor(sf32[:], ub[slot][:], mb_[slot][:], ALU.add, _inc=dv.inc())
        v.wait(dv)
        v.tensor_tensor(sf32[:], sf32[:], fb[slot][:], ALU.add, _inc=dv.inc())
        cb_used[slot] = dv.n
        s.wait(dv)
        s.dma_start(out=D_["xnew_tm"][i * 128:(i + 1) * 128, :], in_=sf32[:],
                    _inc=sds[slot].inc(16))
        emit_ln_tm(P, sf32[:], xn2t_t[:], cent, sqbuf, stm, sts, C)
        s.wait(dv)
        s.dma_start(out=D_["xn2_tm"][i * 128:(i + 1) * 128, :], in_=xn2t_t[:],
                    _inc=sds[slot].inc(16))
        for j in range(4):
            t.wait(dv)
            t.transpose(pT[:, 0:128], xn2t_t[:, (2 * j) * 128:(2 * j + 1) * 128],
                        C["ident_bf"][:])
            t.transpose(pT[:, 128:256], xn2t_t[:, (2 * j + 1) * 128:(2 * j + 2) * 128],
                        C["ident_bf"][:], _inc=pe.inc())
            v.wait(pe)
            v.tensor_copy(xft[:, (2 * j) * 128:(2 * j + 2) * 128], pT[:], _inc=dv.inc())
        xnt_pe[slot] = pe.n
        s.wait(dv)
        s.dma_start(out=fm_ap(D_["xn2_fm"], tsl=(i * 128, (i + 1) * 128)),
                    in_=xft[:].rearrange("p (c t) -> p c t", t=128),
                    _inc=sds[slot].inc(16))

    wtb = P.sbuf("l1_wtb", [128, 8 * 1024], BF16)
    wtb3 = wtb[:].rearrange("p (c d) -> p c d", d=1024)
    tern_a = P.sbuf("l1_terna", [128, 1024], F32)
    tern_b = P.sbuf("l1_ternb", [128, 1024], F32)
    accw = P.sbuf("l1_accw", [128, 1], F32)
    dcol = P.sbuf("l1_dcol", [128, 2], F32)
    alcol = P.sbuf("l1_alcol", [128, 1], F32)
    x2tl = [P.sbuf(f"l1_x2tl{i}", [128, 8 * 512], BF16) for i in range(2)]
    x2tl3 = [x[:].rearrange("p (c t) -> p c t", t=512) for x in x2tl]
    l1f = P.sbuf("l1_l1f", [128, 1024], F32)
    pO = [P.psum(f"l1_pO{i}", [128, 512], F32) for i in range(2)]
    sy.wait_ge(sds[0].h, sds[0].n)
    sy.wait_ge(sds[1].h, sds[1].n)
    v.wait(ld)
    for dc in range(8):
        v.wait(dv)
        if dc == 0:
            v.tensor_reduce(accw[:], wtf3[:, 0, :], axis=AX.X, op=ALU.add,
                            apply_absolute_value=True, _inc=dv.inc())
        else:
            v.tensor_reduce(sts[:], wtf3[:, dc, :], axis=AX.X, op=ALU.add,
                            apply_absolute_value=True, _inc=dv.inc())
            v.wait(dv)
            v.tensor_tensor(accw[:], accw[:], sts[:], ALU.add, _inc=dv.inc())
    t.wait(dv)
    t.matmul(pN[0:1, 0:1], C["ones_col"][:], accw[:], start=True, stop=True,
             _inc=pe.inc())
    v.wait(pe)
    v.tensor_scalar(stm[0:1, :], pN[0:1, 0:1], 0.7 / (1024.0 * 1024.0), None,
                    ALU.mult, _inc=dv.inc())
    t.wait(dv)
    t.matmul(pN[:, 1:2], C["ones_row"][:], stm[0:1, :], start=True, stop=True)
    t.matmul(pN[:, 2:3], C["ones_row"][:], alrow[:], start=True, stop=True,
             _inc=pe.inc())
    v.wait(pe)
    v.tensor_copy(dcol[:, 0:1], pN[:, 1:2])
    v.tensor_scalar_mul(dcol[:, 1:2], pN[:, 1:2], -1.0)
    v.tensor_copy(alcol[:], pN[:, 2:3], _inc=dv.inc())
    for half in range(2):
        t.wait(dv)
        t.matmul(pN[:, 0:512], C["ones_row"][:], b1b[0:1, half * 512:(half + 1) * 512],
                 start=True, stop=True, _inc=pe.inc())
        v.wait(pe)
        v.tensor_copy(b1b[:, half * 512:(half + 1) * 512], pN[:, 0:512], _inc=dv.inc())
    for dc in range(8):
        v.wait(dv)
        v.tensor_scalar(tern_a[:], wtf3[:, dc, :], dcol[:, 0:1], None, ALU.is_gt)
        v.tensor_scalar(tern_b[:], wtf3[:, dc, :], dcol[:, 1:2], None, ALU.is_lt,
                        _inc=dv.inc())
        v.wait(dv)
        v.tensor_tensor(wtb3[:, dc, :], tern_a[:], tern_b[:], ALU.subtract,
                        _inc=dv.inc())
    x2_used = [0, 0]
    for tt in range(8):
        slot = tt % 2
        sy.wait_ge(pe.h, x2_used[slot])
        sy.dma_start(out=x2tl3[slot][:, :, :],
                     in_=fm_ap(D_["xn2_fm"], tsl=(tt * 512, (tt + 1) * 512)),
                     _inc=x2s[slot].inc(16))
        t.wait_ge(x2s[slot].h, x2s[slot].n)
        t.wait(dv)
        for tc4 in range(4):
            for ot in range(2):
                pp = pO[ot]
                t.wait(dv)
                for dc in range(8):
                    t.matmul(pp[:, 0:512], x2tl3[slot][:, dc, tc4 * 128:(tc4 + 1) * 128],
                             wtb3[:, dc, ot * 512:(ot + 1) * 512],
                             start=(dc == 0), stop=(dc == 7),
                             _inc=pe.inc() if dc == 7 else None)
                v.wait(pe)
                v.scalar_tensor_tensor(l1f[:, ot * 512:(ot + 1) * 512], pp[:, 0:512],
                                       alcol[:], b1b[:, ot * 512:(ot + 1) * 512],
                                       ALU.mult, ALU.add, _inc=dv.inc())
            s.wait(dv)
            s.dma_start(out=D_["l1ffn_tm"][(tt * 4 + tc4) * 128:(tt * 4 + tc4 + 1) * 128, :],
                        in_=l1f[:], _inc=sd.inc(16))
            v.wait_ge(sd.h, sd.n)
        x2_used[slot] = pe.n

    P.barrier()
    for nm in ("l1_l1f", "l1_x2tl1", "l1_x2tl0", "l1_alcol", "l1_dcol", "l1_accw",
               "l1_ternb", "l1_terna", "l1_wtb", "l1_b1b", "l1_alrow", "l1_wtf",
               "l1_xft1", "l1_xft0", "l1_xn2t1", "l1_xn2t0", "l1_sqbuf", "l1_cent",
               "l1_sf321", "l1_sf320",
               "l1_f1", "l1_f0", "l1_m1", "l1_m0", "l1_u1", "l1_u0"):
        P.free(nm)
    wqb = P.sbuf("l1_wqb", [128, 8 * 1024], BF16)
    wq3 = wqb[:].rearrange("p (c e) -> p c e", e=1024)
    bqc = P.sbuf("l1_bqc", [128, 8], F32)
    bkc = P.sbuf("l1_bkc", [128, 8], F32)
    xn2f = P.sbuf("l1_xn2f", [128, 8 * 2048], BF16)
    xn2f3 = xn2f[:].rearrange("p (c t) -> p c t", t=2048)
    q1b = P.sbuf("l1_q1", [128, 8 * 2048], BF16)
    q13 = q1b[:].rearrange("p (c t) -> p c t", t=2048)
    k1b = P.sbuf("l1_k1", [128, 8 * 2048], BF16)
    k13 = k1b[:].rearrange("p (c t) -> p c t", t=2048)
    hselc = P.sbuf("l1_hselc", [128, 8 * 16], BF16)
    hs3 = hselc[:].rearrange("p (c m) -> p c m", m=16)
    id16 = P.sbuf("l1_id16", [16, 16], F32)
    negcb = P.sbuf("l1_negcb", [128, 16], F32)
    mq = P.sbuf("l1_mq", [16, 2], F32)
    crow = P.sbuf("l1_crow", [1, 16], F32)
    mn = P.sbuf("l1_mn", [128, 2048], BF16)
    zbs = [P.sbuf(f"l1_zb{i}", [128, 2048], F32) for i in range(2)]
    zb = zbs[0]
    pbs = [P.sbuf(f"l1_pb{i}", [128, 2048], BF16) for i in range(2)]
    stmh = P.sbuf("l1_stmh", [128, 2], F32)
    stsh = P.sbuf("l1_stsh", [128, 2], F32)
    sqr = pbs[0]
    avg = P.sbuf("l1_avg", [128, 2048], F32)
    g2t = [P.sbuf("l1_g2t0", [128, 2048], F32)]
    g2t.append(g2t[0])
    dwn = P.sbuf("l1_dwn", [128, 2048], BF16)
    dwtc = P.sbuf("l1_dwtc", [128, 16 * 128], BF16)
    dwt3 = dwtc[:].rearrange("p (c q) -> p c q", q=128)
    xn2ts = P.sbuf("l1_xn2ts", [128, 16 * 1024], BF16)
    xn2ts3 = xn2ts[:].rearrange("p (c e) -> p c e", e=1024)
    xnb2 = [P.sbuf("l1_xnb0", [128, 1024], F32)]
    xnb2.append(xnb2[0])
    l1fb = [P.sbuf("l1_l1fb0", [128, 1024], F32)]
    l1fb.append(l1fb[0])
    pSf = P.psum("l1_pSf", [128, 2048], F32)
    pS = [pSf[:, 0:1024], pSf[:, 1024:2048]]

    g.memset(hselc[:], 0.0)
    g.memset(id16[:], 0.0, _inc=gp.inc())
    g.wait(gp)
    for oc in range(8):
        g.memset(hs3[0:64, oc, 2 * oc:2 * oc + 1], 1.0)
        g.memset(hs3[64:128, oc, 2 * oc + 1:2 * oc + 2], 1.0)
    g.affine_select(out=id16[:], in_=id16[:], compare_op=ALU.not_equal, fill=1.0,
                    base=0, pattern=[[-1, 16]], channel_multiplier=1, _inc=gp.inc())
    sy.dma_start(out=bqc[:], in_=D_["bq1"], _inc=ld.inc(16))
    sy.dma_start(out=bkc[:], in_=D_["bk1"], _inc=ld.inc(16))
    t.wait(gp)
    t.wait(ld)

    wt_used = 0
    gt_used = [0, 0]
    fa_used = [0, 0]
    for b in range(B):
        bt0 = b * T
        sy.wait_ge(pe.h, pe.n)
        sy.dma_start(out=xn2f3[:], in_=fm_ap(D_["xn2_fm"], tsl=(bt0, bt0 + T)),
                     _inc=xfs.inc(16))
        qk_dst = (q13, k13)
        for iw in range(2):
            g.wait_ge(pe.h, wt_used)
            g.dma_start(out=wq3[:, :, :],
                        in_=D_[("wq1T", "wk1T")[iw]].rearrange("(c p) e -> p c e", p=128),
                        _inc=wts.inc(16))
            t.wait_ge(wts.h, wts.n)
            t.wait_ge(xfs.h, xfs.n)
            t.wait(dv)
            dst = qk_dst[iw]
            bcol = (bqc, bkc)[iw]
            for oc in range(8):
                for qt in range(4):
                    pp = pO[(oc * 4 + qt) % 2]
                    t.wait(ac)
                    for dc in range(8):
                        t.matmul(pp[:, 0:512], wq3[:, dc, oc * 128:(oc + 1) * 128],
                                 xn2f3[:, dc, qt * 512:(qt + 1) * 512],
                                 start=(dc == 0), stop=(dc == 7),
                                 _inc=pe.inc() if dc == 7 else None)
                    s.wait(pe)
                    s.activation(out=dst[:, oc, qt * 512:(qt + 1) * 512],
                                 in_=pp[:, 0:512], func=AF.Identity,
                                 bias=bcol[:, oc:oc + 1], _inc=ac.inc())
            wt_used = pe.n
        for iqk, src3 in enumerate((q13, k13)):
            for oc in range(8):
                s.wait(dv)
                s.wait(ac)
                if oc > 0:
                    s.wait_ge(pe.h, pe.n)
                s.activation(out=sqr[:], in_=src3[:, oc, :], func=AF.Square,
                             _inc=ac.inc())
                t.wait(ac)
                for qt in range(4):
                    t.matmul(pS[qt // 2][0:16, (qt % 2) * 512:(qt % 2) * 512 + 512],
                             hs3[:, oc, :], sqr[:, qt * 512:(qt + 1) * 512],
                             start=(oc == 0), stop=(oc == 7),
                             _inc=pe.inc() if qt == 3 else None)
            for qt in range(4):
                v.wait(pe)
                if qt == 0:
                    v.reduce_max(mq[:, iqk:iqk + 1],
                                 pS[0][0:16, 0:512], axis=AX.X, _inc=dv.inc())
                else:
                    v.wait(dv)
                    v.reduce_max(sts[0:16, :],
                                 pS[qt // 2][0:16, (qt % 2) * 512:(qt % 2) * 512 + 512],
                                 axis=AX.X, _inc=dv.inc())
                    v.wait(dv)
                    v.tensor_tensor(mq[:, iqk:iqk + 1], mq[:, iqk:iqk + 1],
                                    sts[0:16, :], ALU.max, _inc=dv.inc())
        s.wait(dv)
        s.activation(out=mq[:, 0:1], in_=mq[:, 0:1], func=AF.Sqrt, _inc=ac.inc())
        s.wait(ac)
        s.activation(out=mq[:, 1:2], in_=mq[:, 1:2], func=AF.Sqrt, _inc=ac.inc())
        v.wait(ac)
        v.tensor_tensor(mq[:, 0:1], mq[:, 0:1], mq[:, 1:2], ALU.mult, _inc=dv.inc())
        t.wait(dv)
        t.matmul(pN[0:1, 0:16], mq[:, 0:1], id16[:], start=True, stop=True,
                 _inc=pe.inc())
        v.wait(pe)
        v.tensor_scalar_mul(crow[:], pN[0:1, 0:16], -0.125, _inc=dv.inc())
        t.wait(dv)
        t.matmul(pN[:, 16:32], C["ones_row"][:], crow[:], start=True, stop=True,
                 _inc=pe.inc())
        v.wait(pe)
        v.tensor_copy(negcb[:], pN[:, 16:32], _inc=dv.inc())
        sy.wait_ge(pe.h, pe.n)
        sy.dma_start(out=xn2ts3[:, :, :],
                     in_=D_["xn2_tm"][bt0:bt0 + T, :].rearrange("(c p) e -> p c e", p=128),
                     _inc=xns.inc(16))
        zt_done = 0
        zb_free = [0, 0]
        pb_done = [0, 0]
        for qc in range(16):
            w = (qc + 1) * 128
            nk5 = (w + 511) // 512
            g.wait_ge(dv.h, dv.n)
            g.memset(mn[:, 0:w], -1.0e9, _inc=gp.inc())
            g.wait(gp)
            g.affine_select(out=mn[:, 0:w], in_=mn[:, 0:w], compare_op=ALU.is_ge,
                            fill=0.0, base=-qc * 128, pattern=[[1, w]],
                            channel_multiplier=-1, _inc=gp.inc())
            mn_done = gp.n
            gslot = qc % 2
            sy.wait_ge(dv.h, gt_used[gslot])
            sy.dma_start(out=g2t[gslot][:, 0:w],
                         in_=D_["gum2"][bt0 + qc * 128:bt0 + (qc + 1) * 128, 0:w],
                         _inc=gts[gslot].inc(16))
            exp_ac = {}

            def emit_rcstt(h2):
                j2 = h2 % 2
                v.wait_ge(ac.h, exp_ac[h2])
                if qc == 0:
                    v.tensor_scalar(stsh[:, j2:j2 + 1], stmh[:, j2:j2 + 1], 16.0,
                                    1.6e-29, ALU.mult, ALU.max, _inc=dv.inc())
                else:
                    v.tensor_scalar_mul(stsh[:, j2:j2 + 1], stmh[:, j2:j2 + 1], 16.0,
                                        _inc=dv.inc())
                v.wait(dv)
                v.reciprocal(stsh[:, j2:j2 + 1], stsh[:, j2:j2 + 1], _inc=dv.inc())
                v.wait(dv)
                if h2 == 0:
                    v.tensor_scalar(avg[:, 0:w], pbs[j2][:, 0:w], stsh[:, j2:j2 + 1],
                                    None, ALU.mult, _inc=dv.inc())
                else:
                    v.scalar_tensor_tensor(avg[:, 0:w], pbs[j2][:, 0:w],
                                           stsh[:, j2:j2 + 1],
                                           avg[:, 0:w], ALU.mult, ALU.add,
                                           _inc=dv.inc())
                pb_done[j2] = dv.n

            for h in range(16):
                r0 = (h % 2) * 64
                oc = h // 2
                j = h % 2
                t.wait_ge(dv.h, zt_done)
                for kt in range(nk5):
                    kw = min(512, w - kt * 512)
                    t.matmul(pSf[:, kt * 512:kt * 512 + kw],
                             q13[r0:r0 + 64, oc, qc * 128:(qc + 1) * 128],
                             k13[r0:r0 + 64, oc, kt * 512:kt * 512 + kw],
                             start=True, stop=True,
                             _inc=pe.inc() if kt == nk5 - 1 else None)
                spe = pe.n
                v.wait_ge(pe.h, spe)
                v.wait_ge(ac.h, zb_free[j])
                v.wait_ge(gp.h, mn_done)
                v.tensor_tensor(zbs[j][:, 0:w], pSf[:, 0:w], mn[:, 0:w], ALU.add,
                                _inc=dv.inc())
                zt_done = dv.n
                s.wait_ge(dv.h, max(zt_done, pb_done[j]))
                s.activation(out=pbs[j][:, 0:w], in_=zbs[j][:, 0:w], func=AF.Exp,
                             scale=0.125,
                             bias=negcb[:, h:h + 1], accum_out=stmh[:, j:j + 1],
                             _inc=ac.inc())
                exp_ac[h] = ac.n
                zb_free[j] = ac.n
                if h >= 1:
                    emit_rcstt(h - 1)
            emit_rcstt(15)
            v.wait_ge(gts[gslot].h, gts[gslot].n)
            v.wait(dv)
            v.wait(gp)
            v.scalar_tensor_tensor(zb[:, 0:w], avg[:, 0:w], 0.5, g2t[gslot][:, 0:w],
                                   ALU.mult, ALU.add, _inc=dv.inc())
            gt_used[0] = gt_used[1] = dv.n
            v.wait(dv)
            v.reduce_max(stm[:], zb[:, 0:w], axis=AX.X, negate=True, _inc=dv.inc())
            s.wait(dv)
            s.activation(out=dwn[:, 0:w], in_=zb[:, 0:w], func=AF.Exp, bias=stm[:],
                         accum_out=sts[:], _inc=ac.inc())
            zb_free[0] = ac.n
            v.wait(ac)
            v.tensor_single_scalar(sts[:], sts[:], 1.0e-30, ALU.max, _inc=dv.inc())
            v.wait(dv)
            v.reciprocal(sts[:], sts[:], _inc=dv.inc())
            v.wait(dv)
            v.tensor_scalar(dwn[:, 0:w], dwn[:, 0:w], sts[:], None, ALU.mult,
                            _inc=dv.inc())
            for kc in range(qc + 1):
                t.wait(dv)
                t.transpose(pT[:, 0:128], dwn[:, kc * 128:(kc + 1) * 128],
                            C["ident_bf"][:], _inc=pe.inc())
                v.wait(pe)
                v.tensor_copy(dwt3[:, kc, :], pT[:, 0:128], _inc=dv.inc())
            fslot = qc % 2
            sy.wait_ge(dv.h, fa_used[fslot])
            sy.wait_ge(sd.h, sd.n)
            sy.dma_start(out=xnb2[fslot][:],
                         in_=D_["xnew_tm"][bt0 + qc * 128:bt0 + (qc + 1) * 128, :],
                         _inc=fas[fslot].inc(16))
            sy.dma_start(out=l1fb[fslot][:],
                         in_=D_["l1ffn_tm"][bt0 + qc * 128:bt0 + (qc + 1) * 128, :],
                         _inc=fas[fslot].inc(16))
            t.wait_ge(xns.h, xns.n)
            for et in range(2):
                pp = pO[et]
                t.wait(dv)
                for kc in range(qc + 1):
                    t.matmul(pp[:, 0:512], dwt3[:, kc, :],
                             xn2ts3[:, kc, et * 512:(et + 1) * 512],
                             start=(kc == 0), stop=(kc == qc),
                             _inc=pe.inc() if kc == qc else None)
                if qc == 0:
                    v.wait(pe)
                    v.memset(pp[0:1, 0:512], 0.0, _inc=dv.inc())
            v.wait(pe)
            v.wait(dv)
            v.wait_ge(fas[fslot].h, fas[fslot].n)
            for et in range(2):
                v.tensor_tensor(xnb2[fslot][:, et * 512:(et + 1) * 512],
                                xnb2[fslot][:, et * 512:(et + 1) * 512],
                                pO[et][:, 0:512], ALU.add, _inc=dv.inc())
            v.wait(dv)
            v.tensor_tensor(xnb2[fslot][:], xnb2[fslot][:], l1fb[fslot][:], ALU.add,
                            _inc=dv.inc())
            fa_used[0] = fa_used[1] = dv.n
            s.wait(dv)
            s.dma_start(out=D_["out"][bt0 + qc * 128:bt0 + (qc + 1) * 128, :],
                        in_=xnb2[fslot][:], _inc=sd.inc(16))
    return {}



PHASE_DEFS = None


def build_launch(phases, keep_outputs=None):
    nc = bass.Bass("TRN2", target_bir_lowering=False)
    needed = set()
    made_all = set()
    for name, fn, needs, makes in phases:
        for n in needs:
            if n not in made_all:
                needed.add(n)
        made_all |= set(makes)
    D_ = {}
    ext_in = []
    ext_out = []
    for n in sorted(needed):
        if n in SHARED_IN:
            sh, dt = SHARED_IN[n]
        elif n in PERCORE_IN:
            sh, dt = PERCORE_IN[n]
        elif n in SCRATCH:
            sh, dt = SCRATCH[n]
        else:
            raise KeyError(n)
        D_[n] = nc.dram_tensor(n, list(sh), dt, kind="ExternalInput").ap()
        ext_in.append(n)
    for n in sorted(made_all):
        if n == OUT_NAME:
            sh, dt = OUT_SHAPE, F32
        else:
            sh, dt = SCRATCH[n]
        if keep_outputs is None or n in keep_outputs:
            D_[n] = nc.dram_tensor(n, list(sh), dt, kind="ExternalOutput").ap()
            ext_out.append(n)
        else:
            D_[n] = nc.dram_tensor(n, list(sh), dt, kind="Internal").ap()

    P = Prog(nc)
    C = make_consts(P, D_)
    for name, fn, needs, makes in phases:
        fn(P, D_, C)
        P.barrier()
        P.free_prefix(_PHASE_PREFIX.get(name, "\x00"))
    sy = P.E["sync"]
    for snm in ("sd", "ld", "wt", "st"):
        if snm in P._sems:
            sy.wait(P.sem(snm))
    P.finish()
    return nc, ext_in, ext_out


def kernel(**inputs):
    H = _prep_host(inputs)
    u_full = H.pop("_u_full")
    gum_full = H.pop("_gum2_full")

    percore = []
    for c in range(8):
        percore.append({
            "u": np.ascontiguousarray(u_full[2 * c:2 * c + 2].reshape(NT, D)),
            "gum2": np.ascontiguousarray(gum_full[2 * c:2 * c + 2].reshape(NT, T)),
        })

    state = [dict() for _ in range(8)]
    for gi, grp in enumerate(LAUNCH_GROUPS):
        phases = [PHASE_DEFS[nm] for nm in grp]
        later_needs = {OUT_NAME}
        for g2 in LAUNCH_GROUPS[gi + 1:]:
            for nm in g2:
                later_needs |= set(PHASE_DEFS[nm][2])
        keep = {n for nm in grp for n in PHASE_DEFS[nm][3] if n in later_needs}
        nc, ext_in, ext_out = build_launch(phases, keep_outputs=keep)
        in_maps = []
        for c in range(8):
            m = {}
            for n in ext_in:
                if n in SHARED_IN:
                    m[n] = H[n]
                elif n in PERCORE_IN:
                    m[n] = percore[c][n]
                else:
                    m[n] = state[c][n]
            in_maps.append(m)
        res = run_bass_kernel_spmd(nc, in_maps, core_ids=list(range(8)))
        for c in range(8):
            for n in ext_out:
                state[c][n] = res.results[c][n]

    out = np.concatenate([np.asarray(state[c][OUT_NAME], np.float32)[None]
                          for c in range(8)], axis=0)
    return np.ascontiguousarray(out.reshape(16, 2048, 1024))


def _register_phases():
    global PHASE_DEFS
    PHASE_DEFS = {
        "prep": ("prep", ph_prep, ["u"], ["xn_tm", "xn_fm"]),
        "mamba": ("mamba", ph_mamba,
                  ["xn_fm", "inprojT", "convw", "convb", "dtb", "alog", "dp",
                   "normw", "oprojT"],
                  ["mamba_tm"]),
        "attn": ("attn", ph_attn,
                 ["xn_fm", "xn_tm", "wq", "wk", "wv", "wo", "bq", "bk", "bv",
                  "bo", "g1", "be1", "cos32", "sin32"],
                 ["x1_fm", "x1_tm"]),
        "moe": ("moe", ph_moe,
                ["x1_fm", "x1_tm", "wg", "bgv", "w1", "b1", "w2", "b2",
                 "g2ln", "be2ln"],
                ["ffn_tm", "y_acc"]),
        "l1": ("l1", ph_l1,
               ["u", "mamba_tm", "ffn_tm", "wq1T", "wk1T", "bq1", "bk1",
                "wtT", "alpha", "bias1", "gum2"],
               ["xnew_tm", "xn2_tm", "xn2_fm", "l1ffn_tm", "out"]),
    }


_register_phases()
LAUNCH_GROUPS = [["prep", "mamba", "attn", "moe", "l1"]]
